# version 04
# baseline (speedup 1.0000x reference)
import os
import numpy as np
import ml_dtypes
from contextlib import ExitStack
_SKIP_TERN = os.environ.get("K_SKIP_TERN", "0") == "1"
_COARSE = set(os.environ.get("K_COARSE", "").split(","))
_SKIP_TPOSE = os.environ.get("K_SKIP_TPOSE", "0") == "1"

import concourse.bass as bass
import concourse.mybir as mybir
from concourse.bass_utils import run_bass_kernel_spmd

F32 = mybir.dt.float32
BF16 = mybir.dt.bfloat16
AF = mybir.ActivationFunctionType
OP = mybir.AluOpType
AX = mybir.AxisListType.X

D = 1024
T = 2048
NCH = 8
NB = 4
FF = 4096
NT512 = T // 512
NT256 = T // 256
EPS_TOP = 1e-5
EPS_BLK = 1e-6

PC_BQ, PC_BK, PC_BV, PC_BO = 0, 8, 16, 24
PC_B1, PC_B2 = 32, 64
PC_G1, PC_BE1, PC_G2, PC_BE2 = 72, 80, 88, 96
PARC = 104

AB1_WK, AB1_WV, AB1_WQ, AB1_WO = 0, 8192, 16384, 24576
AB2_V = 0
AB2_TERN, AB2_H, AB2_XQ, AB2_LN = 0, 8192, 16384, 18432

CL_KS = 0
CQ_M0, CQ_M1, CQ_MM, CQ_MC, CQ_RC, CQ_R7, CQ_NS = 0, 1, 2, 3, 4, 5, 6
CQ_SO = 24
CL_KV = 0
CQ_HSO = 32

R_A, R_B, R_C, R_D = 0, 256, 512, 768

CC_EPS5, CC_EPS6, CC_NHALF, CC_NONE, CC_NEPS5, CC_LN7, CC_NLN7, CC_R1024 = range(8)
CC_HALF, CC_5E6, CC_N8, CC_85, CC_N5E6 = 8, 9, 10, 11, 12


class Sem:
    def __init__(self, h):
        self.h = h
        self.n = 0


class Prog:
    ENGS = ("sync", "vector", "scalar", "tensor", "gpsimd")

    def __init__(self):
        self.q = {e: [] for e in self.ENGS}

    def op(self, eng, method, *args, incs=(), **kw):
        for s, k in incs:
            s.n += k
        bound = [(s.h, k) for s, k in incs]

        def fn(e, method=method, args=args, kw=kw, bound=bound):
            ins = getattr(e, method)(*args, **kw)
            for h, k in bound:
                ins.then_inc(h, k)
            return ins
        self.q[eng].append(("op", fn))

    def mark(self, eng, *incs):
        for s, k in incs:
            s.n += k
        bound = [(s.h, k) for s, k in incs]
        self.q[eng].append(("mark", bound))

    def wait(self, eng, sem, val):
        if val <= 0:
            return
        self.q[eng].append(("wait", lambda e, h=sem.h, v=val: e.wait_ge(h, v)))


def _runq_outer(e, q, name):
    last = None
    for idx, ent in enumerate(q):
        kind, payload = ent
        try:
            if kind == "op":
                last = payload(e)
            elif kind == "wait":
                payload(e)
            else:
                assert last is not None, "mark with no preceding op"
                for h, k in payload:
                    last.then_inc(h, k)
        except Exception:
            import sys
            print(f"FAILED {name} entry #{idx} kind={kind}", file=sys.stderr)
            raise


def build(stop="full"):
    nc = bass.Bass("TRN2", target_bir_lowering=False, detect_race_conditions=False)

    X = nc.dram_tensor("X", [D, T], F32, kind="ExternalInput").ap()
    WQa = nc.dram_tensor("WQ", [NB, D, D], F32, kind="ExternalInput").ap()
    WKa = nc.dram_tensor("WK", [NB, D, D], F32, kind="ExternalInput").ap()
    WVa = nc.dram_tensor("WV", [NB, D, D], F32, kind="ExternalInput").ap()
    WOa = nc.dram_tensor("WO", [NB, D, D], F32, kind="ExternalInput").ap()
    W1a = nc.dram_tensor("W1", [NB, D, FF], F32, kind="ExternalInput").ap()
    W2a = nc.dram_tensor("W2", [NB, FF, D], F32, kind="ExternalInput").ap()
    PAR = nc.dram_tensor("PAR", [NB, 128, PARC], F32, kind="ExternalInput").ap()
    WT = nc.dram_tensor("WT", [D, D], F32, kind="ExternalInput").ap()
    HAD = nc.dram_tensor("HAD", [D, D], BF16, kind="ExternalInput").ap()
    IDTd = nc.dram_tensor("IDT", [128, 128], BF16, kind="ExternalInput").ap()
    ONE1d = nc.dram_tensor("ONE1", [128, 1], BF16, kind="ExternalInput").ap()
    ONEFd = nc.dram_tensor("ONEF", [1, 128], F32, kind="ExternalInput").ap()
    CONSTd = nc.dram_tensor("CONST", [128, 16], F32, kind="ExternalInput").ap()
    Y = nc.dram_tensor("Y", [D, T], F32, kind="ExternalOutput").ap()

    sy1 = nc.dram_tensor("sy1", [D, T], F32)
    sa = nc.dram_tensor("sa", [D, T], F32)
    sbm = nc.dram_tensor("sbm", [D, T], F32)
    ccin = nc.dram_tensor("ccin", [128, 16], F32)
    ccout = nc.dram_tensor("ccout", [128, 16], F32)
    ternsp = nc.dram_tensor("ternsp", [128, 8192], BF16)

    Xv = X.rearrange("(c p) t -> p c t", p=128)
    Yv = Y.rearrange("(c p) t -> p c t", p=128)
    sy1v = sy1.ap().rearrange("(c p) t -> p c t", p=128)
    sav = sa.ap().rearrange("(c p) t -> p c t", p=128)
    sbv = sbm.ap().rearrange("(c p) t -> p c t", p=128)

    es = ExitStack()
    sb_ = lambda n, sh, dt=F32: es.enter_context(nc.sbuf_tensor(n, sh, dt))
    ps_ = lambda n, sh, dt=F32: es.enter_context(nc.psum_tensor(n, sh, dt))

    XB = sb_("XB", [128, NCH * T], BF16)
    AB1 = sb_("AB1", [128, 32768], BF16)
    AB2 = sb_("AB2", [128, 32768], BF16)
    WS = sb_("WS", [128, 2048], F32)
    XT = sb_("XT", [128, 2048], F32)
    STY = sb_("STY", [128, 2048], F32)
    SBQ = sb_("SBQ", [128, 2048], BF16)
    KQ = [sb_(f"KQ{i}", [128, 512]) for i in range(6)]
    HB = sb_("HB", [128, 512], BF16)
    COLS = sb_("COLS", [128, 32])
    COLS2 = sb_("COLS2", [128, 40])
    PK = sb_("PK", [128, 16])
    CC = PK
    PARS = sb_("PARS", [128, NB * PARC])
    ROWS = sb_("ROWS", [1, 1024])
    IDT = sb_("IDT_s", [128, 128], BF16)
    ONE1 = sb_("ONE1_s", [128, 1], BF16)
    ONEF = sb_("ONEF_s", [1, 128], F32)
    CONSTC = sb_("CONST_s", [128, 16], F32)

    pA = ps_("pA", [128, 512])
    pB = ps_("pB", [128, 512])
    pC = ps_("pC", [128, 512])
    pD = ps_("pD", [128, 512])
    pEF = ps_("pEF", [128, 1024])
    pE = pEF[:, 0:512]
    pF = pEF[:, 512:1024]
    pG = ps_("pG", [128, 512])
    pT = ps_("pT", [128, 1024], BF16)
    PW = [pA, pB, pC, pD]

    semnames = ("sstp", "sw0", "sw1", "sxl", "sxs", "ssp", "scast", "stns",
                "svec", "ssca", "sgps", "scc", "sout")
    sems = {k: Sem(es.enter_context(nc.semaphore(k))) for k in semnames}
    (sstp, sw0, sw1, sxl, sxs, ssp, scast, stns, svec, ssca, sgps, scc, sout) = (
        sems[k] for k in semnames)
    SW = [sw0, sw1]

    P = Prog()
    OPQ = P.op
    W = P.wait
    MK = P.mark

    def pcol(blk, base, idx=0):
        c = blk * PARC + base + idx
        return PARS[:, c:c + 1]

    gcount = [0]

    def granule(src_ap, dst_ap, area_gate_tns=0, extra_gps_gate=0):
        g = gcount[0]
        gcount[0] += 1
        slot = WS[:, (g % 2) * 1024:(g % 2) * 1024 + 1024]
        W("sync", scast, g - 1)
        OPQ("sync", "dma_start", out=slot, in_=src_ap, incs=[(SW[g % 2], 16)])
        W("vector", SW[g % 2], SW[g % 2].n)
        if area_gate_tns:
            W("vector", stns, area_gate_tns)
        if extra_gps_gate:
            W("vector", ssp, extra_gps_gate)
        OPQ("vector", "tensor_copy", dst_ap, slot, incs=[(scast, 1)])
        return scast.n

    for i in range(NB):
        OPQ("sync", "dma_start", out=PARS[:, i * PARC:(i + 1) * PARC], in_=PAR[i],
            incs=[(sstp, 16)])
    OPQ("sync", "dma_start", out=IDT[:, :], in_=IDTd, incs=[(sstp, 16)])
    OPQ("sync", "dma_start", out=ONE1[:, :], in_=ONE1d, incs=[(sstp, 16)])
    OPQ("sync", "dma_start", out=ONEF[:, :], in_=ONEFd, incs=[(sstp, 16)])
    OPQ("sync", "dma_start", out=CONSTC[:, :], in_=CONSTd, incs=[(sstp, 16)])
    setup_dma = sstp.n
    W("vector", sstp, setup_dma)
    W("scalar", sstp, setup_dma)

    for o in range(0 if _SKIP_TERN else NCH):
        g = gcount[0]
        gcount[0] += 1
        slot = WS[:, (g % 2) * 1024:(g % 2) * 1024 + 1024]
        W("sync", scast, g - 1)
        OPQ("sync", "dma_start", out=slot, in_=WT[o * 128:(o + 1) * 128, :],
            incs=[(SW[g % 2], 16)])
        W("vector", SW[g % 2], SW[g % 2].n)
        sraw = COLS[:, CQ_SO + o:CQ_SO + o + 1]
        tcol = COLS[:, 8:9]
        thr_p = COLS2[:, CQ_HSO + o:CQ_HSO + o + 1]
        thr_n = COLS[:, 9:10]
        OPQ("vector", "tensor_reduce", sraw, slot, AX, OP.add,
            apply_absolute_value=True, incs=[(svec, 1)])
        W("scalar", svec, svec.n)
        OPQ("scalar", "activation", tcol, sraw, AF.Relu,
            scale=CONSTC[:, CC_R1024:CC_R1024 + 1], bias=CONSTC[:, CC_NEPS5:CC_NEPS5 + 1])
        OPQ("scalar", "activation", thr_p, tcol, AF.Identity,
            scale=CONSTC[:, CC_HALF:CC_HALF + 1], bias=CONSTC[:, CC_5E6:CC_5E6 + 1])
        OPQ("scalar", "activation", thr_n, tcol, AF.Identity,
            scale=CONSTC[:, CC_NHALF:CC_NHALF + 1], bias=CONSTC[:, CC_N5E6:CC_N5E6 + 1],
            incs=[(ssca, 1)])
        W("vector", ssca, ssca.n)
        for h in range(2):
            sl = slot[:, h * 512:(h + 1) * 512]
            OPQ("vector", "tensor_scalar", KQ[0][:, :], sl, thr_p, None, OP.is_gt)
            OPQ("vector", "tensor_scalar", KQ[1][:, :], sl, thr_n, None, OP.is_lt)
            OPQ("vector", "tensor_tensor",
                AB2[:, AB2_H + o * 1024 + h * 512:AB2_H + o * 1024 + (h + 1) * 512],
                KQ[0][:, :], KQ[1][:, :], OP.subtract)
        MK("vector", (scast, 1))
    W("tensor", scast, scast.n)
    for o in range(0 if (_SKIP_TERN or _SKIP_TPOSE) else NCH):
        for i4 in range(0, NCH, 4):
            W("tensor", svec, svec.n)
            for i in range(i4, i4 + 4):
                OPQ("tensor", "transpose", pT[:, (i - i4) * 128:(i - i4 + 1) * 128],
                    AB2[:, AB2_H + o * 1024 + i * 128:AB2_H + o * 1024 + (i + 1) * 128],
                    IDT[:, :])
            MK("tensor", (stns, 1))
            W("vector", stns, stns.n)
            if os.environ.get("K_SIMPLE_COPY", "0") == "1":
                for i in range(i4, i4 + 4):
                    OPQ("vector", "tensor_copy",
                        AB2[:, AB2_TERN + i * 1024 + o * 128:AB2_TERN + i * 1024 + (o + 1) * 128],
                        pT[:, (i - i4) * 128:(i - i4 + 1) * 128])
                MK("vector", (svec, 1))
            else:
                src = pT[:, 0:512].rearrange("p (j q) -> p j q", j=4)
                dst = AB2[:, AB2_TERN:AB2_TERN + 8192].rearrange(
                    "p (i q) -> p i q", i=NCH)[:, i4:i4 + 4, o * 128:(o + 1) * 128]
                OPQ("vector", "tensor_copy", dst, src, incs=[(svec, 1)])
    tern_ready_vec = svec.n

    W("sync", svec, tern_ready_vec)
    for d in range(NCH):
        OPQ("sync", "dma_start", out=AB2[:, AB2_H + d * 1024:AB2_H + (d + 1) * 1024],
            in_=HAD[d * 128:(d + 1) * 128, :], incs=[(sstp, 16)])
    h_dma = sstp.n
    W("tensor", sstp, h_dma)

    RREP = pEF[:, 0:256]
    MRREP = pEF[:, 512:768]
    PSUM_SUM = pEF[:1, 256:512]
    PSUM_SQ = pEF[:1, 768:1024]

    def ln_stats(eps, src, defer=None):
        eps_col = 0 if eps == EPS_TOP else 1
        for c in range(NCH):
            OPQ("vector", "tensor_copy", SBQ[:, c * 256:(c + 1) * 256], src(c))
        MK("vector", (svec, 1))
        lnsv = svec.n
        if defer is not None:
            defer()
        W("tensor", svec, lnsv)
        W("tensor", ssca, ssca.n)
        for c in range(NCH):
            OPQ("tensor", "matmul", PSUM_SUM[:, :256], ONE1[:, :],
                SBQ[:, c * 256:(c + 1) * 256],
                start=(c == 0), stop=(c == NCH - 1))
        MK("tensor", (stns, 1))
        W("scalar", stns, stns.n)
        for c in range(NCH):
            OPQ("scalar", "activation", SBQ[:, c * 256:(c + 1) * 256], src(c), AF.Square)
        MK("scalar", (ssca, 1))
        W("tensor", ssca, ssca.n)
        for c in range(NCH):
            OPQ("tensor", "matmul", PSUM_SQ[:, :256], ONE1[:, :],
                SBQ[:, c * 256:(c + 1) * 256],
                start=(c == 0), stop=(c == NCH - 1))
        MK("tensor", (stns, 1))
        W("vector", stns, stns.n)
        OPQ("vector", "tensor_copy", ROWS[:, R_A:R_A + 256], PSUM_SUM[:, :256])
        OPQ("vector", "tensor_copy", ROWS[:, R_B:R_B + 256], PSUM_SQ[:, :256])
        OPQ("vector", "tensor_scalar", ROWS[:, R_C:R_C + 256], ROWS[:, R_A:R_A + 256],
            1.0 / D, None, OP.mult)
        OPQ("vector", "scalar_tensor_tensor", ROWS[:, R_A:R_A + 256],
            ROWS[:, R_C:R_C + 256], -1.0, ROWS[:, R_C:R_C + 256], OP.mult, OP.mult)
        OPQ("vector", "scalar_tensor_tensor", ROWS[:, R_D:R_D + 256],
            ROWS[:, R_B:R_B + 256], 1.0 / D, ROWS[:, R_A:R_A + 256], OP.mult, OP.add,
            incs=[(svec, 1)])
        W("scalar", svec, svec.n)
        OPQ("scalar", "activation", ROWS[:, R_B:R_B + 256], ROWS[:, R_D:R_D + 256],
            AF.Ln, bias=CONSTC[:1, eps_col:eps_col + 1])
        OPQ("scalar", "activation", ROWS[:, R_D:R_D + 256], ROWS[:, R_B:R_B + 256],
            AF.Exp, scale=CONSTC[:1, 2:3], incs=[(ssca, 1)])
        W("vector", ssca, ssca.n)
        OPQ("vector", "tensor_tensor", ROWS[:, R_A:R_A + 256], ROWS[:, R_C:R_C + 256],
            ROWS[:, R_D:R_D + 256], OP.mult, incs=[(svec, 1)])
        W("tensor", svec, svec.n)
        OPQ("tensor", "matmul", RREP, ONEF[:, :], ROWS[:1, R_D:R_D + 256],
            start=True, stop=True)
        OPQ("tensor", "matmul", MRREP, ONEF[:, :], ROWS[:1, R_A:R_A + 256],
            start=True, stop=True)
        MK("tensor", (stns, 1))
        W("vector", stns, stns.n)

    def emit_F(kind):
        W("tensor", ssca, ssca.n)
        W("tensor", svec, svec.n)
        if kind == "F2":
            W("vector", stns, stns.n)
            W("scalar", stns, stns.n)
            W("tensor", sstp, sstp.n)
        for tt in range(NT256):
            t0 = tt * 256
            if kind == "F1":
                W("gpsimd", svec, svec.n)
                W("gpsimd", ssca, ssca.n)
                W("gpsimd", sxs, sxs.n)
                OPQ("gpsimd", "dma_start", out=XT.rearrange("p (c t) -> p c t", c=NCH),
                    in_=Xv[:, :, t0:t0 + 256], incs=[(sxl, 16)])
                xt_ready = sxl.n
                W("vector", sxl, xt_ready)
                W("scalar", sxl, xt_ready)
                base = XT
                out_t = STY
            else:
                W("gpsimd", svec, svec.n)
                W("gpsimd", ssca, ssca.n)
                W("gpsimd", sxs, sxs.n)
                OPQ("gpsimd", "dma_start", out=XT.rearrange("p (c t) -> p c t", c=NCH),
                    in_=sav[:, :, t0:t0 + 256], incs=[(sxl, 16)])
                OPQ("gpsimd", "dma_start", out=STY.rearrange("p (c t) -> p c t", c=NCH),
                    in_=sy1v[:, :, t0:t0 + 256], incs=[(sxl, 16)])
                xt_ready = sxl.n
                W("vector", sxl, xt_ready)
                W("scalar", sxl, xt_ready)
                for c in range(NCH):
                    OPQ("vector", "tensor_add", XT[:, c * 256:(c + 1) * 256],
                        XT[:, c * 256:(c + 1) * 256], STY[:, c * 256:(c + 1) * 256])
                MK("vector", (svec, 1))
                W("scalar", svec, svec.n)
                base = XT
                out_t = STY
            src = lambda c: base[:, c * 256:(c + 1) * 256]
            ln_stats(EPS_TOP, src)
            for c in range(NCH):
                OPQ("vector", "tensor_tensor", KQ[0][:, :256], src(c), RREP,
                    OP.mult)
                OPQ("vector", "scalar_tensor_tensor",
                    AB2[:, AB2_LN + c * 256:AB2_LN + (c + 1) * 256],
                    KQ[0][:, :256], 0.0, MRREP, OP.add, OP.subtract)
            MK("vector", (svec, 1))
            ln_done = svec.n
            W("tensor", svec, ln_done)
            for j in range(2):
                W("tensor", svec, svec.n)
                W("tensor", ssca, ssca.n)
                for half, pX in ((0, pE), (1, pF)):
                    for d in range(NCH):
                        OPQ("tensor", "matmul", pX[:, :],
                            AB2[:, AB2_LN + d * 256 + j * 128:AB2_LN + d * 256 + (j + 1) * 128],
                            AB2[:, AB2_H + d * 1024 + half * 512:AB2_H + d * 1024 + (half + 1) * 512],
                            start=(d == 0), stop=(d == NCH - 1))
                MK("tensor", (stns, 1))
                W("vector", stns, stns.n)
                if stop == "XH":
                    OPQ("vector", "tensor_copy", STY[:, j * 1024:j * 1024 + 512], pE[:, :])
                    OPQ("vector", "tensor_copy", STY[:, j * 1024 + 512:j * 1024 + 1024],
                        pF[:, :])
                    MK("vector", (svec, 1))
                OPQ("vector", "tensor_reduce", COLS[:, CQ_M0:CQ_M0 + 1], pEF[:, :], AX,
                    OP.max, apply_absolute_value=True, incs=[(svec, 1)])
                W("scalar", stns, stns.n)
                W("scalar", svec, svec.n)
                OPQ("scalar", "activation", COLS[:, 1:2], COLS[:, CQ_M0:CQ_M0 + 1],
                    AF.Relu, bias=CONSTC[:, CC_NEPS5:CC_NEPS5 + 1])
                OPQ("scalar", "activation", COLS[:, 2:3], COLS[:, 1:2], AF.Ln,
                    bias=CONSTC[:, CC_EPS5:CC_EPS5 + 1])
                OPQ("scalar", "activation", COLS[:, 3:4], COLS[:, 2:3], AF.Exp,
                    scale=CONSTC[:, CC_NONE:CC_NONE + 1],
                    bias=CONSTC[:, CC_LN7:CC_LN7 + 1])
                OPQ("scalar", "activation", COLS[:, 4:5], COLS[:, 2:3], AF.Exp,
                    bias=CONSTC[:, CC_NLN7:CC_NLN7 + 1])
                OPQ("scalar", "activation", COLS[:, 5:6], COLS[:, 4:5], AF.Identity,
                    scale=CONSTC[:, CC_N8:CC_N8 + 1])
                for half, pX in ((0, pE), (1, pF)):
                    OPQ("scalar", "activation", KQ[half][:, :], pX, AF.Identity,
                        scale=COLS[:, 3:4], bias=CONSTC[:, CC_85:CC_85 + 1])
                MK("scalar", (ssca, 1))
                if stop == "QCL":
                    W("vector", ssca, ssca.n)
                    OPQ("vector", "tensor_copy", STY[:, (tt * 2 + j) * 8:(tt * 2 + j) * 8 + 8],
                        COLS[:, 0:8])
                    MK("vector", (svec, 1))
                W("vector", ssca, ssca.n)
                for half in (0, 1):
                    zin = KQ[half]
                    for cst in (8.0, 4.0, 2.0, 1.0):
                        OPQ("vector", "tensor_scalar", KQ[4 + half][:, :], zin[:, :],
                            cst, None, OP.is_ge)
                        OPQ("vector", "scalar_tensor_tensor", KQ[2 + half][:, :],
                            KQ[4 + half][:, :], -cst, zin[:, :], OP.mult, OP.add)
                        zin = KQ[2 + half]
                    OPQ("vector", "scalar_tensor_tensor", KQ[4 + half][:, :],
                        KQ[half][:, :], 1.0, KQ[2 + half][:, :], OP.mult, OP.subtract)
                MK("vector", (svec, 1))
                W("scalar", svec, svec.n)
                for half in (0, 1):
                    OPQ("scalar", "activation", SBQ[:, half * 512:(half + 1) * 512],
                        KQ[4 + half][:, :], AF.Identity,
                        scale=COLS[:, 4:5], bias=COLS[:, 5:6])
                MK("scalar", (ssca, 1))
                W("vector", ssca, ssca.n)
                if stop == "XQT":
                    OPQ("vector", "tensor_copy", STY[:, j * 1024:(j + 1) * 1024],
                        SBQ[:, 0:1024])
                    MK("vector", (svec, 1))
                W("tensor", svec, svec.n)
                W("tensor", ssca, ssca.n)
                for h in range(NCH):
                    OPQ("tensor", "transpose", pT[:, h * 128:(h + 1) * 128],
                        SBQ[:, h * 128:(h + 1) * 128], IDT[:, :])
                MK("tensor", (stns, 1))
                W("vector", stns, stns.n)
                dst = AB2[:, AB2_XQ:AB2_XQ + 2048].rearrange(
                    "p (h q) -> p h q", h=NCH)[:, :, j * 128:(j + 1) * 128]
                OPQ("vector", "tensor_copy", dst,
                    pT[:, :].rearrange("p (h q) -> p h q", h=NCH), incs=[(svec, 1)])
            xq_ready = svec.n
            if stop in ("XH", "XQT"):
                W("gpsimd", svec, svec.n)
                OPQ("gpsimd", "dma_start", out=Yv[:, :, t0:t0 + 256],
                    in_=STY.rearrange("p (c t) -> p c t", c=NCH), incs=[(sout, 16)])
                continue
            if stop == "QNT":
                for c in range(NCH):
                    OPQ("vector", "tensor_copy", STY[:, c * 256:(c + 1) * 256],
                        AB2[:, AB2_XQ + c * 256:AB2_XQ + (c + 1) * 256])
                MK("vector", (svec, 1))
                W("gpsimd", svec, svec.n)
                OPQ("gpsimd", "dma_start", out=Yv[:, :, t0:t0 + 256],
                    in_=STY.rearrange("p (c t) -> p c t", c=NCH), incs=[(sout, 16)])
                continue
            W("tensor", svec, xq_ready)
            for o in range(NCH):
                pX = pE if (o % 2 == 0) else pF
                if o >= 2:
                    W("tensor", svec, svec.n)
                for h in range(NCH):
                    OPQ("tensor", "matmul", pX[:, :256],
                        AB2[:, AB2_TERN + h * 1024 + o * 128:AB2_TERN + h * 1024 + (o + 1) * 128],
                        AB2[:, AB2_XQ + h * 256:AB2_XQ + (h + 1) * 256],
                        start=(h == 0), stop=(h == NCH - 1))
                MK("tensor", (stns, 1))
                W("vector", stns, stns.n)
                OPQ("vector", "scalar_tensor_tensor", out_t[:, o * 256:(o + 1) * 256],
                    pX[:, :256], COLS2[:, CQ_HSO + o:CQ_HSO + o + 1],
                    base[:, o * 256:(o + 1) * 256], OP.mult, OP.add, incs=[(svec, 1)])
            ytile_done = svec.n
            if kind == "F1":
                W("gpsimd", svec, ytile_done)
                OPQ("gpsimd", "dma_start", out=sy1v[:, :, t0:t0 + 256],
                    in_=STY.rearrange("p (c t) -> p c t", c=NCH), incs=[(sxs, 16)])
                srcb = lambda c: STY[:, c * 256:(c + 1) * 256]
                ln_stats(EPS_TOP, srcb)
                for c in range(NCH):
                    OPQ("vector", "tensor_tensor", KQ[0][:, :256], srcb(c),
                        RREP, OP.mult)
                    OPQ("vector", "scalar_tensor_tensor", XT[:, c * 256:(c + 1) * 256],
                        KQ[0][:, :256], 0.0, MRREP, OP.add, OP.subtract)
                MK("vector", (svec, 1))
                W("scalar", svec, svec.n)
                for c in range(NCH):
                    OPQ("scalar", "activation", XB[:, c * T + t0:c * T + t0 + 256],
                        XT[:, c * 256:(c + 1) * 256], AF.Identity)
                MK("scalar", (ssca, 1))
                W("gpsimd", svec, svec.n)
                W("gpsimd", ssca, ssca.n)
                OPQ("gpsimd", "dma_start", out=sav[:, :, t0:t0 + 256],
                    in_=XT.rearrange("p (c t) -> p c t", c=NCH), incs=[(sxs, 16)])
            else:
                W("gpsimd", svec, ytile_done)
                OPQ("gpsimd", "dma_start", out=Yv[:, :, t0:t0 + 256],
                    in_=STY.rearrange("p (c t) -> p c t", c=NCH), incs=[(sout, 16)])

    def emit_block(i, first_w2_gate=0):
        bq = lambda a: pcol(i, PC_BQ, a)
        bk = lambda a: pcol(i, PC_BK, a)
        bv = lambda a: pcol(i, PC_BV, a)
        bo = lambda c: pcol(i, PC_BO, c)
        b1 = lambda f: pcol(i, PC_B1, f)
        b2 = lambda c: pcol(i, PC_B2, c)
        g1 = lambda c: pcol(i, PC_G1, c)
        be1 = lambda c: pcol(i, PC_BE1, c)
        g2 = lambda c: pcol(i, PC_G2, c)
        be2 = lambda c: pcol(i, PC_BE2, c)

        W("tensor", ssca, ssca.n)
        W("tensor", svec, svec.n)
        entry_tns = stns.n
        wk_g = [granule(WKa[i][d * 128:(d + 1) * 128, :],
                        AB1[:, AB1_WK + d * 1024:AB1_WK + (d + 1) * 1024], entry_tns)
                for d in range(NCH)]
        wv_g = [granule(WVa[i][d * 128:(d + 1) * 128, :],
                        AB1[:, AB1_WV + d * 1024:AB1_WV + (d + 1) * 1024], entry_tns)
                for d in range(NCH)]
        wq_g = [granule(WQa[i][d * 128:(d + 1) * 128, :],
                        AB1[:, AB1_WQ + d * 1024:AB1_WQ + (d + 1) * 1024], entry_tns)
                for d in range(NCH)]
        wo_g = [granule(WOa[i][a * 128:(a + 1) * 128, :],
                        AB1[:, AB1_WO + a * 1024:AB1_WO + (a + 1) * 1024], entry_tns)
                for a in range(NCH)]

        W("tensor", scast, max(max(wk_g), max(wv_g)))
        it = 0
        kvevac = {}
        kbank = [pE, pG]
        vbank = [pF, pA]
        p1_entry = ssca.n
        for a in range(NCH):
            for t in range(NT512):
                t0 = t * 512
                p = it % 2
                if "p1" in _COARSE:
                    W("tensor", ssca, ssca.n)
                    W("tensor", svec, svec.n)
                W("tensor", ssca, kvevac.get(it - 2, p1_entry))
                for d in range(NCH):
                    OPQ("tensor", "matmul", kbank[p][:, :512],
                        AB1[:, AB1_WK + d * 1024 + a * 128:AB1_WK + d * 1024 + (a + 1) * 128],
                        XB[:, d * T + t0:d * T + t0 + 512],
                        start=(d == 0), stop=(d == NCH - 1))
                MK("tensor", (stns, 1))
                for d in range(NCH):
                    OPQ("tensor", "matmul", vbank[p][:, :512],
                        AB1[:, AB1_WV + d * 1024 + a * 128:AB1_WV + d * 1024 + (a + 1) * 128],
                        XB[:, d * T + t0:d * T + t0 + 512],
                        start=(d == 0), stop=(d == NCH - 1))
                MK("tensor", (stns, 1))
                W("scalar", svec, svec.n)
                W("scalar", stns, stns.n - 1)
                OPQ("scalar", "activation", KQ[p][:, :], kbank[p][:, :512], AF.Exp,
                    bias=bk(a))
                OPQ("scalar", "activation", KQ[2 + p][:, :], kbank[p][:, :512], AF.Relu,
                    bias=bk(a))
                W("scalar", stns, stns.n)
                OPQ("scalar", "activation", KQ[5][:, :], vbank[p][:, :512], AF.Identity,
                    bias=bv(a), incs=[(ssca, 2)])
                kvevac[it] = ssca.n
                W("vector", ssca, ssca.n)
                OPQ("vector", "scalar_tensor_tensor", KQ[4][:, :], KQ[p][:, :],
                    1.0, KQ[2 + p][:, :], OP.min, OP.add,
                    accum_out=COLS[:, CL_KS + a * 4 + t:CL_KS + a * 4 + t + 1])
                OPQ("vector", "scalar_tensor_tensor", KQ[p][:, :], KQ[4][:, :],
                    1.0, KQ[5][:, :], OP.mult, OP.mult,
                    accum_out=COLS2[:, CL_KV + a * 4 + t:CL_KV + a * 4 + t + 1],
                    incs=[(svec, 1)])
                it += 1
        W("scalar", svec, svec.n)
        W("scalar", sgps, sgps.n)
        for a in range(NCH):
            for base, src in ((8 + a, COLS[:, CL_KS + a * 4:]),
                              (a, COLS2[:, CL_KV + a * 4:])):
                OPQ("scalar", "activation", KQ[5][:, 0:1], src[:, 0:1], AF.Identity,
                    bias=src[:, 1:2])
                OPQ("scalar", "activation", KQ[5][:, 1:2], KQ[5][:, 0:1], AF.Identity,
                    bias=src[:, 2:3])
                OPQ("scalar", "activation", PK[:, base:base + 1], KQ[5][:, 1:2],
                    AF.Identity, bias=src[:, 3:4])
        MK("scalar", (ssca, 1))
        pack_done = ssca.n
        W("gpsimd", ssca, pack_done)
        OPQ("gpsimd", "dma_start", out=ccin.ap(), in_=PK[:, :], incs=[(sgps, 16)])
        W("gpsimd", sgps, sgps.n)
        OPQ("gpsimd", "collective_compute", "AllReduce", OP.add,
            replica_groups=[[0, 1, 2, 3], [4, 5, 6, 7]],
            ins=[ccin.ap()], outs=[ccout.ap()], incs=[(scc, 1)])
        W("gpsimd", scc, scc.n)
        OPQ("gpsimd", "dma_start", out=CC[:, :], in_=ccout.ap(), incs=[(sgps, 16)])
        cc_ready = sgps.n

        W("tensor", scast, max(wq_g))
        cc_waited = [False]
        it = 0
        qevac = {}
        qbank = [pE, pF]
        p2_entry = ssca.n
        for a in range(NCH):
            for t in range(NT512):
                t0 = t * 512
                p = it % 2
                pX = qbank[p]
                if "p2" in _COARSE:
                    W("tensor", ssca, ssca.n)
                    W("tensor", svec, svec.n)
                W("tensor", ssca, qevac.get(it - 2, p2_entry))
                for d in range(NCH):
                    OPQ("tensor", "matmul", pX[:, :512],
                        AB1[:, AB1_WQ + d * 1024 + a * 128:AB1_WQ + d * 1024 + (a + 1) * 128],
                        XB[:, d * T + t0:d * T + t0 + 512],
                        start=(d == 0), stop=(d == NCH - 1))
                MK("tensor", (stns, 1))
                W("scalar", svec, svec.n)
                W("scalar", stns, stns.n)
                OPQ("scalar", "activation", KQ[p][:, :], pX[:, :512], AF.Exp, bias=bq(a))
                OPQ("scalar", "activation", KQ[2 + p][:, :], pX[:, :512], AF.Relu,
                    bias=bq(a), incs=[(ssca, 1)])
                qevac[it] = ssca.n
                W("vector", ssca, ssca.n)
                OPQ("vector", "scalar_tensor_tensor", KQ[4][:, :], KQ[p][:, :],
                    1.0, KQ[2 + p][:, :], OP.min, OP.add, incs=[(svec, 1)])
                if not cc_waited[0]:
                    W("vector", sgps, cc_ready)
                    cc_waited[0] = True
                OPQ("vector", "tensor_scalar", KQ[5][:, :], KQ[4][:, :],
                    CC[:, 8 + a:9 + a], 1e-6, OP.mult, OP.add, incs=[(svec, 1)])
                W("scalar", svec, svec.n)
                OPQ("scalar", "activation", KQ[p][:, :], KQ[5][:, :], AF.Ln)
                OPQ("scalar", "activation", KQ[2 + p][:, :], KQ[p][:, :], AF.Exp,
                    scale=CONSTC[:, 3:4], incs=[(ssca, 1)])
                W("vector", ssca, ssca.n)
                OPQ("vector", "tensor_tensor", KQ[5][:, :], KQ[4][:, :],
                    KQ[2 + p][:, :], OP.mult)
                OPQ("vector", "tensor_scalar",
                    AB2[:, AB2_V + a * T + t0:AB2_V + a * T + t0 + 512],
                    KQ[5][:, :], CC[:, a:a + 1], None, OP.mult, incs=[(svec, 1)])
                it += 1
        v_ready = svec.n

        W("tensor", scast, max(wo_g))
        W("tensor", svec, v_ready)
        p3st = {"evac_prev": ssca.n}
        omm_done = {}

        def emit_omms(tt):
            t0 = tt * 256
            W("tensor", ssca, p3st["evac_prev"])
            for do in range(NCH):
                for a in range(NCH):
                    OPQ("tensor", "matmul",
                        PW[do % 4][:, (do // 4) * 256:(do // 4) * 256 + 256],
                        AB1[:, AB1_WO + a * 1024 + do * 128:AB1_WO + a * 1024 + (do + 1) * 128],
                        AB2[:, AB2_V + a * T + t0:AB2_V + a * T + t0 + 256],
                        start=(a == 0), stop=(a == NCH - 1))
            MK("tensor", (stns, 1))
            omm_done[tt] = stns.n

        for tt in range(NT256):
            t0 = tt * 256
            W("gpsimd", svec, svec.n)
            W("gpsimd", ssca, ssca.n)
            W("gpsimd", sxs, sxs.n)
            OPQ("gpsimd", "dma_start", out=XT.rearrange("p (c t) -> p c t", c=NCH),
                in_=sav[:, :, t0:t0 + 256], incs=[(sxl, 16)])
            x1_ready = sxl.n
            if tt == 0:
                emit_omms(0)
            W("scalar", stns, omm_done[tt])
            W("scalar", sxl, x1_ready)
            W("vector", sxl, x1_ready)
            for do in range(NCH):
                kk = (do % 2) * 2
                W("scalar", svec, svec.n)
                OPQ("scalar", "activation", KQ[kk][:, :256],
                    PW[do % 4][:, (do // 4) * 256:(do // 4) * 256 + 256],
                    AF.Gelu_apprx_tanh, bias=bo(do))
                OPQ("scalar", "activation", KQ[kk + 1][:, :256], KQ[kk][:, :256],
                    AF.Gelu_apprx_tanh, incs=[(ssca, 1)])
                W("vector", ssca, ssca.n)
                OPQ("vector", "tensor_add", STY[:, do * 256:(do + 1) * 256],
                    KQ[kk + 1][:, :256], XT[:, do * 256:(do + 1) * 256],
                    incs=[(svec, 1)])
            p3st["evac_prev"] = ssca.n
            ln_stats(EPS_BLK, lambda c: STY[:, c * 256:(c + 1) * 256],
                     defer=(lambda t=tt + 1: emit_omms(t)) if tt + 1 < NT256 else None)
            for c in range(NCH):
                OPQ("vector", "tensor_tensor", KQ[0][:, :256],
                    STY[:, c * 256:(c + 1) * 256], RREP, OP.mult)
                OPQ("vector", "scalar_tensor_tensor", KQ[1][:, :256], KQ[0][:, :256],
                    0.0, MRREP, OP.add, OP.subtract)
                OPQ("vector", "tensor_scalar", XT[:, c * 256:(c + 1) * 256],
                    KQ[1][:, :256], g1(c), be1(c), OP.mult, OP.add)
            MK("vector", (svec, 1))
            W("scalar", svec, svec.n)
            for c in range(NCH):
                OPQ("scalar", "activation", XB[:, c * T + t0:c * T + t0 + 256],
                    XT[:, c * 256:(c + 1) * 256], AF.Identity)
            MK("scalar", (ssca, 1))
            W("gpsimd", svec, svec.n)
            W("gpsimd", ssca, ssca.n)
            OPQ("gpsimd", "dma_start", out=sbv[:, :, t0:t0 + 256],
                in_=XT.rearrange("p (c t) -> p c t", c=NCH), incs=[(sxs, 16)])
        p3_tns = stns.n

        w1_g = {}
        for s in range(4):
            for d in range(NCH):
                w1_g[(d, s)] = granule(
                    W1a[i][d * 128:(d + 1) * 128, s * 1024:(s + 1) * 1024],
                    AB1[:, d * 4096 + s * 1024:d * 4096 + (s + 1) * 1024], p3_tns)
        w2_g = {}
        for f in range(32):
            w2_g[f] = granule(W2a[i][f * 128:(f + 1) * 128, :],
                              AB2[:, f * 1024:(f + 1) * 1024], p3_tns,
                              extra_gps_gate=(first_w2_gate if f == 0 else 0))

        p4st = {"hb_prev": ssca.n, "f1_prev": ssca.n}
        sweep_done = {}

        def emit_sweep(tt):
            t0 = tt * 256
            W("tensor", svec, svec.n)
            hbcnt = {}

            def emit_fmm(f):
                W("tensor", ssca, hbcnt[f])
                if f == 0:
                    W("tensor", ssca, p4st["f1_prev"])
                for c in range(NCH):
                    OPQ("tensor", "matmul",
                        PW[c % 4][:, (c // 4) * 256:(c // 4) * 256 + 256],
                        AB2[:, f * 1024 + c * 128:f * 1024 + (c + 1) * 128],
                        HB[:, (f % 2) * 256:(f % 2) * 256 + 256],
                        start=(f == 0 and c < 4), stop=(f == 31 and c >= 4))

            for f in range(32):
                need = max(max(w1_g[(d, f // 8)] for d in range(NCH)), w2_g[f])
                W("tensor", scast, need)
                W("tensor", ssca, ssca.n)
                ph = pG[:, 0:256] if f % 2 == 0 else pEF[:, 0:256]
                for d in range(NCH):
                    OPQ("tensor", "matmul", ph,
                        AB1[:, d * 4096 + f * 128:d * 4096 + (f + 1) * 128],
                        XB[:, d * T + t0:d * T + t0 + 256],
                        start=(d == 0), stop=(d == NCH - 1))
                MK("tensor", (stns, 1))
                W("scalar", stns, stns.n)
                hb = HB[:, (f % 2) * 256:(f % 2) * 256 + 256]
                OPQ("scalar", "activation", hb, ph, AF.Gelu_apprx_tanh,
                    bias=b1(f), incs=[(ssca, 1)])
                hbcnt[f] = ssca.n
                if f >= 1:
                    emit_fmm(f - 1)
            emit_fmm(31)
            p4st["hb_prev"] = hbcnt[31]
            MK("tensor", (stns, 1))
            sweep_done[tt] = stns.n

        for tt in range(NT256):
            t0 = tt * 256
            W("gpsimd", svec, svec.n)
            W("gpsimd", ssca, ssca.n)
            W("gpsimd", sxs, sxs.n)
            OPQ("gpsimd", "dma_start", out=XT.rearrange("p (c t) -> p c t", c=NCH),
                in_=sbv[:, :, t0:t0 + 256], incs=[(sxl, 16)])
            x2_ready = sxl.n
            W("vector", sxl, x2_ready)
            if tt == 0:
                emit_sweep(0)
            W("scalar", stns, sweep_done[tt])
            for c in range(NCH):
                kk = c % 2
                W("scalar", svec, svec.n)
                OPQ("scalar", "activation", KQ[kk][:, :256],
                    PW[c % 4][:, (c // 4) * 256:(c // 4) * 256 + 256],
                    AF.Gelu_apprx_tanh, bias=b2(c), incs=[(ssca, 1)])
                W("vector", ssca, ssca.n)
                OPQ("vector", "tensor_add", STY[:, c * 256:(c + 1) * 256],
                    KQ[kk][:, :256], XT[:, c * 256:(c + 1) * 256], incs=[(svec, 1)])
            p4st["f1_prev"] = ssca.n
            ln_stats(EPS_BLK, lambda c: STY[:, c * 256:(c + 1) * 256],
                     defer=(lambda t=tt + 1: emit_sweep(t)) if tt + 1 < NT256 else None)
            for c in range(NCH):
                OPQ("vector", "tensor_tensor", KQ[0][:, :256],
                    STY[:, c * 256:(c + 1) * 256], RREP, OP.mult)
                OPQ("vector", "scalar_tensor_tensor", KQ[1][:, :256], KQ[0][:, :256],
                    0.0, MRREP, OP.add, OP.subtract)
                OPQ("vector", "tensor_scalar", XT[:, c * 256:(c + 1) * 256],
                    KQ[1][:, :256], g2(c), be2(c), OP.mult, OP.add)
            MK("vector", (svec, 1))
            W("scalar", svec, svec.n)
            for c in range(NCH):
                OPQ("scalar", "activation", XB[:, c * T + t0:c * T + t0 + 256],
                    XT[:, c * 256:(c + 1) * 256], AF.Identity)
            MK("scalar", (ssca, 1))
            W("gpsimd", svec, svec.n)
            W("gpsimd", ssca, ssca.n)
            OPQ("gpsimd", "dma_start", out=sav[:, :, t0:t0 + 256],
                in_=XT.rearrange("p (c t) -> p c t", c=NCH), incs=[(sxs, 16)])

    def dump_stream(view):
        for tt in range(NT256):
            t0 = tt * 256
            W("gpsimd", svec, svec.n)
            W("gpsimd", ssca, ssca.n)
            W("gpsimd", sxs, sxs.n)
            OPQ("gpsimd", "dma_start", out=XT.rearrange("p (c t) -> p c t", c=NCH),
                in_=view[:, :, t0:t0 + 256], incs=[(sxl, 16)])
            W("gpsimd", sxl, sxl.n)
            OPQ("gpsimd", "dma_start", out=Yv[:, :, t0:t0 + 256],
                in_=XT.rearrange("p (c t) -> p c t", c=NCH), incs=[(sout, 16)])

    if stop == "S0":
        dump_stream(Xv)
        W("gpsimd", sout, sout.n)
        with es, nc.Block() as block:
            for eng_name in Prog.ENGS:
                getattr(block, eng_name)(
                    lambda e, q=P.q[eng_name], n=eng_name: _runq_outer(e, q, n))
        return nc
    if stop == "LN0":
        for tt in range(NT256):
            t0 = tt * 256
            W("gpsimd", svec, svec.n)
            W("gpsimd", ssca, ssca.n)
            W("gpsimd", sxs, sxs.n)
            OPQ("gpsimd", "dma_start", out=XT.rearrange("p (c t) -> p c t", c=NCH),
                in_=Xv[:, :, t0:t0 + 256], incs=[(sxl, 16)])
            W("vector", sxl, sxl.n)
            W("scalar", sxl, sxl.n)
            ln_stats(EPS_TOP, lambda c: XT[:, c * 256:(c + 1) * 256])
            for c in range(NCH):
                OPQ("vector", "tensor_tensor", KQ[0][:, :256],
                    XT[:, c * 256:(c + 1) * 256], RREP, OP.mult)
                OPQ("vector", "scalar_tensor_tensor", STY[:, c * 256:(c + 1) * 256],
                    KQ[0][:, :256], 0.0, MRREP, OP.add, OP.subtract)
            MK("vector", (svec, 1))
            W("gpsimd", svec, svec.n)
            OPQ("gpsimd", "dma_start", out=Yv[:, :, t0:t0 + 256],
                in_=STY.rearrange("p (c t) -> p c t", c=NCH), incs=[(sout, 16)])
        W("gpsimd", sout, sout.n)
        with es, nc.Block() as block:
            for eng_name in Prog.ENGS:
                getattr(block, eng_name)(
                    lambda e, q=P.q[eng_name], n=eng_name: _runq_outer(e, q, n))
        return nc
    emit_F("F1")
    W("gpsimd", stns, stns.n)
    OPQ("gpsimd", "dma_start", out=ternsp.ap(), in_=AB2[:, AB2_TERN:AB2_TERN + 8192],
        incs=[(ssp, 16)])
    tern_spilled = ssp.n

    n_blocks = {"F1": 0, "F1Y": 0, "QNT": 0, "XH": 0, "XQT": 0, "QCL": 0, "B0": 1, "B1": 2, "B2": 3, "B3": 4, "full": NB}[stop]
    for i in range(n_blocks):
        emit_block(i, first_w2_gate=(tern_spilled if i == 0 else 0))


    if stop == "full":
        W("sync", stns, stns.n)
        W("sync", ssp, tern_spilled)
        OPQ("sync", "dma_start", out=AB2[:, AB2_TERN:AB2_TERN + 8192], in_=ternsp.ap(),
            incs=[(sstp, 16)])
        for d in range(NCH):
            OPQ("sync", "dma_start",
                out=AB2[:, AB2_H + d * 1024:AB2_H + (d + 1) * 1024],
                in_=HAD[d * 128:(d + 1) * 128, :], incs=[(sstp, 16)])
        emit_F("F2")
    elif stop == "F1Y":
        dump_stream(sy1v)
    elif stop in ("QNT", "XH", "XQT", "QCL"):
        pass
    else:
        dump_stream(sav)

    W("gpsimd", sout, sout.n)

    with es, nc.Block() as block:
        for eng_name in Prog.ENGS:
            getattr(block, eng_name)(
                lambda e, q=P.q[eng_name], n=eng_name: _runq_outer(e, q, n))
    return nc


def _hadamard():
    h = np.array([[1.0]], dtype=np.float64)
    while h.shape[0] < D:
        h = np.block([[h, h], [h, -h]])
    return (h / np.sqrt(D)).astype(np.float32)


def _pack_params(inp):
    Pm = np.zeros((NB, 128, PARC), dtype=np.float32)
    cols = lambda v: np.asarray(v, np.float32).reshape(-1, 128).T
    for i in range(NB):
        Pm[i, :, PC_BQ:PC_BQ + 8] = cols(inp["L0_attn_bq"][i])
        Pm[i, :, PC_BK:PC_BK + 8] = cols(inp["L0_attn_bk"][i])
        Pm[i, :, PC_BV:PC_BV + 8] = cols(inp["L0_attn_bv"][i])
        Pm[i, :, PC_BO:PC_BO + 8] = cols(inp["L0_attn_bo"][i])
        Pm[i, :, PC_B1:PC_B1 + 32] = cols(inp["L0_attn_b1"][i])
        Pm[i, :, PC_B2:PC_B2 + 8] = cols(inp["L0_attn_b2"][i])
        Pm[i, :, PC_G1:PC_G1 + 8] = cols(inp["L0_attn_ln1_s"][i])
        Pm[i, :, PC_BE1:PC_BE1 + 8] = cols(inp["L0_attn_ln1_b"][i])
        Pm[i, :, PC_G2:PC_G2 + 8] = cols(inp["L0_attn_ln2_s"][i])
        Pm[i, :, PC_BE2:PC_BE2 + 8] = cols(inp["L0_attn_ln2_b"][i])
    return Pm


def kernel(_stop="full", **inputs):
    inp = {k: np.asarray(v) for k, v in inputs.items()}
    x = np.asarray(inp["L0_attn_x"], np.float32).reshape(16384, D)
    common = dict(
        WQ=np.ascontiguousarray(np.asarray(inp["L0_attn_Wq"], np.float32)),
        WK=np.ascontiguousarray(np.asarray(inp["L0_attn_Wk"], np.float32)),
        WV=np.ascontiguousarray(np.asarray(inp["L0_attn_Wv"], np.float32)),
        WO=np.ascontiguousarray(np.asarray(inp["L0_attn_Wo"], np.float32)),
        W1=np.ascontiguousarray(np.asarray(inp["L0_attn_W1"], np.float32)),
        W2=np.ascontiguousarray(np.asarray(inp["L0_attn_W2"], np.float32)),
        PAR=_pack_params(inp),
        WT=np.ascontiguousarray(np.asarray(inp["L0_ffn_weight"], np.float32)),
        HAD=np.ascontiguousarray(_hadamard().astype(ml_dtypes.bfloat16)),
        IDT=np.eye(128, dtype=ml_dtypes.bfloat16),
        ONE1=np.ones((128, 1), dtype=ml_dtypes.bfloat16),
        ONEF=np.ones((1, 128), dtype=np.float32),
        CONST=np.broadcast_to(np.array(
            [1e-5, 1e-6, -0.5, -1.0, -1e-5, np.log(7.0), -np.log(7.0), 1.0 / 1024,
             0.5, 5e-6, -8.0, 8.5, -5e-6, 0.0, 0.0, 0.0], np.float32),
            (128, 16)).copy(),
    )
    in_maps = []
    for c in range(8):
        m = dict(common)
        m["X"] = np.ascontiguousarray(x[c * T:(c + 1) * T, :].T)
        in_maps.append(m)
    nc = build(stop=_stop)
    res = run_bass_kernel_spmd(nc, in_maps, core_ids=list(range(8)))
    out = np.concatenate([np.asarray(res.results[c]["Y"], np.float32).T
                          for c in range(8)], axis=0)
    return np.ascontiguousarray(out.reshape(2, 8192, D))


# version 05
# speedup vs baseline: 1.0028x; 1.0028x over previous
import os
import numpy as np
import ml_dtypes
from contextlib import ExitStack
_SKIP_TERN = os.environ.get("K_SKIP_TERN", "0") == "1"
_COARSE = set(os.environ.get("K_COARSE", "").split(","))
_SKIP_TPOSE = os.environ.get("K_SKIP_TPOSE", "0") == "1"

import concourse.bass as bass
import concourse.mybir as mybir
from concourse.bass_utils import run_bass_kernel_spmd

F32 = mybir.dt.float32
BF16 = mybir.dt.bfloat16
AF = mybir.ActivationFunctionType
OP = mybir.AluOpType
AX = mybir.AxisListType.X

D = 1024
T = 2048
NCH = 8
NB = 4
FF = 4096
NT512 = T // 512
NT256 = T // 256
EPS_TOP = 1e-5
EPS_BLK = 1e-6

PC_BQ, PC_BK, PC_BV, PC_BO = 0, 8, 16, 24
PC_B1, PC_B2 = 32, 64
PC_G1, PC_BE1, PC_G2, PC_BE2 = 72, 80, 88, 96
PARC = 104

AB1_WK, AB1_WV, AB1_WQ, AB1_WO = 0, 8192, 16384, 24576
AB2_V = 0
AB2_TERN, AB2_H, AB2_XQ, AB2_LN = 0, 8192, 16384, 18432

CL_KS = 0
CQ_M0, CQ_M1, CQ_MM, CQ_MC, CQ_RC, CQ_R7, CQ_NS = 0, 1, 2, 3, 4, 5, 6
CQ_SO = 24
CL_KV = 0
CQ_HSO = 32

R_A, R_B, R_C, R_D = 0, 256, 512, 768

CC_EPS5, CC_EPS6, CC_NHALF, CC_NONE, CC_NEPS5, CC_LN7, CC_NLN7, CC_R1024 = range(8)
CC_HALF, CC_5E6, CC_N8, CC_85, CC_N5E6 = 8, 9, 10, 11, 12


class Sem:
    def __init__(self, h):
        self.h = h
        self.n = 0


class Prog:
    ENGS = ("sync", "vector", "scalar", "tensor", "gpsimd")

    def __init__(self):
        self.q = {e: [] for e in self.ENGS}

    def op(self, eng, method, *args, incs=(), **kw):
        for s, k in incs:
            s.n += k
        bound = [(s.h, k) for s, k in incs]

        def fn(e, method=method, args=args, kw=kw, bound=bound):
            ins = getattr(e, method)(*args, **kw)
            for h, k in bound:
                ins.then_inc(h, k)
            return ins
        self.q[eng].append(("op", fn))

    def mark(self, eng, *incs):
        for s, k in incs:
            s.n += k
        bound = [(s.h, k) for s, k in incs]
        self.q[eng].append(("mark", bound))

    def wait(self, eng, sem, val):
        if val <= 0:
            return
        self.q[eng].append(("wait", lambda e, h=sem.h, v=val: e.wait_ge(h, v)))


def _runq_outer(e, q, name):
    last = None
    for idx, ent in enumerate(q):
        kind, payload = ent
        try:
            if kind == "op":
                last = payload(e)
            elif kind == "wait":
                payload(e)
            else:
                assert last is not None, "mark with no preceding op"
                for h, k in payload:
                    last.then_inc(h, k)
        except Exception:
            import sys
            print(f"FAILED {name} entry #{idx} kind={kind}", file=sys.stderr)
            raise


def build(stop="full"):
    nc = bass.Bass("TRN2", target_bir_lowering=False, detect_race_conditions=False)

    X = nc.dram_tensor("X", [D, T], F32, kind="ExternalInput").ap()
    WQa = nc.dram_tensor("WQ", [NB, D, D], F32, kind="ExternalInput").ap()
    WKa = nc.dram_tensor("WK", [NB, D, D], F32, kind="ExternalInput").ap()
    WVa = nc.dram_tensor("WV", [NB, D, D], F32, kind="ExternalInput").ap()
    WOa = nc.dram_tensor("WO", [NB, D, D], F32, kind="ExternalInput").ap()
    W1a = nc.dram_tensor("W1", [NB, D, FF], F32, kind="ExternalInput").ap()
    W2a = nc.dram_tensor("W2", [NB, FF, D], F32, kind="ExternalInput").ap()
    PAR = nc.dram_tensor("PAR", [NB, 128, PARC], F32, kind="ExternalInput").ap()
    WT = nc.dram_tensor("WT", [D, D], F32, kind="ExternalInput").ap()
    HAD = nc.dram_tensor("HAD", [D, D], BF16, kind="ExternalInput").ap()
    IDTd = nc.dram_tensor("IDT", [128, 128], BF16, kind="ExternalInput").ap()
    ONE1d = nc.dram_tensor("ONE1", [128, 1], BF16, kind="ExternalInput").ap()
    ONEFd = nc.dram_tensor("ONEF", [1, 128], F32, kind="ExternalInput").ap()
    CONSTd = nc.dram_tensor("CONST", [128, 16], F32, kind="ExternalInput").ap()
    Y = nc.dram_tensor("Y", [D, T], F32, kind="ExternalOutput").ap()

    sy1 = nc.dram_tensor("sy1", [D, T], F32)
    sa = nc.dram_tensor("sa", [D, T], F32)
    sbm = nc.dram_tensor("sbm", [D, T], F32)
    ccin = nc.dram_tensor("ccin", [128, 16], F32)
    ccout = nc.dram_tensor("ccout", [128, 16], F32)
    ternsp = nc.dram_tensor("ternsp", [128, 8192], BF16)

    Xv = X.rearrange("(c p) t -> p c t", p=128)
    Yv = Y.rearrange("(c p) t -> p c t", p=128)
    sy1v = sy1.ap().rearrange("(c p) t -> p c t", p=128)
    sav = sa.ap().rearrange("(c p) t -> p c t", p=128)
    sbv = sbm.ap().rearrange("(c p) t -> p c t", p=128)

    es = ExitStack()
    sb_ = lambda n, sh, dt=F32: es.enter_context(nc.sbuf_tensor(n, sh, dt))
    ps_ = lambda n, sh, dt=F32: es.enter_context(nc.psum_tensor(n, sh, dt))

    XB = sb_("XB", [128, NCH * T], BF16)
    AB1 = sb_("AB1", [128, 32768], BF16)
    AB2 = sb_("AB2", [128, 32768], BF16)
    WS = sb_("WS", [128, 2048], F32)
    XT = sb_("XT", [128, 2048], F32)
    STY = sb_("STY", [128, 2048], F32)
    SBQ = sb_("SBQ", [128, 2048], BF16)
    KQ = [sb_(f"KQ{i}", [128, 512]) for i in range(6)]
    HB = sb_("HB", [128, 512], BF16)
    COLS = sb_("COLS", [128, 32])
    COLS2 = sb_("COLS2", [128, 40])
    PK = sb_("PK", [128, 16])
    CC = PK
    PARS = sb_("PARS", [128, NB * PARC])
    ROWS = sb_("ROWS", [1, 1024])
    IDT = sb_("IDT_s", [128, 128], BF16)
    ONE1 = sb_("ONE1_s", [128, 1], BF16)
    ONEF = sb_("ONEF_s", [1, 128], F32)
    CONSTC = sb_("CONST_s", [128, 16], F32)

    pA = ps_("pA", [128, 512])
    pB = ps_("pB", [128, 512])
    pC = ps_("pC", [128, 512])
    pD = ps_("pD", [128, 512])
    pEF = ps_("pEF", [128, 1024])
    pE = pEF[:, 0:512]
    pF = pEF[:, 512:1024]
    pG = ps_("pG", [128, 512])
    pT = ps_("pT", [128, 1024], BF16)
    PW = [pA, pB, pC, pD]

    semnames = ("sstp", "sw0", "sw1", "sxl", "sxs", "ssp", "scast", "stns",
                "svec", "ssca", "sgps", "scc", "sout")
    sems = {k: Sem(es.enter_context(nc.semaphore(k))) for k in semnames}
    (sstp, sw0, sw1, sxl, sxs, ssp, scast, stns, svec, ssca, sgps, scc, sout) = (
        sems[k] for k in semnames)
    SW = [sw0, sw1]

    P = Prog()
    OPQ = P.op
    W = P.wait
    MK = P.mark

    def pcol(blk, base, idx=0):
        c = blk * PARC + base + idx
        return PARS[:, c:c + 1]

    gcount = [0]

    def granule(src_ap, dst_ap, area_gate_tns=0, extra_gps_gate=0):
        g = gcount[0]
        gcount[0] += 1
        slot = WS[:, (g % 2) * 1024:(g % 2) * 1024 + 1024]
        W("sync", scast, g - 1)
        OPQ("sync", "dma_start", out=slot, in_=src_ap, incs=[(SW[g % 2], 16)])
        W("vector", SW[g % 2], SW[g % 2].n)
        if area_gate_tns:
            W("vector", stns, area_gate_tns)
        if extra_gps_gate:
            W("vector", ssp, extra_gps_gate)
        OPQ("vector", "tensor_copy", dst_ap, slot, incs=[(scast, 1)])
        return scast.n

    for i in range(NB):
        OPQ("sync", "dma_start", out=PARS[:, i * PARC:(i + 1) * PARC], in_=PAR[i],
            incs=[(sstp, 16)])
    OPQ("sync", "dma_start", out=IDT[:, :], in_=IDTd, incs=[(sstp, 16)])
    OPQ("sync", "dma_start", out=ONE1[:, :], in_=ONE1d, incs=[(sstp, 16)])
    OPQ("sync", "dma_start", out=ONEF[:, :], in_=ONEFd, incs=[(sstp, 16)])
    OPQ("sync", "dma_start", out=CONSTC[:, :], in_=CONSTd, incs=[(sstp, 16)])
    setup_dma = sstp.n
    W("vector", sstp, setup_dma)
    W("scalar", sstp, setup_dma)

    for o in range(0 if _SKIP_TERN else NCH):
        g = gcount[0]
        gcount[0] += 1
        slot = WS[:, (g % 2) * 1024:(g % 2) * 1024 + 1024]
        W("sync", scast, g - 1)
        OPQ("sync", "dma_start", out=slot, in_=WT[o * 128:(o + 1) * 128, :],
            incs=[(SW[g % 2], 16)])
        W("vector", SW[g % 2], SW[g % 2].n)
        sraw = COLS[:, CQ_SO + o:CQ_SO + o + 1]
        tcol = COLS[:, 8:9]
        thr_p = COLS2[:, CQ_HSO + o:CQ_HSO + o + 1]
        thr_n = COLS[:, 9:10]
        OPQ("vector", "tensor_reduce", sraw, slot, AX, OP.add,
            apply_absolute_value=True, incs=[(svec, 1)])
        W("scalar", svec, svec.n)
        OPQ("scalar", "activation", tcol, sraw, AF.Relu,
            scale=CONSTC[:, CC_R1024:CC_R1024 + 1], bias=CONSTC[:, CC_NEPS5:CC_NEPS5 + 1])
        OPQ("scalar", "activation", thr_p, tcol, AF.Identity,
            scale=CONSTC[:, CC_HALF:CC_HALF + 1], bias=CONSTC[:, CC_5E6:CC_5E6 + 1])
        OPQ("scalar", "activation", thr_n, tcol, AF.Identity,
            scale=CONSTC[:, CC_NHALF:CC_NHALF + 1], bias=CONSTC[:, CC_N5E6:CC_N5E6 + 1],
            incs=[(ssca, 1)])
        W("vector", ssca, ssca.n)
        for h in range(2):
            sl = slot[:, h * 512:(h + 1) * 512]
            OPQ("vector", "tensor_scalar", KQ[0][:, :], sl, thr_p, None, OP.is_gt)
            OPQ("vector", "tensor_scalar", KQ[1][:, :], sl, thr_n, None, OP.is_lt)
            OPQ("vector", "tensor_tensor",
                AB2[:, AB2_H + o * 1024 + h * 512:AB2_H + o * 1024 + (h + 1) * 512],
                KQ[0][:, :], KQ[1][:, :], OP.subtract)
        MK("vector", (scast, 1))
    W("tensor", scast, scast.n)
    for o in range(0 if (_SKIP_TERN or _SKIP_TPOSE) else NCH):
        for i4 in range(0, NCH, 4):
            W("tensor", svec, svec.n)
            for i in range(i4, i4 + 4):
                OPQ("tensor", "transpose", pT[:, (i - i4) * 128:(i - i4 + 1) * 128],
                    AB2[:, AB2_H + o * 1024 + i * 128:AB2_H + o * 1024 + (i + 1) * 128],
                    IDT[:, :])
            MK("tensor", (stns, 1))
            W("vector", stns, stns.n)
            if os.environ.get("K_SIMPLE_COPY", "0") == "1":
                for i in range(i4, i4 + 4):
                    OPQ("vector", "tensor_copy",
                        AB2[:, AB2_TERN + i * 1024 + o * 128:AB2_TERN + i * 1024 + (o + 1) * 128],
                        pT[:, (i - i4) * 128:(i - i4 + 1) * 128])
                MK("vector", (svec, 1))
            else:
                src = pT[:, 0:512].rearrange("p (j q) -> p j q", j=4)
                dst = AB2[:, AB2_TERN:AB2_TERN + 8192].rearrange(
                    "p (i q) -> p i q", i=NCH)[:, i4:i4 + 4, o * 128:(o + 1) * 128]
                OPQ("vector", "tensor_copy", dst, src, incs=[(svec, 1)])
    tern_ready_vec = svec.n

    W("sync", svec, tern_ready_vec)
    for d in range(NCH):
        OPQ("sync", "dma_start", out=AB2[:, AB2_H + d * 1024:AB2_H + (d + 1) * 1024],
            in_=HAD[d * 128:(d + 1) * 128, :], incs=[(sstp, 16)])
    h_dma = sstp.n
    W("tensor", sstp, h_dma)

    RREP = pEF[:, 0:256]
    MRREP = pEF[:, 512:768]
    PSUM_SUM = pEF[:1, 256:512]
    PSUM_SQ = pEF[:1, 768:1024]

    def ln_stats(eps, src, defer=None):
        eps_col = 0 if eps == EPS_TOP else 1
        for c in range(NCH):
            OPQ("vector", "tensor_copy", SBQ[:, c * 256:(c + 1) * 256], src(c))
        MK("vector", (svec, 1))
        lnsv = svec.n
        if defer is not None:
            defer()
        W("tensor", svec, lnsv)
        W("tensor", ssca, ssca.n)
        for c in range(NCH):
            OPQ("tensor", "matmul", PSUM_SUM[:, :256], ONE1[:, :],
                SBQ[:, c * 256:(c + 1) * 256],
                start=(c == 0), stop=(c == NCH - 1))
        MK("tensor", (stns, 1))
        W("scalar", stns, stns.n)
        for c in range(NCH):
            OPQ("scalar", "activation", SBQ[:, c * 256:(c + 1) * 256], src(c), AF.Square)
        MK("scalar", (ssca, 1))
        W("tensor", ssca, ssca.n)
        for c in range(NCH):
            OPQ("tensor", "matmul", PSUM_SQ[:, :256], ONE1[:, :],
                SBQ[:, c * 256:(c + 1) * 256],
                start=(c == 0), stop=(c == NCH - 1))
        MK("tensor", (stns, 1))
        W("vector", stns, stns.n)
        OPQ("vector", "tensor_copy", ROWS[:, R_A:R_A + 256], PSUM_SUM[:, :256])
        OPQ("vector", "tensor_copy", ROWS[:, R_B:R_B + 256], PSUM_SQ[:, :256])
        OPQ("vector", "tensor_scalar", ROWS[:, R_C:R_C + 256], ROWS[:, R_A:R_A + 256],
            1.0 / D, None, OP.mult)
        OPQ("vector", "scalar_tensor_tensor", ROWS[:, R_A:R_A + 256],
            ROWS[:, R_C:R_C + 256], -1.0, ROWS[:, R_C:R_C + 256], OP.mult, OP.mult)
        OPQ("vector", "scalar_tensor_tensor", ROWS[:, R_D:R_D + 256],
            ROWS[:, R_B:R_B + 256], 1.0 / D, ROWS[:, R_A:R_A + 256], OP.mult, OP.add,
            incs=[(svec, 1)])
        W("scalar", svec, svec.n)
        OPQ("scalar", "activation", ROWS[:, R_B:R_B + 256], ROWS[:, R_D:R_D + 256],
            AF.Ln, bias=CONSTC[:1, eps_col:eps_col + 1])
        OPQ("scalar", "activation", ROWS[:, R_D:R_D + 256], ROWS[:, R_B:R_B + 256],
            AF.Exp, scale=CONSTC[:1, 2:3], incs=[(ssca, 1)])
        W("vector", ssca, ssca.n)
        OPQ("vector", "tensor_tensor", ROWS[:, R_A:R_A + 256], ROWS[:, R_C:R_C + 256],
            ROWS[:, R_D:R_D + 256], OP.mult, incs=[(svec, 1)])
        W("tensor", svec, svec.n)
        OPQ("tensor", "matmul", RREP, ONEF[:, :], ROWS[:1, R_D:R_D + 256],
            start=True, stop=True)
        OPQ("tensor", "matmul", MRREP, ONEF[:, :], ROWS[:1, R_A:R_A + 256],
            start=True, stop=True)
        MK("tensor", (stns, 1))
        W("vector", stns, stns.n)

    def emit_F(kind):
        W("tensor", ssca, ssca.n)
        W("tensor", svec, svec.n)
        if kind == "F2":
            W("vector", stns, stns.n)
            W("scalar", stns, stns.n)
            W("tensor", sstp, sstp.n)
        for tt in range(NT256):
            t0 = tt * 256
            if kind == "F1":
                W("gpsimd", svec, svec.n)
                W("gpsimd", ssca, ssca.n)
                W("gpsimd", sxs, sxs.n)
                OPQ("gpsimd", "dma_start", out=XT.rearrange("p (c t) -> p c t", c=NCH),
                    in_=Xv[:, :, t0:t0 + 256], incs=[(sxl, 16)])
                xt_ready = sxl.n
                W("vector", sxl, xt_ready)
                W("scalar", sxl, xt_ready)
                base = XT
                out_t = STY
            else:
                W("gpsimd", svec, svec.n)
                W("gpsimd", ssca, ssca.n)
                W("gpsimd", sxs, sxs.n)
                OPQ("gpsimd", "dma_start", out=XT.rearrange("p (c t) -> p c t", c=NCH),
                    in_=sav[:, :, t0:t0 + 256], incs=[(sxl, 16)])
                OPQ("gpsimd", "dma_start", out=STY.rearrange("p (c t) -> p c t", c=NCH),
                    in_=sy1v[:, :, t0:t0 + 256], incs=[(sxl, 16)])
                xt_ready = sxl.n
                W("vector", sxl, xt_ready)
                W("scalar", sxl, xt_ready)
                for c in range(NCH):
                    OPQ("vector", "tensor_add", XT[:, c * 256:(c + 1) * 256],
                        XT[:, c * 256:(c + 1) * 256], STY[:, c * 256:(c + 1) * 256])
                MK("vector", (svec, 1))
                W("scalar", svec, svec.n)
                base = XT
                out_t = STY
            src = lambda c: base[:, c * 256:(c + 1) * 256]
            ln_stats(EPS_TOP, src)
            for c in range(NCH):
                OPQ("vector", "tensor_tensor", KQ[0][:, :256], src(c), RREP,
                    OP.mult)
                OPQ("vector", "scalar_tensor_tensor",
                    AB2[:, AB2_LN + c * 256:AB2_LN + (c + 1) * 256],
                    KQ[0][:, :256], 0.0, MRREP, OP.add, OP.subtract)
            MK("vector", (svec, 1))
            ln_done = svec.n
            W("tensor", svec, ln_done)
            for j in range(2):
                W("tensor", svec, svec.n)
                W("tensor", ssca, ssca.n)
                for half, pX in ((0, pE), (1, pF)):
                    for d in range(NCH):
                        OPQ("tensor", "matmul", pX[:, :],
                            AB2[:, AB2_LN + d * 256 + j * 128:AB2_LN + d * 256 + (j + 1) * 128],
                            AB2[:, AB2_H + d * 1024 + half * 512:AB2_H + d * 1024 + (half + 1) * 512],
                            start=(d == 0), stop=(d == NCH - 1))
                MK("tensor", (stns, 1))
                W("vector", stns, stns.n)
                if stop == "XH":
                    OPQ("vector", "tensor_copy", STY[:, j * 1024:j * 1024 + 512], pE[:, :])
                    OPQ("vector", "tensor_copy", STY[:, j * 1024 + 512:j * 1024 + 1024],
                        pF[:, :])
                    MK("vector", (svec, 1))
                OPQ("vector", "tensor_reduce", COLS[:, CQ_M0:CQ_M0 + 1], pEF[:, :], AX,
                    OP.max, apply_absolute_value=True, incs=[(svec, 1)])
                W("scalar", stns, stns.n)
                W("scalar", svec, svec.n)
                OPQ("scalar", "activation", COLS[:, 1:2], COLS[:, CQ_M0:CQ_M0 + 1],
                    AF.Relu, bias=CONSTC[:, CC_NEPS5:CC_NEPS5 + 1])
                OPQ("scalar", "activation", COLS[:, 2:3], COLS[:, 1:2], AF.Ln,
                    bias=CONSTC[:, CC_EPS5:CC_EPS5 + 1])
                OPQ("scalar", "activation", COLS[:, 3:4], COLS[:, 2:3], AF.Exp,
                    scale=CONSTC[:, CC_NONE:CC_NONE + 1],
                    bias=CONSTC[:, CC_LN7:CC_LN7 + 1])
                OPQ("scalar", "activation", COLS[:, 4:5], COLS[:, 2:3], AF.Exp,
                    bias=CONSTC[:, CC_NLN7:CC_NLN7 + 1])
                OPQ("scalar", "activation", COLS[:, 5:6], COLS[:, 4:5], AF.Identity,
                    scale=CONSTC[:, CC_N8:CC_N8 + 1])
                for half, pX in ((0, pE), (1, pF)):
                    OPQ("scalar", "activation", KQ[half][:, :], pX, AF.Identity,
                        scale=COLS[:, 3:4], bias=CONSTC[:, CC_85:CC_85 + 1])
                MK("scalar", (ssca, 1))
                if stop == "QCL":
                    W("vector", ssca, ssca.n)
                    OPQ("vector", "tensor_copy", STY[:, (tt * 2 + j) * 8:(tt * 2 + j) * 8 + 8],
                        COLS[:, 0:8])
                    MK("vector", (svec, 1))
                W("vector", ssca, ssca.n)
                for half in (0, 1):
                    zin = KQ[half]
                    for cst in (8.0, 4.0, 2.0, 1.0):
                        OPQ("vector", "tensor_scalar", KQ[4 + half][:, :], zin[:, :],
                            cst, None, OP.is_ge)
                        OPQ("vector", "scalar_tensor_tensor", KQ[2 + half][:, :],
                            KQ[4 + half][:, :], -cst, zin[:, :], OP.mult, OP.add)
                        zin = KQ[2 + half]
                    OPQ("vector", "scalar_tensor_tensor", KQ[4 + half][:, :],
                        KQ[half][:, :], 1.0, KQ[2 + half][:, :], OP.mult, OP.subtract)
                MK("vector", (svec, 1))
                W("scalar", svec, svec.n)
                for half in (0, 1):
                    OPQ("scalar", "activation", SBQ[:, half * 512:(half + 1) * 512],
                        KQ[4 + half][:, :], AF.Identity,
                        scale=COLS[:, 4:5], bias=COLS[:, 5:6])
                MK("scalar", (ssca, 1))
                W("vector", ssca, ssca.n)
                if stop == "XQT":
                    OPQ("vector", "tensor_copy", STY[:, j * 1024:(j + 1) * 1024],
                        SBQ[:, 0:1024])
                    MK("vector", (svec, 1))
                W("tensor", svec, svec.n)
                W("tensor", ssca, ssca.n)
                for h in range(NCH):
                    OPQ("tensor", "transpose", pT[:, h * 128:(h + 1) * 128],
                        SBQ[:, h * 128:(h + 1) * 128], IDT[:, :])
                MK("tensor", (stns, 1))
                W("vector", stns, stns.n)
                dst = AB2[:, AB2_XQ:AB2_XQ + 2048].rearrange(
                    "p (h q) -> p h q", h=NCH)[:, :, j * 128:(j + 1) * 128]
                OPQ("vector", "tensor_copy", dst,
                    pT[:, :].rearrange("p (h q) -> p h q", h=NCH), incs=[(svec, 1)])
            xq_ready = svec.n
            if stop in ("XH", "XQT"):
                W("gpsimd", svec, svec.n)
                OPQ("gpsimd", "dma_start", out=Yv[:, :, t0:t0 + 256],
                    in_=STY.rearrange("p (c t) -> p c t", c=NCH), incs=[(sout, 16)])
                continue
            if stop == "QNT":
                for c in range(NCH):
                    OPQ("vector", "tensor_copy", STY[:, c * 256:(c + 1) * 256],
                        AB2[:, AB2_XQ + c * 256:AB2_XQ + (c + 1) * 256])
                MK("vector", (svec, 1))
                W("gpsimd", svec, svec.n)
                OPQ("gpsimd", "dma_start", out=Yv[:, :, t0:t0 + 256],
                    in_=STY.rearrange("p (c t) -> p c t", c=NCH), incs=[(sout, 16)])
                continue
            W("tensor", svec, xq_ready)
            for o in range(NCH):
                pX = pE if (o % 2 == 0) else pF
                if o >= 2:
                    W("tensor", svec, svec.n)
                for h in range(NCH):
                    OPQ("tensor", "matmul", pX[:, :256],
                        AB2[:, AB2_TERN + h * 1024 + o * 128:AB2_TERN + h * 1024 + (o + 1) * 128],
                        AB2[:, AB2_XQ + h * 256:AB2_XQ + (h + 1) * 256],
                        start=(h == 0), stop=(h == NCH - 1))
                MK("tensor", (stns, 1))
                W("vector", stns, stns.n)
                OPQ("vector", "scalar_tensor_tensor", out_t[:, o * 256:(o + 1) * 256],
                    pX[:, :256], COLS2[:, CQ_HSO + o:CQ_HSO + o + 1],
                    base[:, o * 256:(o + 1) * 256], OP.mult, OP.add, incs=[(svec, 1)])
            ytile_done = svec.n
            if kind == "F1":
                W("gpsimd", svec, ytile_done)
                OPQ("gpsimd", "dma_start", out=sy1v[:, :, t0:t0 + 256],
                    in_=STY.rearrange("p (c t) -> p c t", c=NCH), incs=[(sxs, 16)])
                srcb = lambda c: STY[:, c * 256:(c + 1) * 256]
                ln_stats(EPS_TOP, srcb)
                for c in range(NCH):
                    OPQ("vector", "tensor_tensor", KQ[0][:, :256], srcb(c),
                        RREP, OP.mult)
                    OPQ("vector", "scalar_tensor_tensor", XT[:, c * 256:(c + 1) * 256],
                        KQ[0][:, :256], 0.0, MRREP, OP.add, OP.subtract)
                MK("vector", (svec, 1))
                W("scalar", svec, svec.n)
                for c in range(NCH):
                    OPQ("scalar", "activation", XB[:, c * T + t0:c * T + t0 + 256],
                        XT[:, c * 256:(c + 1) * 256], AF.Identity)
                MK("scalar", (ssca, 1))
                W("gpsimd", svec, svec.n)
                W("gpsimd", ssca, ssca.n)
                OPQ("gpsimd", "dma_start", out=sav[:, :, t0:t0 + 256],
                    in_=XT.rearrange("p (c t) -> p c t", c=NCH), incs=[(sxs, 16)])
            else:
                W("gpsimd", svec, ytile_done)
                OPQ("gpsimd", "dma_start", out=Yv[:, :, t0:t0 + 256],
                    in_=STY.rearrange("p (c t) -> p c t", c=NCH), incs=[(sout, 16)])

    def emit_block(i, first_w2_gate=0):
        bq = lambda a: pcol(i, PC_BQ, a)
        bk = lambda a: pcol(i, PC_BK, a)
        bv = lambda a: pcol(i, PC_BV, a)
        bo = lambda c: pcol(i, PC_BO, c)
        b1 = lambda f: pcol(i, PC_B1, f)
        b2 = lambda c: pcol(i, PC_B2, c)
        g1 = lambda c: pcol(i, PC_G1, c)
        be1 = lambda c: pcol(i, PC_BE1, c)
        g2 = lambda c: pcol(i, PC_G2, c)
        be2 = lambda c: pcol(i, PC_BE2, c)

        W("tensor", ssca, ssca.n)
        W("tensor", svec, svec.n)
        entry_tns = stns.n
        wk_g = [granule(WKa[i][d * 128:(d + 1) * 128, :],
                        AB1[:, AB1_WK + d * 1024:AB1_WK + (d + 1) * 1024], entry_tns)
                for d in range(NCH)]
        wv_g = [granule(WVa[i][d * 128:(d + 1) * 128, :],
                        AB1[:, AB1_WV + d * 1024:AB1_WV + (d + 1) * 1024], entry_tns)
                for d in range(NCH)]
        wq_g = [granule(WQa[i][d * 128:(d + 1) * 128, :],
                        AB1[:, AB1_WQ + d * 1024:AB1_WQ + (d + 1) * 1024], entry_tns)
                for d in range(NCH)]
        wo_g = [granule(WOa[i][a * 128:(a + 1) * 128, :],
                        AB1[:, AB1_WO + a * 1024:AB1_WO + (a + 1) * 1024], entry_tns)
                for a in range(NCH)]

        W("tensor", scast, max(max(wk_g), max(wv_g)))
        it = 0
        kvevac = {}
        kbank = [pE, pG]
        vbank = [pF, pA]
        p1_entry = ssca.n
        for a in range(NCH):
            for t in range(NT512):
                t0 = t * 512
                p = it % 2
                if "p1" in _COARSE:
                    W("tensor", ssca, ssca.n)
                    W("tensor", svec, svec.n)
                W("tensor", ssca, kvevac.get(it - 2, p1_entry))
                for d in range(NCH):
                    OPQ("tensor", "matmul", kbank[p][:, :512],
                        AB1[:, AB1_WK + d * 1024 + a * 128:AB1_WK + d * 1024 + (a + 1) * 128],
                        XB[:, d * T + t0:d * T + t0 + 512],
                        start=(d == 0), stop=(d == NCH - 1))
                MK("tensor", (stns, 1))
                for d in range(NCH):
                    OPQ("tensor", "matmul", vbank[p][:, :512],
                        AB1[:, AB1_WV + d * 1024 + a * 128:AB1_WV + d * 1024 + (a + 1) * 128],
                        XB[:, d * T + t0:d * T + t0 + 512],
                        start=(d == 0), stop=(d == NCH - 1))
                MK("tensor", (stns, 1))
                W("scalar", svec, svec.n)
                W("scalar", stns, stns.n - 1)
                OPQ("scalar", "activation", KQ[p][:, :], kbank[p][:, :512], AF.Exp,
                    bias=bk(a))
                OPQ("scalar", "activation", KQ[2 + p][:, :], kbank[p][:, :512], AF.Relu,
                    bias=bk(a))
                W("scalar", stns, stns.n)
                OPQ("scalar", "activation", KQ[5][:, :], vbank[p][:, :512], AF.Identity,
                    bias=bv(a), incs=[(ssca, 2)])
                kvevac[it] = ssca.n
                W("vector", ssca, ssca.n)
                OPQ("vector", "scalar_tensor_tensor", KQ[4][:, :], KQ[p][:, :],
                    1.0, KQ[2 + p][:, :], OP.min, OP.add,
                    accum_out=COLS[:, CL_KS + a * 4 + t:CL_KS + a * 4 + t + 1])
                OPQ("vector", "scalar_tensor_tensor", KQ[p][:, :], KQ[4][:, :],
                    1.0, KQ[5][:, :], OP.mult, OP.mult,
                    accum_out=COLS2[:, CL_KV + a * 4 + t:CL_KV + a * 4 + t + 1],
                    incs=[(svec, 1)])
                it += 1
        W("scalar", svec, svec.n)
        W("scalar", sgps, sgps.n)
        for a in range(NCH):
            for base, src in ((8 + a, COLS[:, CL_KS + a * 4:]),
                              (a, COLS2[:, CL_KV + a * 4:])):
                OPQ("scalar", "activation", KQ[5][:, 0:1], src[:, 0:1], AF.Identity,
                    bias=src[:, 1:2])
                OPQ("scalar", "activation", KQ[5][:, 1:2], KQ[5][:, 0:1], AF.Identity,
                    bias=src[:, 2:3])
                OPQ("scalar", "activation", PK[:, base:base + 1], KQ[5][:, 1:2],
                    AF.Identity, bias=src[:, 3:4])
        MK("scalar", (ssca, 1))
        pack_done = ssca.n
        W("gpsimd", ssca, pack_done)
        OPQ("gpsimd", "dma_start", out=ccin.ap(), in_=PK[:, :], incs=[(sgps, 16)])
        W("gpsimd", sgps, sgps.n)
        OPQ("gpsimd", "collective_compute", "AllReduce", OP.add,
            replica_groups=[[0, 1, 2, 3], [4, 5, 6, 7]],
            ins=[ccin.ap()], outs=[ccout.ap()], incs=[(scc, 1)])
        W("gpsimd", scc, scc.n)
        OPQ("gpsimd", "dma_start", out=CC[:, :], in_=ccout.ap(), incs=[(sgps, 16)])
        cc_ready = sgps.n

        W("tensor", scast, max(wq_g))
        cc_waited = [False]
        it = 0
        qevac = {}
        qbank = [pE, pF]
        p2_entry = ssca.n
        for a in range(NCH):
            for t in range(NT512):
                t0 = t * 512
                p = it % 2
                pX = qbank[p]
                if "p2" in _COARSE:
                    W("tensor", ssca, ssca.n)
                    W("tensor", svec, svec.n)
                W("tensor", ssca, qevac.get(it - 2, p2_entry))
                for d in range(NCH):
                    OPQ("tensor", "matmul", pX[:, :512],
                        AB1[:, AB1_WQ + d * 1024 + a * 128:AB1_WQ + d * 1024 + (a + 1) * 128],
                        XB[:, d * T + t0:d * T + t0 + 512],
                        start=(d == 0), stop=(d == NCH - 1))
                MK("tensor", (stns, 1))
                W("scalar", svec, svec.n)
                W("scalar", stns, stns.n)
                OPQ("scalar", "activation", KQ[p][:, :], pX[:, :512], AF.Exp, bias=bq(a))
                OPQ("scalar", "activation", KQ[2 + p][:, :], pX[:, :512], AF.Relu,
                    bias=bq(a), incs=[(ssca, 1)])
                qevac[it] = ssca.n
                W("vector", ssca, ssca.n)
                OPQ("vector", "scalar_tensor_tensor", KQ[4][:, :], KQ[p][:, :],
                    1.0, KQ[2 + p][:, :], OP.min, OP.add, incs=[(svec, 1)])
                if not cc_waited[0]:
                    W("vector", sgps, cc_ready)
                    cc_waited[0] = True
                OPQ("vector", "tensor_scalar", KQ[5][:, :], KQ[4][:, :],
                    CC[:, 8 + a:9 + a], 1e-6, OP.mult, OP.add, incs=[(svec, 1)])
                W("scalar", svec, svec.n)
                OPQ("scalar", "activation", KQ[p][:, :], KQ[5][:, :], AF.Ln)
                OPQ("scalar", "activation", KQ[2 + p][:, :], KQ[p][:, :], AF.Exp,
                    scale=CONSTC[:, 3:4], incs=[(ssca, 1)])
                W("vector", ssca, ssca.n)
                OPQ("vector", "tensor_tensor", KQ[5][:, :], KQ[4][:, :],
                    KQ[2 + p][:, :], OP.mult)
                OPQ("vector", "tensor_scalar",
                    AB2[:, AB2_V + a * T + t0:AB2_V + a * T + t0 + 512],
                    KQ[5][:, :], CC[:, a:a + 1], None, OP.mult, incs=[(svec, 1)])
                it += 1
        v_ready = svec.n

        W("tensor", scast, max(wo_g))
        W("tensor", svec, v_ready)
        p3st = {"evac_prev": ssca.n}
        omm_done = {}

        def emit_omms(tt):
            t0 = tt * 256
            W("tensor", ssca, p3st["evac_prev"])
            for do in range(NCH):
                for a in range(NCH):
                    OPQ("tensor", "matmul",
                        PW[do % 4][:, (do // 4) * 256:(do // 4) * 256 + 256],
                        AB1[:, AB1_WO + a * 1024 + do * 128:AB1_WO + a * 1024 + (do + 1) * 128],
                        AB2[:, AB2_V + a * T + t0:AB2_V + a * T + t0 + 256],
                        start=(a == 0), stop=(a == NCH - 1))
            MK("tensor", (stns, 1))
            omm_done[tt] = stns.n

        for tt in range(NT256):
            t0 = tt * 256
            W("gpsimd", svec, svec.n)
            W("gpsimd", ssca, ssca.n)
            W("gpsimd", sxs, sxs.n)
            OPQ("gpsimd", "dma_start", out=XT.rearrange("p (c t) -> p c t", c=NCH),
                in_=sav[:, :, t0:t0 + 256], incs=[(sxl, 16)])
            x1_ready = sxl.n
            if tt == 0:
                emit_omms(0)
            W("scalar", stns, omm_done[tt])
            W("scalar", sxl, x1_ready)
            W("vector", sxl, x1_ready)
            for do in range(NCH):
                kk = (do % 2) * 2
                W("scalar", svec, svec.n)
                OPQ("scalar", "activation", KQ[kk][:, :256],
                    PW[do % 4][:, (do // 4) * 256:(do // 4) * 256 + 256],
                    AF.Gelu_apprx_tanh, bias=bo(do))
                OPQ("scalar", "activation", KQ[kk + 1][:, :256], KQ[kk][:, :256],
                    AF.Gelu_apprx_tanh, incs=[(ssca, 1)])
                W("vector", ssca, ssca.n)
                OPQ("vector", "tensor_add", STY[:, do * 256:(do + 1) * 256],
                    KQ[kk + 1][:, :256], XT[:, do * 256:(do + 1) * 256],
                    incs=[(svec, 1)])
            p3st["evac_prev"] = ssca.n
            ln_stats(EPS_BLK, lambda c: STY[:, c * 256:(c + 1) * 256],
                     defer=(lambda t=tt + 1: emit_omms(t)) if tt + 1 < NT256 else None)
            for c in range(NCH):
                OPQ("vector", "tensor_tensor", KQ[0][:, :256],
                    STY[:, c * 256:(c + 1) * 256], RREP, OP.mult)
                OPQ("vector", "scalar_tensor_tensor", KQ[1][:, :256], KQ[0][:, :256],
                    0.0, MRREP, OP.add, OP.subtract)
                OPQ("vector", "tensor_scalar", XT[:, c * 256:(c + 1) * 256],
                    KQ[1][:, :256], g1(c), be1(c), OP.mult, OP.add)
            MK("vector", (svec, 1))
            W("scalar", svec, svec.n)
            for c in range(NCH):
                OPQ("scalar", "activation", XB[:, c * T + t0:c * T + t0 + 256],
                    XT[:, c * 256:(c + 1) * 256], AF.Identity)
            MK("scalar", (ssca, 1))
            W("gpsimd", svec, svec.n)
            W("gpsimd", ssca, ssca.n)
            OPQ("gpsimd", "dma_start", out=sbv[:, :, t0:t0 + 256],
                in_=XT.rearrange("p (c t) -> p c t", c=NCH), incs=[(sxs, 16)])
        p3_tns = stns.n

        w1_g = {}
        for s in range(4):
            for d in range(NCH):
                w1_g[(d, s)] = granule(
                    W1a[i][d * 128:(d + 1) * 128, s * 1024:(s + 1) * 1024],
                    AB1[:, d * 4096 + s * 1024:d * 4096 + (s + 1) * 1024], p3_tns)
        w2_g = {}
        for f in range(32):
            w2_g[f] = granule(W2a[i][f * 128:(f + 1) * 128, :],
                              AB2[:, f * 1024:(f + 1) * 1024], p3_tns,
                              extra_gps_gate=(first_w2_gate if f == 0 else 0))

        p4st = {"hb_prev": ssca.n, "f1_prev": ssca.n}
        sweep_done = {}

        def emit_sweep(tt):
            t0 = tt * 256
            W("tensor", svec, svec.n)
            hbcnt = {}

            def emit_fmm(f):
                W("tensor", ssca, hbcnt[f])
                if f == 0:
                    W("tensor", ssca, p4st["f1_prev"])
                for c in range(NCH):
                    OPQ("tensor", "matmul",
                        PW[c % 4][:, (c // 4) * 256:(c // 4) * 256 + 256],
                        AB2[:, f * 1024 + c * 128:f * 1024 + (c + 1) * 128],
                        HB[:, (f % 2) * 256:(f % 2) * 256 + 256],
                        start=(f == 0 and c < 4), stop=(f == 31 and c >= 4))

            for f in range(32):
                need = max(max(w1_g[(d, f // 8)] for d in range(NCH)), w2_g[f])
                W("tensor", scast, need)
                W("tensor", ssca, ssca.n)
                ph = pG[:, 0:256] if f % 2 == 0 else pEF[:, 0:256]
                for d in range(NCH):
                    OPQ("tensor", "matmul", ph,
                        AB1[:, d * 4096 + f * 128:d * 4096 + (f + 1) * 128],
                        XB[:, d * T + t0:d * T + t0 + 256],
                        start=(d == 0), stop=(d == NCH - 1))
                MK("tensor", (stns, 1))
                W("scalar", stns, stns.n)
                hb = HB[:, (f % 2) * 256:(f % 2) * 256 + 256]
                OPQ("scalar", "activation", hb, ph, AF.Gelu_apprx_tanh,
                    bias=b1(f), incs=[(ssca, 1)])
                hbcnt[f] = ssca.n
                if f >= 1:
                    emit_fmm(f - 1)
            emit_fmm(31)
            p4st["hb_prev"] = hbcnt[31]
            MK("tensor", (stns, 1))
            sweep_done[tt] = stns.n

        for tt in range(NT256):
            t0 = tt * 256
            W("gpsimd", svec, svec.n)
            W("gpsimd", ssca, ssca.n)
            W("gpsimd", sxs, sxs.n)
            OPQ("gpsimd", "dma_start", out=XT.rearrange("p (c t) -> p c t", c=NCH),
                in_=sbv[:, :, t0:t0 + 256], incs=[(sxl, 16)])
            x2_ready = sxl.n
            W("vector", sxl, x2_ready)
            if tt == 0:
                emit_sweep(0)
            W("scalar", stns, sweep_done[tt])
            for c in range(NCH):
                kk = c % 2
                W("scalar", svec, svec.n)
                OPQ("scalar", "activation", KQ[kk][:, :256],
                    PW[c % 4][:, (c // 4) * 256:(c // 4) * 256 + 256],
                    AF.Gelu_apprx_tanh, bias=b2(c), incs=[(ssca, 1)])
                W("vector", ssca, ssca.n)
                OPQ("vector", "tensor_add", STY[:, c * 256:(c + 1) * 256],
                    KQ[kk][:, :256], XT[:, c * 256:(c + 1) * 256], incs=[(svec, 1)])
            p4st["f1_prev"] = ssca.n
            if tt + 1 < NT256:
                emit_sweep(tt + 1)
            ln_stats(EPS_BLK, lambda c: STY[:, c * 256:(c + 1) * 256])
            for c in range(NCH):
                OPQ("vector", "tensor_tensor", KQ[0][:, :256],
                    STY[:, c * 256:(c + 1) * 256], RREP, OP.mult)
                OPQ("vector", "scalar_tensor_tensor", KQ[1][:, :256], KQ[0][:, :256],
                    0.0, MRREP, OP.add, OP.subtract)
                OPQ("vector", "tensor_scalar", XT[:, c * 256:(c + 1) * 256],
                    KQ[1][:, :256], g2(c), be2(c), OP.mult, OP.add)
            MK("vector", (svec, 1))
            W("scalar", svec, svec.n)
            for c in range(NCH):
                OPQ("scalar", "activation", XB[:, c * T + t0:c * T + t0 + 256],
                    XT[:, c * 256:(c + 1) * 256], AF.Identity)
            MK("scalar", (ssca, 1))
            W("gpsimd", svec, svec.n)
            W("gpsimd", ssca, ssca.n)
            OPQ("gpsimd", "dma_start", out=sav[:, :, t0:t0 + 256],
                in_=XT.rearrange("p (c t) -> p c t", c=NCH), incs=[(sxs, 16)])

    def dump_stream(view):
        for tt in range(NT256):
            t0 = tt * 256
            W("gpsimd", svec, svec.n)
            W("gpsimd", ssca, ssca.n)
            W("gpsimd", sxs, sxs.n)
            OPQ("gpsimd", "dma_start", out=XT.rearrange("p (c t) -> p c t", c=NCH),
                in_=view[:, :, t0:t0 + 256], incs=[(sxl, 16)])
            W("gpsimd", sxl, sxl.n)
            OPQ("gpsimd", "dma_start", out=Yv[:, :, t0:t0 + 256],
                in_=XT.rearrange("p (c t) -> p c t", c=NCH), incs=[(sout, 16)])

    if stop == "S0":
        dump_stream(Xv)
        W("gpsimd", sout, sout.n)
        with es, nc.Block() as block:
            for eng_name in Prog.ENGS:
                getattr(block, eng_name)(
                    lambda e, q=P.q[eng_name], n=eng_name: _runq_outer(e, q, n))
        return nc
    if stop == "LN0":
        for tt in range(NT256):
            t0 = tt * 256
            W("gpsimd", svec, svec.n)
            W("gpsimd", ssca, ssca.n)
            W("gpsimd", sxs, sxs.n)
            OPQ("gpsimd", "dma_start", out=XT.rearrange("p (c t) -> p c t", c=NCH),
                in_=Xv[:, :, t0:t0 + 256], incs=[(sxl, 16)])
            W("vector", sxl, sxl.n)
            W("scalar", sxl, sxl.n)
            ln_stats(EPS_TOP, lambda c: XT[:, c * 256:(c + 1) * 256])
            for c in range(NCH):
                OPQ("vector", "tensor_tensor", KQ[0][:, :256],
                    XT[:, c * 256:(c + 1) * 256], RREP, OP.mult)
                OPQ("vector", "scalar_tensor_tensor", STY[:, c * 256:(c + 1) * 256],
                    KQ[0][:, :256], 0.0, MRREP, OP.add, OP.subtract)
            MK("vector", (svec, 1))
            W("gpsimd", svec, svec.n)
            OPQ("gpsimd", "dma_start", out=Yv[:, :, t0:t0 + 256],
                in_=STY.rearrange("p (c t) -> p c t", c=NCH), incs=[(sout, 16)])
        W("gpsimd", sout, sout.n)
        with es, nc.Block() as block:
            for eng_name in Prog.ENGS:
                getattr(block, eng_name)(
                    lambda e, q=P.q[eng_name], n=eng_name: _runq_outer(e, q, n))
        return nc
    emit_F("F1")
    W("gpsimd", stns, stns.n)
    OPQ("gpsimd", "dma_start", out=ternsp.ap(), in_=AB2[:, AB2_TERN:AB2_TERN + 8192],
        incs=[(ssp, 16)])
    tern_spilled = ssp.n

    n_blocks = {"F1": 0, "F1Y": 0, "QNT": 0, "XH": 0, "XQT": 0, "QCL": 0, "B0": 1, "B1": 2, "B2": 3, "B3": 4, "full": NB}[stop]
    for i in range(n_blocks):
        emit_block(i, first_w2_gate=(tern_spilled if i == 0 else 0))


    if stop == "full":
        W("sync", stns, stns.n)
        W("sync", ssp, tern_spilled)
        OPQ("sync", "dma_start", out=AB2[:, AB2_TERN:AB2_TERN + 8192], in_=ternsp.ap(),
            incs=[(sstp, 16)])
        for d in range(NCH):
            OPQ("sync", "dma_start",
                out=AB2[:, AB2_H + d * 1024:AB2_H + (d + 1) * 1024],
                in_=HAD[d * 128:(d + 1) * 128, :], incs=[(sstp, 16)])
        emit_F("F2")
    elif stop == "F1Y":
        dump_stream(sy1v)
    elif stop in ("QNT", "XH", "XQT", "QCL"):
        pass
    else:
        dump_stream(sav)

    W("gpsimd", sout, sout.n)

    with es, nc.Block() as block:
        for eng_name in Prog.ENGS:
            getattr(block, eng_name)(
                lambda e, q=P.q[eng_name], n=eng_name: _runq_outer(e, q, n))
    return nc


def _hadamard():
    h = np.array([[1.0]], dtype=np.float64)
    while h.shape[0] < D:
        h = np.block([[h, h], [h, -h]])
    return (h / np.sqrt(D)).astype(np.float32)


def _pack_params(inp):
    Pm = np.zeros((NB, 128, PARC), dtype=np.float32)
    cols = lambda v: np.asarray(v, np.float32).reshape(-1, 128).T
    for i in range(NB):
        Pm[i, :, PC_BQ:PC_BQ + 8] = cols(inp["L0_attn_bq"][i])
        Pm[i, :, PC_BK:PC_BK + 8] = cols(inp["L0_attn_bk"][i])
        Pm[i, :, PC_BV:PC_BV + 8] = cols(inp["L0_attn_bv"][i])
        Pm[i, :, PC_BO:PC_BO + 8] = cols(inp["L0_attn_bo"][i])
        Pm[i, :, PC_B1:PC_B1 + 32] = cols(inp["L0_attn_b1"][i])
        Pm[i, :, PC_B2:PC_B2 + 8] = cols(inp["L0_attn_b2"][i])
        Pm[i, :, PC_G1:PC_G1 + 8] = cols(inp["L0_attn_ln1_s"][i])
        Pm[i, :, PC_BE1:PC_BE1 + 8] = cols(inp["L0_attn_ln1_b"][i])
        Pm[i, :, PC_G2:PC_G2 + 8] = cols(inp["L0_attn_ln2_s"][i])
        Pm[i, :, PC_BE2:PC_BE2 + 8] = cols(inp["L0_attn_ln2_b"][i])
    return Pm


def kernel(_stop="full", **inputs):
    inp = {k: np.asarray(v) for k, v in inputs.items()}
    x = np.asarray(inp["L0_attn_x"], np.float32).reshape(16384, D)
    common = dict(
        WQ=np.ascontiguousarray(np.asarray(inp["L0_attn_Wq"], np.float32)),
        WK=np.ascontiguousarray(np.asarray(inp["L0_attn_Wk"], np.float32)),
        WV=np.ascontiguousarray(np.asarray(inp["L0_attn_Wv"], np.float32)),
        WO=np.ascontiguousarray(np.asarray(inp["L0_attn_Wo"], np.float32)),
        W1=np.ascontiguousarray(np.asarray(inp["L0_attn_W1"], np.float32)),
        W2=np.ascontiguousarray(np.asarray(inp["L0_attn_W2"], np.float32)),
        PAR=_pack_params(inp),
        WT=np.ascontiguousarray(np.asarray(inp["L0_ffn_weight"], np.float32)),
        HAD=np.ascontiguousarray(_hadamard().astype(ml_dtypes.bfloat16)),
        IDT=np.eye(128, dtype=ml_dtypes.bfloat16),
        ONE1=np.ones((128, 1), dtype=ml_dtypes.bfloat16),
        ONEF=np.ones((1, 128), dtype=np.float32),
        CONST=np.broadcast_to(np.array(
            [1e-5, 1e-6, -0.5, -1.0, -1e-5, np.log(7.0), -np.log(7.0), 1.0 / 1024,
             0.5, 5e-6, -8.0, 8.5, -5e-6, 0.0, 0.0, 0.0], np.float32),
            (128, 16)).copy(),
    )
    in_maps = []
    for c in range(8):
        m = dict(common)
        m["X"] = np.ascontiguousarray(x[c * T:(c + 1) * T, :].T)
        in_maps.append(m)
    nc = build(stop=_stop)
    res = run_bass_kernel_spmd(nc, in_maps, core_ids=list(range(8)))
    out = np.concatenate([np.asarray(res.results[c]["Y"], np.float32).T
                          for c in range(8)], axis=0)
    return np.ascontiguousarray(out.reshape(2, 8192, D))


# version 08
# speedup vs baseline: 1.2250x; 1.2216x over previous
import os
import numpy as np
import ml_dtypes
from contextlib import ExitStack
_SKIP_TERN = os.environ.get("K_SKIP_TERN", "0") == "1"
_COARSE = set(os.environ.get("K_COARSE", "").split(","))
_SKIP_TPOSE = os.environ.get("K_SKIP_TPOSE", "0") == "1"

import concourse.bass as bass
import concourse.mybir as mybir
from concourse.bass_utils import run_bass_kernel_spmd

F32 = mybir.dt.float32
BF16 = mybir.dt.bfloat16
AF = mybir.ActivationFunctionType
OP = mybir.AluOpType
AX = mybir.AxisListType.X

D = 1024
T = 2048
NCH = 8
NB = 4
FF = 4096
NT512 = T // 512
NT256 = T // 256
EPS_TOP = 1e-5
EPS_BLK = 1e-6

PC_BQ, PC_BK, PC_BV, PC_BO = 0, 8, 16, 24
PC_B1, PC_B2 = 32, 64
PC_G1, PC_BE1, PC_G2, PC_BE2 = 72, 80, 88, 96
PARC = 104

AB1_WK, AB1_WV, AB1_WQ, AB1_WO = 0, 8192, 16384, 24576
AB2_V = 0
AB2_TERN, AB2_H, AB2_XQ, AB2_LN = 0, 8192, 16384, 18432

CL_KS = 0
CQ_M0, CQ_M1, CQ_MM, CQ_MC, CQ_RC, CQ_R7, CQ_NS = 0, 1, 2, 3, 4, 5, 6
CQ_SO = 24
CL_KV = 0
CQ_HSO = 32

R_A, R_B, R_C, R_D = 0, 256, 512, 768

CC_EPS5, CC_EPS6, CC_NHALF, CC_NONE, CC_NEPS5, CC_LN7, CC_NLN7, CC_R1024 = range(8)
CC_HALF, CC_5E6, CC_N8, CC_85, CC_N5E6 = 8, 9, 10, 11, 12


class Sem:
    def __init__(self, h):
        self.h = h
        self.n = 0


class Prog:
    ENGS = ("sync", "vector", "scalar", "tensor", "gpsimd")

    def __init__(self):
        self.q = {e: [] for e in self.ENGS}

    def op(self, eng, method, *args, incs=(), **kw):
        for s, k in incs:
            s.n += k
        bound = [(s.h, k) for s, k in incs]

        def fn(e, method=method, args=args, kw=kw, bound=bound):
            ins = getattr(e, method)(*args, **kw)
            for h, k in bound:
                ins.then_inc(h, k)
            return ins
        self.q[eng].append(("op", fn))

    def mark(self, eng, *incs):
        for s, k in incs:
            s.n += k
        bound = [(s.h, k) for s, k in incs]
        self.q[eng].append(("mark", bound))

    def wait(self, eng, sem, val):
        if val <= 0:
            return
        self.q[eng].append(("wait", lambda e, h=sem.h, v=val: e.wait_ge(h, v)))

    def dwait(self, eng, sem, ref, key):
        def fn(e, h=sem.h, ref=ref, key=key):
            v = ref[key]
            if v > 0:
                e.wait_ge(h, v)
        self.q[eng].append(("wait", fn))


def _runq_outer(e, q, name):
    last = None
    for idx, ent in enumerate(q):
        kind, payload = ent
        try:
            if kind == "op":
                last = payload(e)
            elif kind == "wait":
                payload(e)
            else:
                assert last is not None, "mark with no preceding op"
                for h, k in payload:
                    last.then_inc(h, k)
        except Exception:
            import sys
            print(f"FAILED {name} entry #{idx} kind={kind}", file=sys.stderr)
            raise


def build(stop="full"):
    nc = bass.Bass("TRN2", target_bir_lowering=False, detect_race_conditions=False)

    X = nc.dram_tensor("X", [D, T], F32, kind="ExternalInput").ap()
    WQa = nc.dram_tensor("WQ", [NB, D, D], F32, kind="ExternalInput").ap()
    WKa = nc.dram_tensor("WK", [NB, D, D], F32, kind="ExternalInput").ap()
    WVa = nc.dram_tensor("WV", [NB, D, D], F32, kind="ExternalInput").ap()
    WOa = nc.dram_tensor("WO", [NB, D, D], F32, kind="ExternalInput").ap()
    W1a = nc.dram_tensor("W1", [NB, D, FF], F32, kind="ExternalInput").ap()
    W2a = nc.dram_tensor("W2", [NB, FF, D], F32, kind="ExternalInput").ap()
    PAR = nc.dram_tensor("PAR", [NB, 128, PARC], F32, kind="ExternalInput").ap()
    WT = nc.dram_tensor("WT", [D, D], F32, kind="ExternalInput").ap()
    HAD = nc.dram_tensor("HAD", [D, D], BF16, kind="ExternalInput").ap()
    IDTd = nc.dram_tensor("IDT", [128, 128], BF16, kind="ExternalInput").ap()
    ONE1d = nc.dram_tensor("ONE1", [128, 1], BF16, kind="ExternalInput").ap()
    ONEFd = nc.dram_tensor("ONEF", [1, 128], F32, kind="ExternalInput").ap()
    CONSTd = nc.dram_tensor("CONST", [128, 16], F32, kind="ExternalInput").ap()
    Y = nc.dram_tensor("Y", [D, T], F32, kind="ExternalOutput").ap()

    sy1 = nc.dram_tensor("sy1", [D, T], F32)
    sa = nc.dram_tensor("sa", [D, T], F32)
    sbm = nc.dram_tensor("sbm", [D, T], F32)
    ccin = nc.dram_tensor("ccin", [128, 16], F32)
    ccout = nc.dram_tensor("ccout", [128, 16], F32)
    ternsp = nc.dram_tensor("ternsp", [128, 8192], BF16)

    Xv = X.rearrange("(c p) t -> p c t", p=128)
    Yv = Y.rearrange("(c p) t -> p c t", p=128)
    sy1v = sy1.ap().rearrange("(c p) t -> p c t", p=128)
    sav = sa.ap().rearrange("(c p) t -> p c t", p=128)
    sbv = sbm.ap().rearrange("(c p) t -> p c t", p=128)

    es = ExitStack()
    sb_ = lambda n, sh, dt=F32: es.enter_context(nc.sbuf_tensor(n, sh, dt))
    ps_ = lambda n, sh, dt=F32: es.enter_context(nc.psum_tensor(n, sh, dt))

    XB = sb_("XB", [128, NCH * T], BF16)
    AB1 = sb_("AB1", [128, 32768], BF16)
    AB2 = sb_("AB2", [128, 32768], BF16)
    WS = sb_("WS", [128, 2048], F32)
    XT = sb_("XT", [128, 2048], F32)
    STY = sb_("STY", [128, 2048], F32)
    SBQ = sb_("SBQ", [128, 2048], BF16)
    KQ = [sb_(f"KQ{i}", [128, 512]) for i in range(6)]
    HB = sb_("HB", [128, 512], BF16)
    COLS = sb_("COLS", [128, 32])
    COLS2 = sb_("COLS2", [128, 40])
    PK = sb_("PK", [128, 16])
    CC = PK
    PARS = sb_("PARS", [128, NB * PARC])
    ROWS = sb_("ROWS", [1, 1024])
    IDT = sb_("IDT_s", [128, 128], BF16)
    ONE1 = sb_("ONE1_s", [128, 1], BF16)
    ONEF = sb_("ONEF_s", [1, 128], F32)
    CONSTC = sb_("CONST_s", [128, 16], F32)

    pA = ps_("pA", [128, 512])
    pB = ps_("pB", [128, 512])
    pC = ps_("pC", [128, 512])
    pD = ps_("pD", [128, 512])
    pEF = ps_("pEF", [128, 1024])
    pE = pEF[:, 0:512]
    pF = pEF[:, 512:1024]
    pG = ps_("pG", [128, 512])
    pT = ps_("pT", [128, 1024], BF16)
    PW = [pA, pB, pC, pD]

    semnames = ("sstp", "sw0", "sw1", "sxl", "sxs", "ssp", "scast", "stns",
                "svec", "ssca", "sgps", "scc", "sout")
    sems = {k: Sem(es.enter_context(nc.semaphore(k))) for k in semnames}
    (sstp, sw0, sw1, sxl, sxs, ssp, scast, stns, svec, ssca, sgps, scc, sout) = (
        sems[k] for k in semnames)
    SW = [sw0, sw1]

    P = Prog()
    OPQ = P.op
    W = P.wait
    MK = P.mark

    def pcol(blk, base, idx=0):
        c = blk * PARC + base + idx
        return PARS[:, c:c + 1]

    gcount = [0]

    def granule(src_ap, dst_ap, area_gate_tns=0, extra_gps_gate=0):
        g = gcount[0]
        gcount[0] += 1
        slot = WS[:, (g % 2) * 1024:(g % 2) * 1024 + 1024]
        W("sync", scast, g - 1)
        OPQ("sync", "dma_start", out=slot, in_=src_ap, incs=[(SW[g % 2], 16)])
        W("vector", SW[g % 2], SW[g % 2].n)
        if area_gate_tns:
            W("vector", stns, area_gate_tns)
        if extra_gps_gate:
            W("vector", ssp, extra_gps_gate)
        OPQ("vector", "tensor_copy", dst_ap, slot, incs=[(scast, 1)])
        return scast.n

    def granule3(src3, dst3, area_gate_tns=0):
        g = gcount[0]
        gcount[0] += 1
        slot = WS[:, (g % 2) * 1024:(g % 2) * 1024 + 1024]
        slot3 = slot.rearrange("p (d c) -> p d c", d=8)
        W("sync", scast, g - 1)
        OPQ("sync", "dma_start", out=slot3, in_=src3, incs=[(SW[g % 2], 16)])
        W("vector", SW[g % 2], SW[g % 2].n)
        if area_gate_tns:
            W("vector", stns, area_gate_tns)
        OPQ("vector", "tensor_copy", dst3, slot3, incs=[(scast, 1)])
        return scast.n

    for i in range(NB):
        OPQ("sync", "dma_start", out=PARS[:, i * PARC:(i + 1) * PARC], in_=PAR[i],
            incs=[(sstp, 16)])
    OPQ("sync", "dma_start", out=IDT[:, :], in_=IDTd, incs=[(sstp, 16)])
    OPQ("sync", "dma_start", out=ONE1[:, :], in_=ONE1d, incs=[(sstp, 16)])
    OPQ("sync", "dma_start", out=ONEF[:, :], in_=ONEFd, incs=[(sstp, 16)])
    OPQ("sync", "dma_start", out=CONSTC[:, :], in_=CONSTd, incs=[(sstp, 16)])
    setup_dma = sstp.n
    W("vector", sstp, setup_dma)
    W("scalar", sstp, setup_dma)

    for o in range(0 if _SKIP_TERN else NCH):
        g = gcount[0]
        gcount[0] += 1
        slot = WS[:, (g % 2) * 1024:(g % 2) * 1024 + 1024]
        W("sync", scast, g - 1)
        OPQ("sync", "dma_start", out=slot, in_=WT[o * 128:(o + 1) * 128, :],
            incs=[(SW[g % 2], 16)])
        W("vector", SW[g % 2], SW[g % 2].n)
        sraw = COLS[:, CQ_SO + o:CQ_SO + o + 1]
        tcol = COLS[:, 8:9]
        thr_p = COLS2[:, CQ_HSO + o:CQ_HSO + o + 1]
        thr_n = COLS[:, 9:10]
        OPQ("vector", "tensor_reduce", sraw, slot, AX, OP.add,
            apply_absolute_value=True, incs=[(svec, 1)])
        W("scalar", svec, svec.n)
        OPQ("scalar", "activation", tcol, sraw, AF.Relu,
            scale=CONSTC[:, CC_R1024:CC_R1024 + 1], bias=CONSTC[:, CC_NEPS5:CC_NEPS5 + 1])
        OPQ("scalar", "activation", thr_p, tcol, AF.Identity,
            scale=CONSTC[:, CC_HALF:CC_HALF + 1], bias=CONSTC[:, CC_5E6:CC_5E6 + 1])
        OPQ("scalar", "activation", thr_n, tcol, AF.Identity,
            scale=CONSTC[:, CC_NHALF:CC_NHALF + 1], bias=CONSTC[:, CC_N5E6:CC_N5E6 + 1],
            incs=[(ssca, 1)])
        W("vector", ssca, ssca.n)
        for h in range(2):
            sl = slot[:, h * 512:(h + 1) * 512]
            OPQ("vector", "tensor_scalar", KQ[0][:, :], sl, thr_p, None, OP.is_gt)
            OPQ("vector", "tensor_scalar", KQ[1][:, :], sl, thr_n, None, OP.is_lt)
            OPQ("vector", "tensor_tensor",
                AB2[:, AB2_H + o * 1024 + h * 512:AB2_H + o * 1024 + (h + 1) * 512],
                KQ[0][:, :], KQ[1][:, :], OP.subtract)
        MK("vector", (scast, 1))
    W("tensor", scast, scast.n)
    for o in range(0 if (_SKIP_TERN or _SKIP_TPOSE) else NCH):
        for i4 in range(0, NCH, 4):
            W("tensor", svec, svec.n)
            for i in range(i4, i4 + 4):
                OPQ("tensor", "transpose", pT[:, (i - i4) * 128:(i - i4 + 1) * 128],
                    AB2[:, AB2_H + o * 1024 + i * 128:AB2_H + o * 1024 + (i + 1) * 128],
                    IDT[:, :])
            MK("tensor", (stns, 1))
            W("vector", stns, stns.n)
            if os.environ.get("K_SIMPLE_COPY", "0") == "1":
                for i in range(i4, i4 + 4):
                    OPQ("vector", "tensor_copy",
                        AB2[:, AB2_TERN + i * 1024 + o * 128:AB2_TERN + i * 1024 + (o + 1) * 128],
                        pT[:, (i - i4) * 128:(i - i4 + 1) * 128])
                MK("vector", (svec, 1))
            else:
                src = pT[:, 0:512].rearrange("p (j q) -> p j q", j=4)
                dst = AB2[:, AB2_TERN:AB2_TERN + 8192].rearrange(
                    "p (i q) -> p i q", i=NCH)[:, i4:i4 + 4, o * 128:(o + 1) * 128]
                OPQ("vector", "tensor_copy", dst, src, incs=[(svec, 1)])
    tern_ready_vec = svec.n

    W("sync", svec, tern_ready_vec)
    for d in range(NCH):
        OPQ("sync", "dma_start", out=AB2[:, AB2_H + d * 1024:AB2_H + (d + 1) * 1024],
            in_=HAD[d * 128:(d + 1) * 128, :], incs=[(sstp, 16)])
    h_dma = sstp.n
    W("tensor", sstp, h_dma)

    RREP = pEF[:, 0:256]
    MRREP = pEF[:, 512:768]
    PSUM_SUM = pEF[:1, 256:512]
    PSUM_SQ = pEF[:1, 768:1024]

    def ln_stats(eps, src, defer=None):
        eps_col = 0 if eps == EPS_TOP else 1
        for c in range(NCH):
            OPQ("vector", "tensor_copy", SBQ[:, c * 256:(c + 1) * 256], src(c))
        MK("vector", (svec, 1))
        lnsv = svec.n
        if defer is not None:
            defer()
        W("tensor", svec, lnsv)
        W("tensor", ssca, ssca.n)
        for c in range(NCH):
            OPQ("tensor", "matmul", PSUM_SUM[:, :256], ONE1[:, :],
                SBQ[:, c * 256:(c + 1) * 256],
                start=(c == 0), stop=(c == NCH - 1))
        MK("tensor", (stns, 1))
        W("scalar", stns, stns.n)
        for c in range(NCH):
            OPQ("scalar", "activation", SBQ[:, c * 256:(c + 1) * 256], src(c), AF.Square)
        MK("scalar", (ssca, 1))
        W("tensor", ssca, ssca.n)
        for c in range(NCH):
            OPQ("tensor", "matmul", PSUM_SQ[:, :256], ONE1[:, :],
                SBQ[:, c * 256:(c + 1) * 256],
                start=(c == 0), stop=(c == NCH - 1))
        MK("tensor", (stns, 1))
        W("vector", stns, stns.n)
        OPQ("vector", "tensor_copy", ROWS[:, R_A:R_A + 256], PSUM_SUM[:, :256])
        OPQ("vector", "tensor_copy", ROWS[:, R_B:R_B + 256], PSUM_SQ[:, :256])
        OPQ("vector", "tensor_scalar", ROWS[:, R_C:R_C + 256], ROWS[:, R_A:R_A + 256],
            1.0 / D, None, OP.mult)
        OPQ("vector", "scalar_tensor_tensor", ROWS[:, R_A:R_A + 256],
            ROWS[:, R_C:R_C + 256], -1.0, ROWS[:, R_C:R_C + 256], OP.mult, OP.mult)
        OPQ("vector", "scalar_tensor_tensor", ROWS[:, R_D:R_D + 256],
            ROWS[:, R_B:R_B + 256], 1.0 / D, ROWS[:, R_A:R_A + 256], OP.mult, OP.add,
            incs=[(svec, 1)])
        W("scalar", svec, svec.n)
        OPQ("scalar", "activation", ROWS[:, R_B:R_B + 256], ROWS[:, R_D:R_D + 256],
            AF.Ln, bias=CONSTC[:1, eps_col:eps_col + 1])
        OPQ("scalar", "activation", ROWS[:, R_D:R_D + 256], ROWS[:, R_B:R_B + 256],
            AF.Exp, scale=CONSTC[:1, 2:3], incs=[(ssca, 1)])
        W("vector", ssca, ssca.n)
        OPQ("vector", "tensor_tensor", ROWS[:, R_A:R_A + 256], ROWS[:, R_C:R_C + 256],
            ROWS[:, R_D:R_D + 256], OP.mult, incs=[(svec, 1)])
        W("tensor", svec, svec.n)
        OPQ("tensor", "matmul", RREP, ONEF[:, :], ROWS[:1, R_D:R_D + 256],
            start=True, stop=True)
        OPQ("tensor", "matmul", MRREP, ONEF[:, :], ROWS[:1, R_A:R_A + 256],
            start=True, stop=True)
        MK("tensor", (stns, 1))
        W("vector", stns, stns.n)

    def stat_sums(st):
        W("tensor", svec, st["casts_done"])
        for c in range(NCH):
            OPQ("tensor", "matmul", PSUM_SUM[:, :256], ONE1[:, :],
                SBQ[:, c * 256:(c + 1) * 256], start=(c == 0), stop=(c == NCH - 1))
        MK("tensor", (stns, 1))
        st["sum_done"] = stns.n

    def stat_squares(st, half):
        if half == 0:
            P.dwait("scalar", stns, st, "sum_done")
        for c in range(half * 4, half * 4 + 4):
            OPQ("scalar", "activation", SBQ[:, c * 256:(c + 1) * 256], st["src"](c),
                AF.Square)
        if half == 1:
            MK("scalar", (ssca, 1))
            st["sq_done"] = ssca.n

    def stat_sumsq(st):
        P.dwait("tensor", ssca, st, "sq_done")
        for c in range(NCH):
            OPQ("tensor", "matmul", PSUM_SQ[:, :256], ONE1[:, :],
                SBQ[:, c * 256:(c + 1) * 256], start=(c == 0), stop=(c == NCH - 1))
        MK("tensor", (stns, 1))
        st["sumsq_done"] = stns.n

    def stat_lnexp(st):
        P.dwait("scalar", svec, st, "math_done")
        OPQ("scalar", "activation", ROWS[:, R_B:R_B + 256], ROWS[:, R_D:R_D + 256],
            AF.Ln, bias=CONSTC[:1, 1:2])
        OPQ("scalar", "activation", ROWS[:, R_D:R_D + 256], ROWS[:, R_B:R_B + 256],
            AF.Exp, scale=CONSTC[:1, 2:3], incs=[(ssca, 1)])
        st["lnexp_done"] = ssca.n

    def stat_rrep(st):
        P.dwait("tensor", svec, st, "mrstd_done")
        OPQ("tensor", "matmul", RREP, ONEF[:, :], ROWS[:1, R_D:R_D + 256],
            start=True, stop=True)
        OPQ("tensor", "matmul", MRREP, ONEF[:, :], ROWS[:1, R_A:R_A + 256],
            start=True, stop=True)
        MK("tensor", (stns, 1))
        st["rrep_done"] = stns.n

    def stat_rows(st):
        W("vector", stns, st["sumsq_done"])
        OPQ("vector", "tensor_copy", ROWS[:, R_A:R_A + 256], PSUM_SUM[:, :256])
        OPQ("vector", "tensor_copy", ROWS[:, R_B:R_B + 256], PSUM_SQ[:, :256])
        OPQ("vector", "tensor_scalar", ROWS[:, R_C:R_C + 256], ROWS[:, R_A:R_A + 256],
            1.0 / D, None, OP.mult)
        OPQ("vector", "scalar_tensor_tensor", ROWS[:, R_A:R_A + 256],
            ROWS[:, R_C:R_C + 256], -1.0, ROWS[:, R_C:R_C + 256], OP.mult, OP.mult)
        OPQ("vector", "scalar_tensor_tensor", ROWS[:, R_D:R_D + 256],
            ROWS[:, R_B:R_B + 256], 1.0 / D, ROWS[:, R_A:R_A + 256], OP.mult, OP.add,
            incs=[(svec, 1)])
        st["math_done"] = svec.n

    def stat_mrstd(st):
        W("vector", ssca, st["lnexp_done"])
        OPQ("vector", "tensor_tensor", ROWS[:, R_A:R_A + 256], ROWS[:, R_C:R_C + 256],
            ROWS[:, R_D:R_D + 256], OP.mult, incs=[(svec, 1)])
        st["mrstd_done"] = svec.n

    def emit_F(kind):
        W("tensor", ssca, ssca.n)
        W("tensor", svec, svec.n)
        if kind == "F2":
            W("vector", stns, stns.n)
            W("scalar", stns, stns.n)
            W("tensor", sstp, sstp.n)
        for tt in range(NT256):
            t0 = tt * 256
            if kind == "F1":
                W("gpsimd", svec, svec.n)
                W("gpsimd", ssca, ssca.n)
                W("gpsimd", sxs, sxs.n)
                OPQ("gpsimd", "dma_start", out=XT.rearrange("p (c t) -> p c t", c=NCH),
                    in_=Xv[:, :, t0:t0 + 256], incs=[(sxl, 16)])
                xt_ready = sxl.n
                W("vector", sxl, xt_ready)
                W("scalar", sxl, xt_ready)
                base = XT
                out_t = STY
            else:
                W("gpsimd", svec, svec.n)
                W("gpsimd", ssca, ssca.n)
                W("gpsimd", sxs, sxs.n)
                OPQ("gpsimd", "dma_start", out=XT.rearrange("p (c t) -> p c t", c=NCH),
                    in_=sav[:, :, t0:t0 + 256], incs=[(sxl, 16)])
                OPQ("gpsimd", "dma_start", out=STY.rearrange("p (c t) -> p c t", c=NCH),
                    in_=sy1v[:, :, t0:t0 + 256], incs=[(sxl, 16)])
                xt_ready = sxl.n
                W("vector", sxl, xt_ready)
                W("scalar", sxl, xt_ready)
                for c in range(NCH):
                    OPQ("vector", "tensor_add", XT[:, c * 256:(c + 1) * 256],
                        XT[:, c * 256:(c + 1) * 256], STY[:, c * 256:(c + 1) * 256])
                MK("vector", (svec, 1))
                W("scalar", svec, svec.n)
                base = XT
                out_t = STY
            src = lambda c: base[:, c * 256:(c + 1) * 256]
            ln_stats(EPS_TOP, src)
            for c in range(NCH):
                OPQ("vector", "tensor_tensor", KQ[0][:, :256], src(c), RREP,
                    OP.mult)
                OPQ("vector", "scalar_tensor_tensor",
                    AB2[:, AB2_LN + c * 256:AB2_LN + (c + 1) * 256],
                    KQ[0][:, :256], 0.0, MRREP, OP.add, OP.subtract)
            MK("vector", (svec, 1))
            ln_done = svec.n
            W("tensor", svec, ln_done)
            for j in range(2):
                W("tensor", svec, svec.n)
                W("tensor", ssca, ssca.n)
                for half, pX in ((0, pE), (1, pF)):
                    for d in range(NCH):
                        OPQ("tensor", "matmul", pX[:, :],
                            AB2[:, AB2_LN + d * 256 + j * 128:AB2_LN + d * 256 + (j + 1) * 128],
                            AB2[:, AB2_H + d * 1024 + half * 512:AB2_H + d * 1024 + (half + 1) * 512],
                            start=(d == 0), stop=(d == NCH - 1))
                MK("tensor", (stns, 1))
                W("vector", stns, stns.n)
                if stop == "XH":
                    OPQ("vector", "tensor_copy", STY[:, j * 1024:j * 1024 + 512], pE[:, :])
                    OPQ("vector", "tensor_copy", STY[:, j * 1024 + 512:j * 1024 + 1024],
                        pF[:, :])
                    MK("vector", (svec, 1))
                OPQ("vector", "tensor_reduce", COLS[:, CQ_M0:CQ_M0 + 1], pEF[:, :], AX,
                    OP.max, apply_absolute_value=True, incs=[(svec, 1)])
                W("scalar", stns, stns.n)
                W("scalar", svec, svec.n)
                OPQ("scalar", "activation", COLS[:, 1:2], COLS[:, CQ_M0:CQ_M0 + 1],
                    AF.Relu, bias=CONSTC[:, CC_NEPS5:CC_NEPS5 + 1])
                OPQ("scalar", "activation", COLS[:, 2:3], COLS[:, 1:2], AF.Ln,
                    bias=CONSTC[:, CC_EPS5:CC_EPS5 + 1])
                OPQ("scalar", "activation", COLS[:, 3:4], COLS[:, 2:3], AF.Exp,
                    scale=CONSTC[:, CC_NONE:CC_NONE + 1],
                    bias=CONSTC[:, CC_LN7:CC_LN7 + 1])
                OPQ("scalar", "activation", COLS[:, 4:5], COLS[:, 2:3], AF.Exp,
                    bias=CONSTC[:, CC_NLN7:CC_NLN7 + 1])
                OPQ("scalar", "activation", COLS[:, 5:6], COLS[:, 4:5], AF.Identity,
                    scale=CONSTC[:, CC_N8:CC_N8 + 1])
                for half, pX in ((0, pE), (1, pF)):
                    OPQ("scalar", "activation", KQ[half][:, :], pX, AF.Identity,
                        scale=COLS[:, 3:4], bias=CONSTC[:, CC_85:CC_85 + 1])
                MK("scalar", (ssca, 1))
                if stop == "QCL":
                    W("vector", ssca, ssca.n)
                    OPQ("vector", "tensor_copy", STY[:, (tt * 2 + j) * 8:(tt * 2 + j) * 8 + 8],
                        COLS[:, 0:8])
                    MK("vector", (svec, 1))
                W("vector", ssca, ssca.n)
                for half in (0, 1):
                    zin = KQ[half]
                    for cst in (8.0, 4.0, 2.0, 1.0):
                        OPQ("vector", "tensor_scalar", KQ[4 + half][:, :], zin[:, :],
                            cst, None, OP.is_ge)
                        OPQ("vector", "scalar_tensor_tensor", KQ[2 + half][:, :],
                            KQ[4 + half][:, :], -cst, zin[:, :], OP.mult, OP.add)
                        zin = KQ[2 + half]
                    OPQ("vector", "scalar_tensor_tensor", KQ[4 + half][:, :],
                        KQ[half][:, :], 1.0, KQ[2 + half][:, :], OP.mult, OP.subtract)
                MK("vector", (svec, 1))
                W("scalar", svec, svec.n)
                for half in (0, 1):
                    OPQ("scalar", "activation", SBQ[:, half * 512:(half + 1) * 512],
                        KQ[4 + half][:, :], AF.Identity,
                        scale=COLS[:, 4:5], bias=COLS[:, 5:6])
                MK("scalar", (ssca, 1))
                W("vector", ssca, ssca.n)
                if stop == "XQT":
                    OPQ("vector", "tensor_copy", STY[:, j * 1024:(j + 1) * 1024],
                        SBQ[:, 0:1024])
                    MK("vector", (svec, 1))
                W("tensor", svec, svec.n)
                W("tensor", ssca, ssca.n)
                for h in range(NCH):
                    OPQ("tensor", "transpose", pT[:, h * 128:(h + 1) * 128],
                        SBQ[:, h * 128:(h + 1) * 128], IDT[:, :])
                MK("tensor", (stns, 1))
                W("vector", stns, stns.n)
                dst = AB2[:, AB2_XQ:AB2_XQ + 2048].rearrange(
                    "p (h q) -> p h q", h=NCH)[:, :, j * 128:(j + 1) * 128]
                OPQ("vector", "tensor_copy", dst,
                    pT[:, :].rearrange("p (h q) -> p h q", h=NCH), incs=[(svec, 1)])
            xq_ready = svec.n
            if stop in ("XH", "XQT"):
                W("gpsimd", svec, svec.n)
                OPQ("gpsimd", "dma_start", out=Yv[:, :, t0:t0 + 256],
                    in_=STY.rearrange("p (c t) -> p c t", c=NCH), incs=[(sout, 16)])
                continue
            if stop == "QNT":
                for c in range(NCH):
                    OPQ("vector", "tensor_copy", STY[:, c * 256:(c + 1) * 256],
                        AB2[:, AB2_XQ + c * 256:AB2_XQ + (c + 1) * 256])
                MK("vector", (svec, 1))
                W("gpsimd", svec, svec.n)
                OPQ("gpsimd", "dma_start", out=Yv[:, :, t0:t0 + 256],
                    in_=STY.rearrange("p (c t) -> p c t", c=NCH), incs=[(sout, 16)])
                continue
            W("tensor", svec, xq_ready)
            for o in range(NCH):
                pX = pE if (o % 2 == 0) else pF
                if o >= 2:
                    W("tensor", svec, svec.n)
                for h in range(NCH):
                    OPQ("tensor", "matmul", pX[:, :256],
                        AB2[:, AB2_TERN + h * 1024 + o * 128:AB2_TERN + h * 1024 + (o + 1) * 128],
                        AB2[:, AB2_XQ + h * 256:AB2_XQ + (h + 1) * 256],
                        start=(h == 0), stop=(h == NCH - 1))
                MK("tensor", (stns, 1))
                W("vector", stns, stns.n)
                OPQ("vector", "scalar_tensor_tensor", out_t[:, o * 256:(o + 1) * 256],
                    pX[:, :256], COLS2[:, CQ_HSO + o:CQ_HSO + o + 1],
                    base[:, o * 256:(o + 1) * 256], OP.mult, OP.add, incs=[(svec, 1)])
            ytile_done = svec.n
            if kind == "F1":
                W("gpsimd", svec, ytile_done)
                OPQ("gpsimd", "dma_start", out=sy1v[:, :, t0:t0 + 256],
                    in_=STY.rearrange("p (c t) -> p c t", c=NCH), incs=[(sxs, 16)])
                srcb = lambda c: STY[:, c * 256:(c + 1) * 256]
                ln_stats(EPS_TOP, srcb)
                for c in range(NCH):
                    OPQ("vector", "tensor_tensor", KQ[0][:, :256], srcb(c),
                        RREP, OP.mult)
                    OPQ("vector", "scalar_tensor_tensor", XT[:, c * 256:(c + 1) * 256],
                        KQ[0][:, :256], 0.0, MRREP, OP.add, OP.subtract)
                MK("vector", (svec, 1))
                W("scalar", svec, svec.n)
                for c in range(NCH):
                    OPQ("scalar", "activation", XB[:, c * T + t0:c * T + t0 + 256],
                        XT[:, c * 256:(c + 1) * 256], AF.Identity)
                MK("scalar", (ssca, 1))
                W("gpsimd", svec, svec.n)
                W("gpsimd", ssca, ssca.n)
                OPQ("gpsimd", "dma_start", out=sav[:, :, t0:t0 + 256],
                    in_=XT.rearrange("p (c t) -> p c t", c=NCH), incs=[(sxs, 16)])
            else:
                W("gpsimd", svec, ytile_done)
                OPQ("gpsimd", "dma_start", out=Yv[:, :, t0:t0 + 256],
                    in_=STY.rearrange("p (c t) -> p c t", c=NCH), incs=[(sout, 16)])

    def emit_block(i, first_w2_gate=0):
        bq = lambda a: pcol(i, PC_BQ, a)
        bk = lambda a: pcol(i, PC_BK, a)
        bv = lambda a: pcol(i, PC_BV, a)
        bo = lambda c: pcol(i, PC_BO, c)
        b1 = lambda f: pcol(i, PC_B1, f)
        b2 = lambda c: pcol(i, PC_B2, c)
        g1 = lambda c: pcol(i, PC_G1, c)
        be1 = lambda c: pcol(i, PC_BE1, c)
        g2 = lambda c: pcol(i, PC_G2, c)
        be2 = lambda c: pcol(i, PC_BE2, c)

        W("tensor", ssca, ssca.n)
        W("tensor", svec, svec.n)
        entry_tns = stns.n
        qkv_gate = 0 if i == 0 else entry_tns
        src3 = lambda W_, j: W_[i].rearrange("(d p) c -> p d c", p=128)[:, :, j * 128:(j + 1) * 128]
        dst3 = lambda base, j: AB1[:, base:base + 8192].rearrange(
            "p (d c) -> p d c", d=NCH)[:, :, j * 128:(j + 1) * 128]
        wkv_g = []
        for a in range(NCH):
            granule3(src3(WKa, a), dst3(AB1_WK, a), qkv_gate)
            wkv_g.append(granule3(src3(WVa, a), dst3(AB1_WV, a), qkv_gate))
        wq_g = [granule3(src3(WQa, a), dst3(AB1_WQ, a), qkv_gate)
                for a in range(NCH)]
        wo_g = [granule3(src3(WOa, do), dst3(AB1_WO, do), qkv_gate)
                for do in range(NCH)]

        it = 0
        kvevac = {}
        kbank = [pE, pG]
        vbank = [pF, pA]
        p1_entry = ssca.n
        for a in range(NCH):
            W("tensor", scast, wkv_g[a])
            for t in range(NT512):
                t0 = t * 512
                p = it % 2
                if "p1" in _COARSE:
                    W("tensor", ssca, ssca.n)
                    W("tensor", svec, svec.n)
                W("tensor", ssca, kvevac.get(it - 2, p1_entry))
                for d in range(NCH):
                    OPQ("tensor", "matmul", kbank[p][:, :512],
                        AB1[:, AB1_WK + d * 1024 + a * 128:AB1_WK + d * 1024 + (a + 1) * 128],
                        XB[:, d * T + t0:d * T + t0 + 512],
                        start=(d == 0), stop=(d == NCH - 1))
                MK("tensor", (stns, 1))
                for d in range(NCH):
                    OPQ("tensor", "matmul", vbank[p][:, :512],
                        AB1[:, AB1_WV + d * 1024 + a * 128:AB1_WV + d * 1024 + (a + 1) * 128],
                        XB[:, d * T + t0:d * T + t0 + 512],
                        start=(d == 0), stop=(d == NCH - 1))
                MK("tensor", (stns, 1))
                W("scalar", svec, svec.n)
                W("scalar", stns, stns.n - 1)
                OPQ("scalar", "activation", KQ[p][:, :], kbank[p][:, :512], AF.Exp,
                    bias=bk(a))
                OPQ("scalar", "activation", KQ[2 + p][:, :], kbank[p][:, :512], AF.Relu,
                    bias=bk(a))
                W("scalar", stns, stns.n)
                OPQ("scalar", "activation", KQ[5][:, :], vbank[p][:, :512], AF.Identity,
                    bias=bv(a), incs=[(ssca, 2)])
                kvevac[it] = ssca.n
                W("vector", ssca, ssca.n)
                OPQ("vector", "scalar_tensor_tensor", KQ[4][:, :], KQ[p][:, :],
                    1.0, KQ[2 + p][:, :], OP.min, OP.add,
                    accum_out=COLS[:, CL_KS + a * 4 + t:CL_KS + a * 4 + t + 1])
                OPQ("vector", "scalar_tensor_tensor", KQ[p][:, :], KQ[4][:, :],
                    1.0, KQ[5][:, :], OP.mult, OP.mult,
                    accum_out=COLS2[:, CL_KV + a * 4 + t:CL_KV + a * 4 + t + 1],
                    incs=[(svec, 1)])
                it += 1
        W("scalar", svec, svec.n)
        W("scalar", sgps, sgps.n)
        for a in range(NCH):
            for base, src in ((8 + a, COLS[:, CL_KS + a * 4:]),
                              (a, COLS2[:, CL_KV + a * 4:])):
                OPQ("scalar", "activation", KQ[5][:, 0:1], src[:, 0:1], AF.Identity,
                    bias=src[:, 1:2])
                OPQ("scalar", "activation", KQ[5][:, 1:2], KQ[5][:, 0:1], AF.Identity,
                    bias=src[:, 2:3])
                OPQ("scalar", "activation", PK[:, base:base + 1], KQ[5][:, 1:2],
                    AF.Identity, bias=src[:, 3:4])
        MK("scalar", (ssca, 1))
        pack_done = ssca.n
        W("gpsimd", ssca, pack_done)
        OPQ("gpsimd", "dma_start", out=ccin.ap(), in_=PK[:, :], incs=[(sgps, 16)])
        W("gpsimd", sgps, sgps.n)
        OPQ("gpsimd", "collective_compute", "AllReduce", OP.add,
            replica_groups=[[0, 1, 2, 3], [4, 5, 6, 7]],
            ins=[ccin.ap()], outs=[ccout.ap()], incs=[(scc, 1)])
        W("gpsimd", scc, scc.n)
        OPQ("gpsimd", "dma_start", out=CC[:, :], in_=ccout.ap(), incs=[(sgps, 16)])
        cc_ready = sgps.n

        cc_waited = [False]
        it = 0
        qevac = {}
        qbank = [pE, pF]
        p2_entry = ssca.n
        for a in range(NCH):
            W("tensor", scast, wq_g[a])
            for t in range(NT512):
                t0 = t * 512
                p = it % 2
                pX = qbank[p]
                if "p2" in _COARSE:
                    W("tensor", ssca, ssca.n)
                    W("tensor", svec, svec.n)
                W("tensor", ssca, qevac.get(it - 2, p2_entry))
                for d in range(NCH):
                    OPQ("tensor", "matmul", pX[:, :512],
                        AB1[:, AB1_WQ + d * 1024 + a * 128:AB1_WQ + d * 1024 + (a + 1) * 128],
                        XB[:, d * T + t0:d * T + t0 + 512],
                        start=(d == 0), stop=(d == NCH - 1))
                MK("tensor", (stns, 1))
                W("scalar", svec, svec.n)
                W("scalar", stns, stns.n)
                OPQ("scalar", "activation", KQ[p][:, :], pX[:, :512], AF.Exp, bias=bq(a))
                OPQ("scalar", "activation", KQ[2 + p][:, :], pX[:, :512], AF.Relu,
                    bias=bq(a), incs=[(ssca, 1)])
                qevac[it] = ssca.n
                W("vector", ssca, ssca.n)
                OPQ("vector", "scalar_tensor_tensor", KQ[4][:, :], KQ[p][:, :],
                    1.0, KQ[2 + p][:, :], OP.min, OP.add, incs=[(svec, 1)])
                if not cc_waited[0]:
                    W("vector", sgps, cc_ready)
                    cc_waited[0] = True
                OPQ("vector", "tensor_scalar", KQ[5][:, :], KQ[4][:, :],
                    CC[:, 8 + a:9 + a], 1e-6, OP.mult, OP.add, incs=[(svec, 1)])
                W("scalar", svec, svec.n)
                OPQ("scalar", "activation", KQ[p][:, :], KQ[5][:, :], AF.Ln)
                OPQ("scalar", "activation", KQ[2 + p][:, :], KQ[p][:, :], AF.Exp,
                    scale=CONSTC[:, 3:4], incs=[(ssca, 1)])
                W("vector", ssca, ssca.n)
                OPQ("vector", "tensor_tensor", KQ[5][:, :], KQ[4][:, :],
                    KQ[2 + p][:, :], OP.mult)
                OPQ("vector", "tensor_scalar",
                    AB2[:, AB2_V + a * T + t0:AB2_V + a * T + t0 + 512],
                    KQ[5][:, :], CC[:, a:a + 1], None, OP.mult, incs=[(svec, 1)])
                it += 1
        v_ready = svec.n

        W("tensor", scast, max(wo_g))
        W("tensor", svec, v_ready)
        p3st = {"evac_prev": ssca.n}
        omm_done = {}
        p3_src = lambda c: STY[:, c * 256:(c + 1) * 256]

        def emit_omms(tt, st=None):
            t0 = tt * 256
            W("tensor", ssca, p3st["evac_prev"])
            for do in range(NCH):
                for a in range(NCH):
                    OPQ("tensor", "matmul",
                        PW[do % 4][:, (do // 4) * 256:(do // 4) * 256 + 256],
                        AB1[:, AB1_WO + a * 1024 + do * 128:AB1_WO + a * 1024 + (do + 1) * 128],
                        AB2[:, AB2_V + a * T + t0:AB2_V + a * T + t0 + 256],
                        start=(a == 0), stop=(a == NCH - 1))
                if st is not None:
                    if do == 2:
                        stat_sums(st)
                    elif do == 5:
                        stat_sumsq(st)
                    elif do == 6:
                        stat_rrep(st)
            MK("tensor", (stns, 1))
            omm_done[tt] = stns.n

        x_ready = {}
        store_done = {}
        adds_end_prev = svec.n
        for tt in range(NT256):
            t0 = tt * 256
            if tt == 0:
                W("gpsimd", svec, svec.n)
                W("gpsimd", ssca, ssca.n)
                W("gpsimd", sxs, sxs.n)
                OPQ("gpsimd", "dma_start", out=XT.rearrange("p (c t) -> p c t", c=NCH),
                    in_=sav[:, :, 0:256], incs=[(sxl, 16)])
                x_ready[0] = sxl.n
                store_done[-1] = sxs.n
                emit_omms(0)
            st = {"src": p3_src}
            W("scalar", stns, omm_done[tt])
            adds_base = svec.n
            W("vector", sxl, x_ready[tt])
            W("vector", sxs, store_done[tt - 1])
            for do in range(NCH):
                kk = (do % 2) * 2
                W("scalar", svec, (adds_base + do - 1) if do >= 2 else adds_end_prev)
                OPQ("scalar", "activation", KQ[kk][:, :256],
                    PW[do % 4][:, (do // 4) * 256:(do // 4) * 256 + 256],
                    AF.Gelu_apprx_tanh, bias=bo(do))
                OPQ("scalar", "activation", KQ[kk + 1][:, :256], KQ[kk][:, :256],
                    AF.Gelu_apprx_tanh, incs=[(ssca, 1)])
                W("vector", ssca, ssca.n)
                OPQ("vector", "tensor_add", STY[:, do * 256:(do + 1) * 256],
                    KQ[kk + 1][:, :256], XT[:, do * 256:(do + 1) * 256],
                    incs=[(svec, 1)])
                OPQ("vector", "tensor_copy", SBQ[:, do * 256:(do + 1) * 256],
                    STY[:, do * 256:(do + 1) * 256])
            MK("vector", (svec, 1))
            adds_end_prev = adds_base + NCH
            st["casts_done"] = svec.n
            p3st["evac_prev"] = ssca.n
            if tt + 1 < NT256:
                W("gpsimd", svec, adds_base + NCH)
                OPQ("gpsimd", "dma_start", out=XT.rearrange("p (c t) -> p c t", c=NCH),
                    in_=sav[:, :, t0 + 256:t0 + 512], incs=[(sxl, 16)])
                x_ready[tt + 1] = sxl.n
                emit_omms(tt + 1, st)
            else:
                stat_sums(st)
                stat_sumsq(st)
                stat_rrep(st)
            stat_squares(st, 0)
            stat_squares(st, 1)
            stat_lnexp(st)
            stat_rows(st)
            stat_mrstd(st)
            W("vector", stns, st["rrep_done"])
            for c in range(NCH):
                OPQ("vector", "tensor_tensor", KQ[4][:, :256],
                    STY[:, c * 256:(c + 1) * 256], RREP, OP.mult)
                OPQ("vector", "scalar_tensor_tensor", KQ[5][:, :256], KQ[4][:, :256],
                    0.0, MRREP, OP.add, OP.subtract)
                OPQ("vector", "tensor_scalar", STY[:, c * 256:(c + 1) * 256],
                    KQ[5][:, :256], g1(c), be1(c), OP.mult, OP.add)
            MK("vector", (svec, 1))
            apply_done = svec.n
            W("scalar", svec, apply_done)
            for c in range(NCH):
                OPQ("scalar", "activation", XB[:, c * T + t0:c * T + t0 + 256],
                    STY[:, c * 256:(c + 1) * 256], AF.Identity)
            MK("scalar", (ssca, 1))
            W("gpsimd", svec, apply_done)
            OPQ("gpsimd", "dma_start", out=sbv[:, :, t0:t0 + 256],
                in_=STY.rearrange("p (c t) -> p c t", c=NCH), incs=[(sxs, 16)])
            store_done[tt] = sxs.n
        p3_tns = stns.n

        w1_g = {}
        w2_g = {}
        for s in range(4):
            for d in range(NCH):
                w1_g[(d, s)] = granule(
                    W1a[i][d * 128:(d + 1) * 128, s * 1024:(s + 1) * 1024],
                    AB1[:, d * 4096 + s * 1024:d * 4096 + (s + 1) * 1024], p3_tns)
            for f in range(s * 8, s * 8 + 8):
                w2_g[f] = granule(W2a[i][f * 128:(f + 1) * 128, :],
                                  AB2[:, f * 1024:(f + 1) * 1024], p3_tns,
                                  extra_gps_gate=(first_w2_gate if f == 0 else 0))

        p4st = {"hb_prev": ssca.n, "f1_prev": ssca.n}
        sweep_done = {}
        p4_src = lambda c: STY[:, c * 256:(c + 1) * 256]

        def emit_sweep(tt, st=None):
            t0 = tt * 256
            hbcnt = {}

            def emit_fmm(f):
                W("tensor", ssca, hbcnt[f])
                if f == 0:
                    W("tensor", ssca, p4st["f1_prev"])
                for c in range(NCH):
                    OPQ("tensor", "matmul",
                        PW[c % 4][:, (c // 4) * 256:(c // 4) * 256 + 256],
                        AB2[:, f * 1024 + c * 128:f * 1024 + (c + 1) * 128],
                        HB[:, (f % 2) * 256:(f % 2) * 256 + 256],
                        start=(f == 0 and c < 4), stop=(f == 31 and c >= 4))

            for f in range(32):
                need = max(max(w1_g[(d, f // 8)] for d in range(NCH)), w2_g[f])
                W("tensor", scast, need)
                W("tensor", ssca, hbcnt[f - 1] if f > 0 else p4st["hb_prev"])
                for d in range(NCH):
                    OPQ("tensor", "matmul", pG[:, 0:256],
                        AB1[:, d * 4096 + f * 128:d * 4096 + (f + 1) * 128],
                        XB[:, d * T + t0:d * T + t0 + 256],
                        start=(d == 0), stop=(d == NCH - 1))
                MK("tensor", (stns, 1))
                W("scalar", stns, stns.n)
                hb = HB[:, (f % 2) * 256:(f % 2) * 256 + 256]
                OPQ("scalar", "activation", hb, pG[:, 0:256], AF.Gelu_apprx_tanh,
                    bias=b1(f), incs=[(ssca, 1)])
                hbcnt[f] = ssca.n
                if st is not None:
                    if f == 4:
                        stat_squares(st, 0)
                    elif f == 5:
                        stat_squares(st, 1)
                    elif f == 9:
                        stat_lnexp(st)
                if f >= 1:
                    emit_fmm(f - 1)
                if st is not None:
                    if f == 3:
                        stat_sums(st)
                    elif f == 7:
                        stat_sumsq(st)
                    elif f == 12:
                        stat_rrep(st)
            emit_fmm(31)
            p4st["hb_prev"] = hbcnt[31]
            MK("tensor", (stns, 1))
            sweep_done[tt] = stns.n

        x_ready = {}
        store_done = {}
        adds_end_prev = svec.n
        for tt in range(NT256):
            t0 = tt * 256
            if tt == 0:
                W("gpsimd", svec, svec.n)
                W("gpsimd", ssca, ssca.n)
                W("gpsimd", sxs, sxs.n)
                OPQ("gpsimd", "dma_start", out=XT.rearrange("p (c t) -> p c t", c=NCH),
                    in_=sbv[:, :, 0:256], incs=[(sxl, 16)])
                x_ready[0] = sxl.n
                store_done[-1] = sxs.n
                emit_sweep(0)
            st = {"src": p4_src}
            W("scalar", stns, sweep_done[tt])
            adds_base = svec.n
            W("vector", sxl, x_ready[tt])
            W("vector", sxs, store_done[tt - 1])
            for c in range(NCH):
                kk = c % 2
                W("scalar", svec, (adds_base + c - 1) if c >= 2 else adds_end_prev)
                OPQ("scalar", "activation", KQ[kk][:, :256],
                    PW[c % 4][:, (c // 4) * 256:(c // 4) * 256 + 256],
                    AF.Gelu_apprx_tanh, bias=b2(c), incs=[(ssca, 1)])
                W("vector", ssca, ssca.n)
                OPQ("vector", "tensor_add", STY[:, c * 256:(c + 1) * 256],
                    KQ[kk][:, :256], XT[:, c * 256:(c + 1) * 256], incs=[(svec, 1)])
                OPQ("vector", "tensor_copy", SBQ[:, c * 256:(c + 1) * 256],
                    STY[:, c * 256:(c + 1) * 256])
            MK("vector", (svec, 1))
            adds_end_prev = adds_base + NCH
            st["casts_done"] = svec.n
            p4st["f1_prev"] = ssca.n
            if tt + 1 < NT256:
                W("gpsimd", svec, adds_base + NCH)
                OPQ("gpsimd", "dma_start", out=XT.rearrange("p (c t) -> p c t", c=NCH),
                    in_=sbv[:, :, t0 + 256:t0 + 512], incs=[(sxl, 16)])
                x_ready[tt + 1] = sxl.n
                emit_sweep(tt + 1, st)
            else:
                stat_sums(st)
                stat_sumsq(st)
                stat_rrep(st)
                stat_squares(st, 0)
                stat_squares(st, 1)
                stat_lnexp(st)
            stat_rows(st)
            stat_mrstd(st)
            W("vector", stns, st["rrep_done"])
            for c in range(NCH):
                OPQ("vector", "tensor_tensor", KQ[2][:, :256],
                    STY[:, c * 256:(c + 1) * 256], RREP, OP.mult)
                OPQ("vector", "scalar_tensor_tensor", KQ[3][:, :256], KQ[2][:, :256],
                    0.0, MRREP, OP.add, OP.subtract)
                OPQ("vector", "tensor_scalar", STY[:, c * 256:(c + 1) * 256],
                    KQ[3][:, :256], g2(c), be2(c), OP.mult, OP.add)
            MK("vector", (svec, 1))
            apply_done = svec.n
            W("scalar", svec, apply_done)
            for c in range(NCH):
                OPQ("scalar", "activation", XB[:, c * T + t0:c * T + t0 + 256],
                    STY[:, c * 256:(c + 1) * 256], AF.Identity)
            MK("scalar", (ssca, 1))
            W("gpsimd", svec, apply_done)
            OPQ("gpsimd", "dma_start", out=sav[:, :, t0:t0 + 256],
                in_=STY.rearrange("p (c t) -> p c t", c=NCH), incs=[(sxs, 16)])
            store_done[tt] = sxs.n

    def dump_stream(view):
        for tt in range(NT256):
            t0 = tt * 256
            W("gpsimd", svec, svec.n)
            W("gpsimd", ssca, ssca.n)
            W("gpsimd", sxs, sxs.n)
            OPQ("gpsimd", "dma_start", out=XT.rearrange("p (c t) -> p c t", c=NCH),
                in_=view[:, :, t0:t0 + 256], incs=[(sxl, 16)])
            W("gpsimd", sxl, sxl.n)
            OPQ("gpsimd", "dma_start", out=Yv[:, :, t0:t0 + 256],
                in_=XT.rearrange("p (c t) -> p c t", c=NCH), incs=[(sout, 16)])

    if stop == "S0":
        dump_stream(Xv)
        W("gpsimd", sout, sout.n)
        with es, nc.Block() as block:
            for eng_name in Prog.ENGS:
                getattr(block, eng_name)(
                    lambda e, q=P.q[eng_name], n=eng_name: _runq_outer(e, q, n))
        return nc
    if stop == "LN0":
        for tt in range(NT256):
            t0 = tt * 256
            W("gpsimd", svec, svec.n)
            W("gpsimd", ssca, ssca.n)
            W("gpsimd", sxs, sxs.n)
            OPQ("gpsimd", "dma_start", out=XT.rearrange("p (c t) -> p c t", c=NCH),
                in_=Xv[:, :, t0:t0 + 256], incs=[(sxl, 16)])
            W("vector", sxl, sxl.n)
            W("scalar", sxl, sxl.n)
            ln_stats(EPS_TOP, lambda c: XT[:, c * 256:(c + 1) * 256])
            for c in range(NCH):
                OPQ("vector", "tensor_tensor", KQ[0][:, :256],
                    XT[:, c * 256:(c + 1) * 256], RREP, OP.mult)
                OPQ("vector", "scalar_tensor_tensor", STY[:, c * 256:(c + 1) * 256],
                    KQ[0][:, :256], 0.0, MRREP, OP.add, OP.subtract)
            MK("vector", (svec, 1))
            W("gpsimd", svec, svec.n)
            OPQ("gpsimd", "dma_start", out=Yv[:, :, t0:t0 + 256],
                in_=STY.rearrange("p (c t) -> p c t", c=NCH), incs=[(sout, 16)])
        W("gpsimd", sout, sout.n)
        with es, nc.Block() as block:
            for eng_name in Prog.ENGS:
                getattr(block, eng_name)(
                    lambda e, q=P.q[eng_name], n=eng_name: _runq_outer(e, q, n))
        return nc
    emit_F("F1")
    W("gpsimd", stns, stns.n)
    OPQ("gpsimd", "dma_start", out=ternsp.ap(), in_=AB2[:, AB2_TERN:AB2_TERN + 8192],
        incs=[(ssp, 16)])
    tern_spilled = ssp.n

    n_blocks = {"F1": 0, "F1Y": 0, "QNT": 0, "XH": 0, "XQT": 0, "QCL": 0, "B0": 1, "B1": 2, "B2": 3, "B3": 4, "full": NB}[stop]
    for i in range(n_blocks):
        emit_block(i, first_w2_gate=(tern_spilled if i == 0 else 0))


    if stop == "full":
        W("sync", stns, stns.n)
        W("sync", ssp, tern_spilled)
        OPQ("sync", "dma_start", out=AB2[:, AB2_TERN:AB2_TERN + 8192], in_=ternsp.ap(),
            incs=[(sstp, 16)])
        for d in range(NCH):
            OPQ("sync", "dma_start",
                out=AB2[:, AB2_H + d * 1024:AB2_H + (d + 1) * 1024],
                in_=HAD[d * 128:(d + 1) * 128, :], incs=[(sstp, 16)])
        emit_F("F2")
    elif stop == "F1Y":
        dump_stream(sy1v)
    elif stop in ("QNT", "XH", "XQT", "QCL"):
        pass
    else:
        dump_stream(sav)

    W("gpsimd", sout, sout.n)

    with es, nc.Block() as block:
        for eng_name in Prog.ENGS:
            getattr(block, eng_name)(
                lambda e, q=P.q[eng_name], n=eng_name: _runq_outer(e, q, n))
    return nc


def _hadamard():
    h = np.array([[1.0]], dtype=np.float64)
    while h.shape[0] < D:
        h = np.block([[h, h], [h, -h]])
    return (h / np.sqrt(D)).astype(np.float32)


def _pack_params(inp):
    Pm = np.zeros((NB, 128, PARC), dtype=np.float32)
    cols = lambda v: np.asarray(v, np.float32).reshape(-1, 128).T
    for i in range(NB):
        Pm[i, :, PC_BQ:PC_BQ + 8] = cols(inp["L0_attn_bq"][i])
        Pm[i, :, PC_BK:PC_BK + 8] = cols(inp["L0_attn_bk"][i])
        Pm[i, :, PC_BV:PC_BV + 8] = cols(inp["L0_attn_bv"][i])
        Pm[i, :, PC_BO:PC_BO + 8] = cols(inp["L0_attn_bo"][i])
        Pm[i, :, PC_B1:PC_B1 + 32] = cols(inp["L0_attn_b1"][i])
        Pm[i, :, PC_B2:PC_B2 + 8] = cols(inp["L0_attn_b2"][i])
        Pm[i, :, PC_G1:PC_G1 + 8] = cols(inp["L0_attn_ln1_s"][i])
        Pm[i, :, PC_BE1:PC_BE1 + 8] = cols(inp["L0_attn_ln1_b"][i])
        Pm[i, :, PC_G2:PC_G2 + 8] = cols(inp["L0_attn_ln2_s"][i])
        Pm[i, :, PC_BE2:PC_BE2 + 8] = cols(inp["L0_attn_ln2_b"][i])
    return Pm


def kernel(_stop="full", **inputs):
    inp = {k: np.asarray(v) for k, v in inputs.items()}
    x = np.asarray(inp["L0_attn_x"], np.float32).reshape(16384, D)
    common = dict(
        WQ=np.ascontiguousarray(np.asarray(inp["L0_attn_Wq"], np.float32)),
        WK=np.ascontiguousarray(np.asarray(inp["L0_attn_Wk"], np.float32)),
        WV=np.ascontiguousarray(np.asarray(inp["L0_attn_Wv"], np.float32)),
        WO=np.ascontiguousarray(np.asarray(inp["L0_attn_Wo"], np.float32)),
        W1=np.ascontiguousarray(np.asarray(inp["L0_attn_W1"], np.float32)),
        W2=np.ascontiguousarray(np.asarray(inp["L0_attn_W2"], np.float32)),
        PAR=_pack_params(inp),
        WT=np.ascontiguousarray(np.asarray(inp["L0_ffn_weight"], np.float32)),
        HAD=np.ascontiguousarray(_hadamard().astype(ml_dtypes.bfloat16)),
        IDT=np.eye(128, dtype=ml_dtypes.bfloat16),
        ONE1=np.ones((128, 1), dtype=ml_dtypes.bfloat16),
        ONEF=np.ones((1, 128), dtype=np.float32),
        CONST=np.broadcast_to(np.array(
            [1e-5, 1e-6, -0.5, -1.0, -1e-5, np.log(7.0), -np.log(7.0), 1.0 / 1024,
             0.5, 5e-6, -8.0, 8.5, -5e-6, 0.0, 0.0, 0.0], np.float32),
            (128, 16)).copy(),
    )
    in_maps = []
    for c in range(8):
        m = dict(common)
        m["X"] = np.ascontiguousarray(x[c * T:(c + 1) * T, :].T)
        in_maps.append(m)
    nc = build(stop=_stop)
    res = run_bass_kernel_spmd(nc, in_maps, core_ids=list(range(8)))
    out = np.concatenate([np.asarray(res.results[c]["Y"], np.float32).T
                          for c in range(8)], axis=0)
    return np.ascontiguousarray(out.reshape(2, 8192, D))


# version 09
# speedup vs baseline: 1.2338x; 1.0072x over previous
import os
import numpy as np
import ml_dtypes
from contextlib import ExitStack
_SKIP_TERN = os.environ.get("K_SKIP_TERN", "0") == "1"
_COARSE = set(os.environ.get("K_COARSE", "").split(","))
_SKIP_TPOSE = os.environ.get("K_SKIP_TPOSE", "0") == "1"

import concourse.bass as bass
import concourse.mybir as mybir
from concourse.bass_utils import run_bass_kernel_spmd

F32 = mybir.dt.float32
BF16 = mybir.dt.bfloat16
AF = mybir.ActivationFunctionType
OP = mybir.AluOpType
AX = mybir.AxisListType.X

D = 1024
T = 2048
NCH = 8
NB = 4
FF = 4096
NT512 = T // 512
NT256 = T // 256
EPS_TOP = 1e-5
EPS_BLK = 1e-6

PC_BQ, PC_BK, PC_BV, PC_BO = 0, 8, 16, 24
PC_B1, PC_B2 = 32, 64
PC_G1, PC_BE1, PC_G2, PC_BE2 = 72, 80, 88, 96
PARC = 104

AB1_WK, AB1_WV, AB1_WQ, AB1_WO = 0, 8192, 16384, 24576
AB2_V = 0
AB2_TERN, AB2_H, AB2_XQ, AB2_LN = 0, 8192, 16384, 18432

CL_KS = 0
CQ_M0, CQ_M1, CQ_MM, CQ_MC, CQ_RC, CQ_R7, CQ_NS = 0, 1, 2, 3, 4, 5, 6
CQ_SO = 24
CL_KV = 0
CQ_HSO = 32

R_A, R_B, R_C, R_D = 0, 256, 512, 768

CC_EPS5, CC_EPS6, CC_NHALF, CC_NONE, CC_NEPS5, CC_LN7, CC_NLN7, CC_R1024 = range(8)
CC_HALF, CC_5E6, CC_N8, CC_85, CC_N5E6 = 8, 9, 10, 11, 12


class Sem:
    def __init__(self, h):
        self.h = h
        self.n = 0


class Prog:
    ENGS = ("sync", "vector", "scalar", "tensor", "gpsimd")

    def __init__(self):
        self.q = {e: [] for e in self.ENGS}

    def op(self, eng, method, *args, incs=(), **kw):
        for s, k in incs:
            s.n += k
        bound = [(s.h, k) for s, k in incs]

        def fn(e, method=method, args=args, kw=kw, bound=bound):
            ins = getattr(e, method)(*args, **kw)
            for h, k in bound:
                ins.then_inc(h, k)
            return ins
        self.q[eng].append(("op", fn))

    def mark(self, eng, *incs):
        for s, k in incs:
            s.n += k
        bound = [(s.h, k) for s, k in incs]
        self.q[eng].append(("mark", bound))

    def wait(self, eng, sem, val):
        if val <= 0:
            return
        self.q[eng].append(("wait", lambda e, h=sem.h, v=val: e.wait_ge(h, v)))

    def dwait(self, eng, sem, ref, key):
        def fn(e, h=sem.h, ref=ref, key=key):
            v = ref[key]
            if v > 0:
                e.wait_ge(h, v)
        self.q[eng].append(("wait", fn))


def _runq_outer(e, q, name):
    last = None
    for idx, ent in enumerate(q):
        kind, payload = ent
        try:
            if kind == "op":
                last = payload(e)
            elif kind == "wait":
                payload(e)
            else:
                assert last is not None, "mark with no preceding op"
                for h, k in payload:
                    last.then_inc(h, k)
        except Exception:
            import sys
            print(f"FAILED {name} entry #{idx} kind={kind}", file=sys.stderr)
            raise


def build(stop="full"):
    nc = bass.Bass("TRN2", target_bir_lowering=False, detect_race_conditions=False)

    X = nc.dram_tensor("X", [D, T], F32, kind="ExternalInput").ap()
    WQa = nc.dram_tensor("WQ", [NB, D, D], F32, kind="ExternalInput").ap()
    WKa = nc.dram_tensor("WK", [NB, D, D], F32, kind="ExternalInput").ap()
    WVa = nc.dram_tensor("WV", [NB, D, D], F32, kind="ExternalInput").ap()
    WOa = nc.dram_tensor("WO", [NB, D, D], F32, kind="ExternalInput").ap()
    W1a = nc.dram_tensor("W1", [NB, D, FF], F32, kind="ExternalInput").ap()
    W2a = nc.dram_tensor("W2", [NB, FF, D], F32, kind="ExternalInput").ap()
    PAR = nc.dram_tensor("PAR", [NB, 128, PARC], F32, kind="ExternalInput").ap()
    WT = nc.dram_tensor("WT", [D, D], F32, kind="ExternalInput").ap()
    HAD = nc.dram_tensor("HAD", [D, D], BF16, kind="ExternalInput").ap()
    IDTd = nc.dram_tensor("IDT", [128, 128], BF16, kind="ExternalInput").ap()
    ONE1d = nc.dram_tensor("ONE1", [128, 1], BF16, kind="ExternalInput").ap()
    ONEFd = nc.dram_tensor("ONEF", [1, 128], F32, kind="ExternalInput").ap()
    CONSTd = nc.dram_tensor("CONST", [128, 16], F32, kind="ExternalInput").ap()
    Y = nc.dram_tensor("Y", [D, T], F32, kind="ExternalOutput").ap()

    sy1 = nc.dram_tensor("sy1", [D, T], F32)
    sa = nc.dram_tensor("sa", [D, T], F32)
    sbm = nc.dram_tensor("sbm", [D, T], F32)
    ccin = nc.dram_tensor("ccin", [128, 16], F32)
    ccout = nc.dram_tensor("ccout", [128, 16], F32)
    ternsp = nc.dram_tensor("ternsp", [128, 8192], BF16)

    Xv = X.rearrange("(c p) t -> p c t", p=128)
    Yv = Y.rearrange("(c p) t -> p c t", p=128)
    sy1v = sy1.ap().rearrange("(c p) t -> p c t", p=128)
    sav = sa.ap().rearrange("(c p) t -> p c t", p=128)
    sbv = sbm.ap().rearrange("(c p) t -> p c t", p=128)

    es = ExitStack()
    sb_ = lambda n, sh, dt=F32: es.enter_context(nc.sbuf_tensor(n, sh, dt))
    ps_ = lambda n, sh, dt=F32: es.enter_context(nc.psum_tensor(n, sh, dt))

    XB = sb_("XB", [128, NCH * T], BF16)
    AB1 = sb_("AB1", [128, 32768], BF16)
    AB2 = sb_("AB2", [128, 32768], BF16)
    WS = sb_("WS", [128, 2048], F32)
    XT = sb_("XT", [128, 2048], F32)
    STY = sb_("STY", [128, 2048], F32)
    SBQ = sb_("SBQ", [128, 2048], BF16)
    KQ = [sb_(f"KQ{i}", [128, 512]) for i in range(6)]
    HB = sb_("HB", [128, 512], BF16)
    COLS = sb_("COLS", [128, 32])
    COLS2 = sb_("COLS2", [128, 40])
    PK = sb_("PK", [128, 16])
    CC = PK
    PARS = sb_("PARS", [128, NB * PARC])
    ROWS = sb_("ROWS", [1, 1024])
    IDT = sb_("IDT_s", [128, 128], BF16)
    ONE1 = sb_("ONE1_s", [128, 1], BF16)
    ONEF = sb_("ONEF_s", [1, 128], F32)
    CONSTC = sb_("CONST_s", [128, 16], F32)

    pA = ps_("pA", [128, 512])
    pB = ps_("pB", [128, 512])
    pC = ps_("pC", [128, 512])
    pD = ps_("pD", [128, 512])
    pEF = ps_("pEF", [128, 1024])
    pE = pEF[:, 0:512]
    pF = pEF[:, 512:1024]
    pG = ps_("pG", [128, 512])
    pT = ps_("pT", [128, 1024], BF16)
    PW = [pA, pB, pC, pD]

    semnames = ("sstp", "sw0", "sw1", "sxl", "sxs", "ssp", "scast", "stns",
                "svec", "ssca", "sgps", "scc", "sout")
    sems = {k: Sem(es.enter_context(nc.semaphore(k))) for k in semnames}
    (sstp, sw0, sw1, sxl, sxs, ssp, scast, stns, svec, ssca, sgps, scc, sout) = (
        sems[k] for k in semnames)
    SW = [sw0, sw1]

    P = Prog()
    OPQ = P.op
    W = P.wait
    MK = P.mark

    def pcol(blk, base, idx=0):
        c = blk * PARC + base + idx
        return PARS[:, c:c + 1]

    gcount = [0]

    def granule(src_ap, dst_ap, area_gate_tns=0, extra_gps_gate=0):
        g = gcount[0]
        gcount[0] += 1
        slot = WS[:, (g % 2) * 1024:(g % 2) * 1024 + 1024]
        W("sync", scast, g - 1)
        OPQ("sync", "dma_start", out=slot, in_=src_ap, incs=[(SW[g % 2], 16)])
        W("vector", SW[g % 2], SW[g % 2].n)
        if area_gate_tns:
            W("vector", stns, area_gate_tns)
        if extra_gps_gate:
            W("vector", ssp, extra_gps_gate)
        OPQ("vector", "tensor_copy", dst_ap, slot, incs=[(scast, 1)])
        return scast.n

    def granule3(src3, dst3, area_gate_tns=0):
        g = gcount[0]
        gcount[0] += 1
        slot = WS[:, (g % 2) * 1024:(g % 2) * 1024 + 1024]
        slot3 = slot.rearrange("p (d c) -> p d c", d=8)
        W("sync", scast, g - 1)
        OPQ("sync", "dma_start", out=slot3, in_=src3, incs=[(SW[g % 2], 16)])
        W("vector", SW[g % 2], SW[g % 2].n)
        if area_gate_tns:
            W("vector", stns, area_gate_tns)
        OPQ("vector", "tensor_copy", dst3, slot3, incs=[(scast, 1)])
        return scast.n

    for i in range(NB):
        OPQ("sync", "dma_start", out=PARS[:, i * PARC:(i + 1) * PARC], in_=PAR[i],
            incs=[(sstp, 16)])
    OPQ("sync", "dma_start", out=IDT[:, :], in_=IDTd, incs=[(sstp, 16)])
    OPQ("sync", "dma_start", out=ONE1[:, :], in_=ONE1d, incs=[(sstp, 16)])
    OPQ("sync", "dma_start", out=ONEF[:, :], in_=ONEFd, incs=[(sstp, 16)])
    OPQ("sync", "dma_start", out=CONSTC[:, :], in_=CONSTd, incs=[(sstp, 16)])
    setup_dma = sstp.n
    W("vector", sstp, setup_dma)
    W("scalar", sstp, setup_dma)

    for o in range(0 if _SKIP_TERN else NCH):
        g = gcount[0]
        gcount[0] += 1
        slot = WS[:, (g % 2) * 1024:(g % 2) * 1024 + 1024]
        W("sync", scast, g - 1)
        OPQ("sync", "dma_start", out=slot, in_=WT[o * 128:(o + 1) * 128, :],
            incs=[(SW[g % 2], 16)])
        W("vector", SW[g % 2], SW[g % 2].n)
        sraw = COLS[:, CQ_SO + o:CQ_SO + o + 1]
        tcol = COLS[:, 8:9]
        thr_p = COLS2[:, CQ_HSO + o:CQ_HSO + o + 1]
        thr_n = COLS[:, 9:10]
        OPQ("vector", "tensor_reduce", sraw, slot, AX, OP.add,
            apply_absolute_value=True, incs=[(svec, 1)])
        W("scalar", svec, svec.n)
        OPQ("scalar", "activation", tcol, sraw, AF.Relu,
            scale=CONSTC[:, CC_R1024:CC_R1024 + 1], bias=CONSTC[:, CC_NEPS5:CC_NEPS5 + 1])
        OPQ("scalar", "activation", thr_p, tcol, AF.Identity,
            scale=CONSTC[:, CC_HALF:CC_HALF + 1], bias=CONSTC[:, CC_5E6:CC_5E6 + 1])
        OPQ("scalar", "activation", thr_n, tcol, AF.Identity,
            scale=CONSTC[:, CC_NHALF:CC_NHALF + 1], bias=CONSTC[:, CC_N5E6:CC_N5E6 + 1],
            incs=[(ssca, 1)])
        W("vector", ssca, ssca.n)
        for h in range(2):
            sl = slot[:, h * 512:(h + 1) * 512]
            OPQ("vector", "tensor_scalar", KQ[0][:, :], sl, thr_p, None, OP.is_gt)
            OPQ("vector", "tensor_scalar", KQ[1][:, :], sl, thr_n, None, OP.is_lt)
            OPQ("vector", "tensor_tensor",
                AB2[:, AB2_H + o * 1024 + h * 512:AB2_H + o * 1024 + (h + 1) * 512],
                KQ[0][:, :], KQ[1][:, :], OP.subtract)
        MK("vector", (scast, 1))
    W("tensor", scast, scast.n)
    for o in range(0 if (_SKIP_TERN or _SKIP_TPOSE) else NCH):
        for i4 in range(0, NCH, 4):
            W("tensor", svec, svec.n)
            for i in range(i4, i4 + 4):
                OPQ("tensor", "transpose", pT[:, (i - i4) * 128:(i - i4 + 1) * 128],
                    AB2[:, AB2_H + o * 1024 + i * 128:AB2_H + o * 1024 + (i + 1) * 128],
                    IDT[:, :])
            MK("tensor", (stns, 1))
            W("vector", stns, stns.n)
            if os.environ.get("K_SIMPLE_COPY", "0") == "1":
                for i in range(i4, i4 + 4):
                    OPQ("vector", "tensor_copy",
                        AB2[:, AB2_TERN + i * 1024 + o * 128:AB2_TERN + i * 1024 + (o + 1) * 128],
                        pT[:, (i - i4) * 128:(i - i4 + 1) * 128])
                MK("vector", (svec, 1))
            else:
                src = pT[:, 0:512].rearrange("p (j q) -> p j q", j=4)
                dst = AB2[:, AB2_TERN:AB2_TERN + 8192].rearrange(
                    "p (i q) -> p i q", i=NCH)[:, i4:i4 + 4, o * 128:(o + 1) * 128]
                OPQ("vector", "tensor_copy", dst, src, incs=[(svec, 1)])
    tern_ready_vec = svec.n

    W("sync", svec, tern_ready_vec)
    for d in range(NCH):
        OPQ("sync", "dma_start", out=AB2[:, AB2_H + d * 1024:AB2_H + (d + 1) * 1024],
            in_=HAD[d * 128:(d + 1) * 128, :], incs=[(sstp, 16)])
    h_dma = sstp.n
    W("tensor", sstp, h_dma)

    RREP = pEF[:, 0:256]
    MRREP = pEF[:, 512:768]
    PSUM_SUM = pEF[:1, 256:512]
    PSUM_SQ = pEF[:1, 768:1024]

    def ln_stats(eps, src, defer=None):
        eps_col = 0 if eps == EPS_TOP else 1
        for c in range(NCH):
            OPQ("vector", "tensor_copy", SBQ[:, c * 256:(c + 1) * 256], src(c))
        MK("vector", (svec, 1))
        lnsv = svec.n
        if defer is not None:
            defer()
        W("tensor", svec, lnsv)
        W("tensor", ssca, ssca.n)
        for c in range(NCH):
            OPQ("tensor", "matmul", PSUM_SUM[:, :256], ONE1[:, :],
                SBQ[:, c * 256:(c + 1) * 256],
                start=(c == 0), stop=(c == NCH - 1))
        MK("tensor", (stns, 1))
        W("scalar", stns, stns.n)
        for c in range(NCH):
            OPQ("scalar", "activation", SBQ[:, c * 256:(c + 1) * 256], src(c), AF.Square)
        MK("scalar", (ssca, 1))
        W("tensor", ssca, ssca.n)
        for c in range(NCH):
            OPQ("tensor", "matmul", PSUM_SQ[:, :256], ONE1[:, :],
                SBQ[:, c * 256:(c + 1) * 256],
                start=(c == 0), stop=(c == NCH - 1))
        MK("tensor", (stns, 1))
        W("vector", stns, stns.n)
        OPQ("vector", "tensor_copy", ROWS[:, R_A:R_A + 256], PSUM_SUM[:, :256])
        OPQ("vector", "tensor_copy", ROWS[:, R_B:R_B + 256], PSUM_SQ[:, :256])
        OPQ("vector", "tensor_scalar", ROWS[:, R_C:R_C + 256], ROWS[:, R_A:R_A + 256],
            1.0 / D, None, OP.mult)
        OPQ("vector", "scalar_tensor_tensor", ROWS[:, R_A:R_A + 256],
            ROWS[:, R_C:R_C + 256], -1.0, ROWS[:, R_C:R_C + 256], OP.mult, OP.mult)
        OPQ("vector", "scalar_tensor_tensor", ROWS[:, R_D:R_D + 256],
            ROWS[:, R_B:R_B + 256], 1.0 / D, ROWS[:, R_A:R_A + 256], OP.mult, OP.add,
            incs=[(svec, 1)])
        W("scalar", svec, svec.n)
        OPQ("scalar", "activation", ROWS[:, R_B:R_B + 256], ROWS[:, R_D:R_D + 256],
            AF.Ln, bias=CONSTC[:1, eps_col:eps_col + 1])
        OPQ("scalar", "activation", ROWS[:, R_D:R_D + 256], ROWS[:, R_B:R_B + 256],
            AF.Exp, scale=CONSTC[:1, 2:3], incs=[(ssca, 1)])
        W("vector", ssca, ssca.n)
        OPQ("vector", "tensor_tensor", ROWS[:, R_A:R_A + 256], ROWS[:, R_C:R_C + 256],
            ROWS[:, R_D:R_D + 256], OP.mult, incs=[(svec, 1)])
        W("tensor", svec, svec.n)
        OPQ("tensor", "matmul", RREP, ONEF[:, :], ROWS[:1, R_D:R_D + 256],
            start=True, stop=True)
        OPQ("tensor", "matmul", MRREP, ONEF[:, :], ROWS[:1, R_A:R_A + 256],
            start=True, stop=True)
        MK("tensor", (stns, 1))
        W("vector", stns, stns.n)

    def stat_sums(st):
        W("tensor", svec, st["casts_done"])
        for c in range(NCH):
            OPQ("tensor", "matmul", PSUM_SUM[:, :256], ONE1[:, :],
                SBQ[:, c * 256:(c + 1) * 256], start=(c == 0), stop=(c == NCH - 1))
        MK("tensor", (stns, 1))
        st["sum_done"] = stns.n

    def stat_squares(st, half):
        if half == 0:
            P.dwait("scalar", stns, st, "sum_done")
        for c in range(half * 4, half * 4 + 4):
            OPQ("scalar", "activation", SBQ[:, c * 256:(c + 1) * 256], st["src"](c),
                AF.Square)
        if half == 1:
            MK("scalar", (ssca, 1))
            st["sq_done"] = ssca.n

    def stat_sumsq(st):
        P.dwait("tensor", ssca, st, "sq_done")
        for c in range(NCH):
            OPQ("tensor", "matmul", PSUM_SQ[:, :256], ONE1[:, :],
                SBQ[:, c * 256:(c + 1) * 256], start=(c == 0), stop=(c == NCH - 1))
        MK("tensor", (stns, 1))
        st["sumsq_done"] = stns.n

    def stat_lnexp(st):
        P.dwait("scalar", svec, st, "math_done")
        OPQ("scalar", "activation", ROWS[:, R_B:R_B + 256], ROWS[:, R_D:R_D + 256],
            AF.Ln, bias=CONSTC[:1, 1:2])
        OPQ("scalar", "activation", ROWS[:, R_D:R_D + 256], ROWS[:, R_B:R_B + 256],
            AF.Exp, scale=CONSTC[:1, 2:3], incs=[(ssca, 1)])
        st["lnexp_done"] = ssca.n

    def stat_rrep(st):
        P.dwait("tensor", svec, st, "mrstd_done")
        OPQ("tensor", "matmul", RREP, ONEF[:, :], ROWS[:1, R_D:R_D + 256],
            start=True, stop=True)
        OPQ("tensor", "matmul", MRREP, ONEF[:, :], ROWS[:1, R_A:R_A + 256],
            start=True, stop=True)
        MK("tensor", (stns, 1))
        st["rrep_done"] = stns.n

    def stat_rows(st):
        W("vector", stns, st["sumsq_done"])
        OPQ("vector", "tensor_copy", ROWS[:, R_A:R_A + 256], PSUM_SUM[:, :256])
        OPQ("vector", "tensor_copy", ROWS[:, R_B:R_B + 256], PSUM_SQ[:, :256])
        OPQ("vector", "tensor_scalar", ROWS[:, R_C:R_C + 256], ROWS[:, R_A:R_A + 256],
            1.0 / D, None, OP.mult)
        OPQ("vector", "scalar_tensor_tensor", ROWS[:, R_A:R_A + 256],
            ROWS[:, R_C:R_C + 256], -1.0, ROWS[:, R_C:R_C + 256], OP.mult, OP.mult)
        OPQ("vector", "scalar_tensor_tensor", ROWS[:, R_D:R_D + 256],
            ROWS[:, R_B:R_B + 256], 1.0 / D, ROWS[:, R_A:R_A + 256], OP.mult, OP.add,
            incs=[(svec, 1)])
        st["math_done"] = svec.n

    def stat_mrstd(st):
        W("vector", ssca, st["lnexp_done"])
        OPQ("vector", "tensor_tensor", ROWS[:, R_A:R_A + 256], ROWS[:, R_C:R_C + 256],
            ROWS[:, R_D:R_D + 256], OP.mult, incs=[(svec, 1)])
        st["mrstd_done"] = svec.n

    def emit_F(kind):
        W("tensor", ssca, ssca.n)
        W("tensor", svec, svec.n)
        if kind == "F2":
            W("vector", stns, stns.n)
            W("scalar", stns, stns.n)
            W("tensor", sstp, sstp.n)
        for tt in range(NT256):
            t0 = tt * 256
            if kind == "F1":
                W("gpsimd", svec, svec.n)
                W("gpsimd", ssca, ssca.n)
                W("gpsimd", sxs, sxs.n)
                OPQ("gpsimd", "dma_start", out=XT.rearrange("p (c t) -> p c t", c=NCH),
                    in_=Xv[:, :, t0:t0 + 256], incs=[(sxl, 16)])
                xt_ready = sxl.n
                W("vector", sxl, xt_ready)
                W("scalar", sxl, xt_ready)
                base = XT
                out_t = STY
            else:
                W("gpsimd", svec, svec.n)
                W("gpsimd", ssca, ssca.n)
                W("gpsimd", sxs, sxs.n)
                OPQ("gpsimd", "dma_start", out=XT.rearrange("p (c t) -> p c t", c=NCH),
                    in_=sav[:, :, t0:t0 + 256], incs=[(sxl, 16)])
                OPQ("gpsimd", "dma_start", out=STY.rearrange("p (c t) -> p c t", c=NCH),
                    in_=sy1v[:, :, t0:t0 + 256], incs=[(sxl, 16)])
                xt_ready = sxl.n
                W("vector", sxl, xt_ready)
                W("scalar", sxl, xt_ready)
                for c in range(NCH):
                    OPQ("vector", "tensor_add", XT[:, c * 256:(c + 1) * 256],
                        XT[:, c * 256:(c + 1) * 256], STY[:, c * 256:(c + 1) * 256])
                MK("vector", (svec, 1))
                W("scalar", svec, svec.n)
                base = XT
                out_t = STY
            src = lambda c: base[:, c * 256:(c + 1) * 256]
            ln_stats(EPS_TOP, src)
            for c in range(NCH):
                OPQ("vector", "tensor_tensor", KQ[0][:, :256], src(c), RREP,
                    OP.mult)
                OPQ("vector", "scalar_tensor_tensor",
                    AB2[:, AB2_LN + c * 256:AB2_LN + (c + 1) * 256],
                    KQ[0][:, :256], 0.0, MRREP, OP.add, OP.subtract)
            MK("vector", (svec, 1))
            ln_done = svec.n
            W("tensor", svec, ln_done)
            for j in range(2):
                W("tensor", svec, svec.n)
                W("tensor", ssca, ssca.n)
                for half, pX in ((0, pE), (1, pF)):
                    for d in range(NCH):
                        OPQ("tensor", "matmul", pX[:, :],
                            AB2[:, AB2_LN + d * 256 + j * 128:AB2_LN + d * 256 + (j + 1) * 128],
                            AB2[:, AB2_H + d * 1024 + half * 512:AB2_H + d * 1024 + (half + 1) * 512],
                            start=(d == 0), stop=(d == NCH - 1))
                MK("tensor", (stns, 1))
                W("vector", stns, stns.n)
                if stop == "XH":
                    OPQ("vector", "tensor_copy", STY[:, j * 1024:j * 1024 + 512], pE[:, :])
                    OPQ("vector", "tensor_copy", STY[:, j * 1024 + 512:j * 1024 + 1024],
                        pF[:, :])
                    MK("vector", (svec, 1))
                OPQ("vector", "tensor_reduce", COLS[:, CQ_M0:CQ_M0 + 1], pEF[:, :], AX,
                    OP.max, apply_absolute_value=True, incs=[(svec, 1)])
                W("scalar", stns, stns.n)
                W("scalar", svec, svec.n)
                OPQ("scalar", "activation", COLS[:, 1:2], COLS[:, CQ_M0:CQ_M0 + 1],
                    AF.Relu, bias=CONSTC[:, CC_NEPS5:CC_NEPS5 + 1])
                OPQ("scalar", "activation", COLS[:, 2:3], COLS[:, 1:2], AF.Ln,
                    bias=CONSTC[:, CC_EPS5:CC_EPS5 + 1])
                OPQ("scalar", "activation", COLS[:, 3:4], COLS[:, 2:3], AF.Exp,
                    scale=CONSTC[:, CC_NONE:CC_NONE + 1],
                    bias=CONSTC[:, CC_LN7:CC_LN7 + 1])
                OPQ("scalar", "activation", COLS[:, 4:5], COLS[:, 2:3], AF.Exp,
                    bias=CONSTC[:, CC_NLN7:CC_NLN7 + 1])
                OPQ("scalar", "activation", COLS[:, 5:6], COLS[:, 4:5], AF.Identity,
                    scale=CONSTC[:, CC_N8:CC_N8 + 1])
                for half, pX in ((0, pE), (1, pF)):
                    OPQ("scalar", "activation", KQ[half][:, :], pX, AF.Identity,
                        scale=COLS[:, 3:4], bias=CONSTC[:, CC_85:CC_85 + 1])
                MK("scalar", (ssca, 1))
                if stop == "QCL":
                    W("vector", ssca, ssca.n)
                    OPQ("vector", "tensor_copy", STY[:, (tt * 2 + j) * 8:(tt * 2 + j) * 8 + 8],
                        COLS[:, 0:8])
                    MK("vector", (svec, 1))
                W("vector", ssca, ssca.n)
                for half in (0, 1):
                    zin = KQ[half]
                    for cst in (8.0, 4.0, 2.0, 1.0):
                        OPQ("vector", "tensor_scalar", KQ[4 + half][:, :], zin[:, :],
                            cst, None, OP.is_ge)
                        OPQ("vector", "scalar_tensor_tensor", KQ[2 + half][:, :],
                            KQ[4 + half][:, :], -cst, zin[:, :], OP.mult, OP.add)
                        zin = KQ[2 + half]
                    OPQ("vector", "scalar_tensor_tensor", KQ[4 + half][:, :],
                        KQ[half][:, :], 1.0, KQ[2 + half][:, :], OP.mult, OP.subtract)
                MK("vector", (svec, 1))
                W("scalar", svec, svec.n)
                for half in (0, 1):
                    OPQ("scalar", "activation", SBQ[:, half * 512:(half + 1) * 512],
                        KQ[4 + half][:, :], AF.Identity,
                        scale=COLS[:, 4:5], bias=COLS[:, 5:6])
                MK("scalar", (ssca, 1))
                W("vector", ssca, ssca.n)
                if stop == "XQT":
                    OPQ("vector", "tensor_copy", STY[:, j * 1024:(j + 1) * 1024],
                        SBQ[:, 0:1024])
                    MK("vector", (svec, 1))
                W("tensor", svec, svec.n)
                W("tensor", ssca, ssca.n)
                for h in range(NCH):
                    OPQ("tensor", "transpose", pT[:, h * 128:(h + 1) * 128],
                        SBQ[:, h * 128:(h + 1) * 128], IDT[:, :])
                MK("tensor", (stns, 1))
                W("vector", stns, stns.n)
                dst = AB2[:, AB2_XQ:AB2_XQ + 2048].rearrange(
                    "p (h q) -> p h q", h=NCH)[:, :, j * 128:(j + 1) * 128]
                OPQ("vector", "tensor_copy", dst,
                    pT[:, :].rearrange("p (h q) -> p h q", h=NCH), incs=[(svec, 1)])
            xq_ready = svec.n
            if stop in ("XH", "XQT"):
                W("gpsimd", svec, svec.n)
                OPQ("gpsimd", "dma_start", out=Yv[:, :, t0:t0 + 256],
                    in_=STY.rearrange("p (c t) -> p c t", c=NCH), incs=[(sout, 16)])
                continue
            if stop == "QNT":
                for c in range(NCH):
                    OPQ("vector", "tensor_copy", STY[:, c * 256:(c + 1) * 256],
                        AB2[:, AB2_XQ + c * 256:AB2_XQ + (c + 1) * 256])
                MK("vector", (svec, 1))
                W("gpsimd", svec, svec.n)
                OPQ("gpsimd", "dma_start", out=Yv[:, :, t0:t0 + 256],
                    in_=STY.rearrange("p (c t) -> p c t", c=NCH), incs=[(sout, 16)])
                continue
            W("tensor", svec, xq_ready)
            for o in range(NCH):
                pX = pE if (o % 2 == 0) else pF
                if o >= 2:
                    W("tensor", svec, svec.n)
                for h in range(NCH):
                    OPQ("tensor", "matmul", pX[:, :256],
                        AB2[:, AB2_TERN + h * 1024 + o * 128:AB2_TERN + h * 1024 + (o + 1) * 128],
                        AB2[:, AB2_XQ + h * 256:AB2_XQ + (h + 1) * 256],
                        start=(h == 0), stop=(h == NCH - 1))
                MK("tensor", (stns, 1))
                W("vector", stns, stns.n)
                OPQ("vector", "scalar_tensor_tensor", out_t[:, o * 256:(o + 1) * 256],
                    pX[:, :256], COLS2[:, CQ_HSO + o:CQ_HSO + o + 1],
                    base[:, o * 256:(o + 1) * 256], OP.mult, OP.add, incs=[(svec, 1)])
            ytile_done = svec.n
            if kind == "F1":
                W("gpsimd", svec, ytile_done)
                OPQ("gpsimd", "dma_start", out=sy1v[:, :, t0:t0 + 256],
                    in_=STY.rearrange("p (c t) -> p c t", c=NCH), incs=[(sxs, 16)])
                srcb = lambda c: STY[:, c * 256:(c + 1) * 256]
                ln_stats(EPS_TOP, srcb)
                for c in range(NCH):
                    OPQ("vector", "tensor_tensor", KQ[0][:, :256], srcb(c),
                        RREP, OP.mult)
                    OPQ("vector", "scalar_tensor_tensor", XT[:, c * 256:(c + 1) * 256],
                        KQ[0][:, :256], 0.0, MRREP, OP.add, OP.subtract)
                MK("vector", (svec, 1))
                W("scalar", svec, svec.n)
                for c in range(NCH):
                    OPQ("scalar", "activation", XB[:, c * T + t0:c * T + t0 + 256],
                        XT[:, c * 256:(c + 1) * 256], AF.Identity)
                MK("scalar", (ssca, 1))
                W("gpsimd", svec, svec.n)
                W("gpsimd", ssca, ssca.n)
                OPQ("gpsimd", "dma_start", out=sav[:, :, t0:t0 + 256],
                    in_=XT.rearrange("p (c t) -> p c t", c=NCH), incs=[(sxs, 16)])
            else:
                W("gpsimd", svec, ytile_done)
                OPQ("gpsimd", "dma_start", out=Yv[:, :, t0:t0 + 256],
                    in_=STY.rearrange("p (c t) -> p c t", c=NCH), incs=[(sout, 16)])

    def emit_block(i, first_w2_gate=0):
        bq = lambda a: pcol(i, PC_BQ, a)
        bk = lambda a: pcol(i, PC_BK, a)
        bv = lambda a: pcol(i, PC_BV, a)
        bo = lambda c: pcol(i, PC_BO, c)
        b1 = lambda f: pcol(i, PC_B1, f)
        b2 = lambda c: pcol(i, PC_B2, c)
        g1 = lambda c: pcol(i, PC_G1, c)
        be1 = lambda c: pcol(i, PC_BE1, c)
        g2 = lambda c: pcol(i, PC_G2, c)
        be2 = lambda c: pcol(i, PC_BE2, c)

        W("tensor", ssca, ssca.n)
        W("tensor", svec, svec.n)
        entry_tns = stns.n
        qkv_gate = 0 if i == 0 else entry_tns
        src3 = lambda W_, j: W_[i].rearrange("(d p) c -> p d c", p=128)[:, :, j * 128:(j + 1) * 128]
        dst3 = lambda base, j: AB1[:, base:base + 8192].rearrange(
            "p (d c) -> p d c", d=NCH)[:, :, j * 128:(j + 1) * 128]
        wkv_g = []
        for a in range(NCH):
            granule3(src3(WKa, a), dst3(AB1_WK, a), qkv_gate)
            wkv_g.append(granule3(src3(WVa, a), dst3(AB1_WV, a), qkv_gate))
        wq_g = [granule3(src3(WQa, a), dst3(AB1_WQ, a), qkv_gate)
                for a in range(NCH)]
        wo_g = [granule3(src3(WOa, do), dst3(AB1_WO, do), qkv_gate)
                for do in range(NCH)]

        it = 0
        kvevac = {}
        kbank = [pE, pG]
        vbank = [pF, pA]
        p1_entry = ssca.n
        for a in range(NCH):
            W("tensor", scast, wkv_g[a])
            for t in range(NT512):
                t0 = t * 512
                p = it % 2
                if "p1" in _COARSE:
                    W("tensor", ssca, ssca.n)
                    W("tensor", svec, svec.n)
                W("tensor", ssca, kvevac.get(it - 2, p1_entry))
                for d in range(NCH):
                    OPQ("tensor", "matmul", kbank[p][:, :512],
                        AB1[:, AB1_WK + d * 1024 + a * 128:AB1_WK + d * 1024 + (a + 1) * 128],
                        XB[:, d * T + t0:d * T + t0 + 512],
                        start=(d == 0), stop=(d == NCH - 1))
                MK("tensor", (stns, 1))
                for d in range(NCH):
                    OPQ("tensor", "matmul", vbank[p][:, :512],
                        AB1[:, AB1_WV + d * 1024 + a * 128:AB1_WV + d * 1024 + (a + 1) * 128],
                        XB[:, d * T + t0:d * T + t0 + 512],
                        start=(d == 0), stop=(d == NCH - 1))
                MK("tensor", (stns, 1))
                W("scalar", svec, svec.n)
                W("scalar", stns, stns.n - 1)
                OPQ("scalar", "activation", KQ[p][:, :], kbank[p][:, :512], AF.Exp,
                    bias=bk(a))
                OPQ("scalar", "activation", KQ[2 + p][:, :], kbank[p][:, :512], AF.Relu,
                    bias=bk(a))
                W("scalar", stns, stns.n)
                OPQ("scalar", "activation", KQ[5][:, :], vbank[p][:, :512], AF.Identity,
                    bias=bv(a), incs=[(ssca, 2)])
                kvevac[it] = ssca.n
                W("vector", ssca, ssca.n)
                OPQ("vector", "scalar_tensor_tensor", KQ[4][:, :], KQ[p][:, :],
                    1.0, KQ[2 + p][:, :], OP.min, OP.add,
                    accum_out=COLS[:, CL_KS + a * 4 + t:CL_KS + a * 4 + t + 1])
                OPQ("vector", "scalar_tensor_tensor", KQ[p][:, :], KQ[4][:, :],
                    1.0, KQ[5][:, :], OP.mult, OP.mult,
                    accum_out=COLS2[:, CL_KV + a * 4 + t:CL_KV + a * 4 + t + 1],
                    incs=[(svec, 1)])
                it += 1
        W("scalar", svec, svec.n)
        W("scalar", sgps, sgps.n)
        for a in range(NCH):
            for base, src in ((8 + a, COLS[:, CL_KS + a * 4:]),
                              (a, COLS2[:, CL_KV + a * 4:])):
                OPQ("scalar", "activation", KQ[5][:, 0:1], src[:, 0:1], AF.Identity,
                    bias=src[:, 1:2])
                OPQ("scalar", "activation", KQ[5][:, 1:2], KQ[5][:, 0:1], AF.Identity,
                    bias=src[:, 2:3])
                OPQ("scalar", "activation", PK[:, base:base + 1], KQ[5][:, 1:2],
                    AF.Identity, bias=src[:, 3:4])
        MK("scalar", (ssca, 1))
        pack_done = ssca.n
        W("gpsimd", ssca, pack_done)
        OPQ("gpsimd", "dma_start", out=ccin.ap(), in_=PK[:, :], incs=[(sgps, 16)])
        W("gpsimd", sgps, sgps.n)
        OPQ("gpsimd", "collective_compute", "AllReduce", OP.add,
            replica_groups=[[0, 1, 2, 3], [4, 5, 6, 7]],
            ins=[ccin.ap()], outs=[ccout.ap()], incs=[(scc, 1)])
        W("gpsimd", scc, scc.n)
        OPQ("gpsimd", "dma_start", out=CC[:, :], in_=ccout.ap(), incs=[(sgps, 16)])
        cc_ready = sgps.n

        cc_waited = [False]
        it = 0
        qevac = {}
        qbank = [pE, pF]
        p2_entry = ssca.n
        for a in range(NCH):
            W("tensor", scast, wq_g[a])
            for t in range(NT512):
                t0 = t * 512
                p = it % 2
                pX = qbank[p]
                if "p2" in _COARSE:
                    W("tensor", ssca, ssca.n)
                    W("tensor", svec, svec.n)
                W("tensor", ssca, qevac.get(it - 2, p2_entry))
                for d in range(NCH):
                    OPQ("tensor", "matmul", pX[:, :512],
                        AB1[:, AB1_WQ + d * 1024 + a * 128:AB1_WQ + d * 1024 + (a + 1) * 128],
                        XB[:, d * T + t0:d * T + t0 + 512],
                        start=(d == 0), stop=(d == NCH - 1))
                MK("tensor", (stns, 1))
                W("scalar", svec, svec.n)
                W("scalar", stns, stns.n)
                OPQ("scalar", "activation", KQ[p][:, :], pX[:, :512], AF.Exp, bias=bq(a))
                OPQ("scalar", "activation", KQ[2 + p][:, :], pX[:, :512], AF.Relu,
                    bias=bq(a), incs=[(ssca, 1)])
                qevac[it] = ssca.n
                W("vector", ssca, ssca.n)
                OPQ("vector", "scalar_tensor_tensor", KQ[4][:, :], KQ[p][:, :],
                    1.0, KQ[2 + p][:, :], OP.min, OP.add, incs=[(svec, 1)])
                if not cc_waited[0]:
                    W("vector", sgps, cc_ready)
                    cc_waited[0] = True
                OPQ("vector", "tensor_scalar", KQ[5][:, :], KQ[4][:, :],
                    CC[:, 8 + a:9 + a], 1e-6, OP.mult, OP.add, incs=[(svec, 1)])
                W("scalar", svec, svec.n)
                OPQ("scalar", "activation", KQ[p][:, :], KQ[5][:, :], AF.Ln)
                OPQ("scalar", "activation", KQ[2 + p][:, :], KQ[p][:, :], AF.Exp,
                    scale=CONSTC[:, 3:4], incs=[(ssca, 1)])
                W("vector", ssca, ssca.n)
                OPQ("vector", "tensor_tensor", KQ[5][:, :], KQ[4][:, :],
                    KQ[2 + p][:, :], OP.mult)
                OPQ("vector", "tensor_scalar",
                    AB2[:, AB2_V + a * T + t0:AB2_V + a * T + t0 + 512],
                    KQ[5][:, :], CC[:, a:a + 1], None, OP.mult, incs=[(svec, 1)])
                it += 1
        v_ready = svec.n

        W("tensor", scast, max(wo_g))
        W("tensor", svec, v_ready)
        p3st = {"evac_prev": ssca.n}
        omm_done = {}
        p3_src = lambda c: STY[:, c * 256:(c + 1) * 256]

        def emit_omms(tt, st=None):
            t0 = tt * 256
            bd = p3st.get("bankdone") or [p3st["evac_prev"]] * 4
            for do in range(NCH):
                if do < 4:
                    W("tensor", ssca, bd[do % 4])
                for a in range(NCH):
                    OPQ("tensor", "matmul",
                        PW[do % 4][:, (do // 4) * 256:(do // 4) * 256 + 256],
                        AB1[:, AB1_WO + a * 1024 + do * 128:AB1_WO + a * 1024 + (do + 1) * 128],
                        AB2[:, AB2_V + a * T + t0:AB2_V + a * T + t0 + 256],
                        start=(a == 0), stop=(a == NCH - 1))
                if st is not None:
                    if do == 2:
                        stat_sums(st)
                    elif do == 5:
                        stat_sumsq(st)
                    elif do == 6:
                        stat_rrep(st)
            MK("tensor", (stns, 1))
            omm_done[tt] = stns.n

        x_ready = {}
        store_done = {}
        adds_end_prev = svec.n
        for tt in range(NT256):
            t0 = tt * 256
            if tt == 0:
                W("gpsimd", svec, svec.n)
                W("gpsimd", ssca, ssca.n)
                W("gpsimd", sxs, sxs.n)
                OPQ("gpsimd", "dma_start", out=XT.rearrange("p (c t) -> p c t", c=NCH),
                    in_=sav[:, :, 0:256], incs=[(sxl, 16)])
                x_ready[0] = sxl.n
                store_done[-1] = sxs.n
                emit_omms(0)
            st = {"src": p3_src}
            W("scalar", stns, omm_done[tt])
            adds_base = svec.n
            W("vector", sxl, x_ready[tt])
            W("vector", sxs, store_done[tt - 1])
            bankdone = [0] * 4
            for idx, do in enumerate((0, 4, 1, 5, 2, 6, 3, 7)):
                kk = (idx % 2) * 2
                W("scalar", svec, (adds_base + idx - 1) if idx >= 2 else adds_end_prev)
                OPQ("scalar", "activation", KQ[kk][:, :256],
                    PW[do % 4][:, (do // 4) * 256:(do // 4) * 256 + 256],
                    AF.Gelu_apprx_tanh, bias=bo(do))
                OPQ("scalar", "activation", KQ[kk + 1][:, :256], KQ[kk][:, :256],
                    AF.Gelu_apprx_tanh, incs=[(ssca, 1)])
                if idx % 2 == 1:
                    bankdone[do % 4] = ssca.n
                W("vector", ssca, ssca.n)
                OPQ("vector", "tensor_add", STY[:, do * 256:(do + 1) * 256],
                    KQ[kk + 1][:, :256], XT[:, do * 256:(do + 1) * 256],
                    incs=[(svec, 1)])
                OPQ("vector", "tensor_copy", SBQ[:, do * 256:(do + 1) * 256],
                    STY[:, do * 256:(do + 1) * 256])
            MK("vector", (svec, 1))
            adds_end_prev = adds_base + NCH
            st["casts_done"] = svec.n
            p3st["evac_prev"] = ssca.n
            p3st["bankdone"] = bankdone
            if tt + 1 < NT256:
                W("gpsimd", svec, adds_base + NCH)
                OPQ("gpsimd", "dma_start", out=XT.rearrange("p (c t) -> p c t", c=NCH),
                    in_=sav[:, :, t0 + 256:t0 + 512], incs=[(sxl, 16)])
                x_ready[tt + 1] = sxl.n
                emit_omms(tt + 1, st)
            else:
                stat_sums(st)
                stat_sumsq(st)
                stat_rrep(st)
            stat_squares(st, 0)
            stat_squares(st, 1)
            stat_lnexp(st)
            stat_rows(st)
            stat_mrstd(st)
            W("vector", stns, st["rrep_done"])
            for c in range(NCH):
                OPQ("vector", "tensor_tensor", KQ[4][:, :256],
                    STY[:, c * 256:(c + 1) * 256], RREP, OP.mult)
                OPQ("vector", "scalar_tensor_tensor", KQ[5][:, :256], KQ[4][:, :256],
                    0.0, MRREP, OP.add, OP.subtract)
                OPQ("vector", "tensor_scalar", STY[:, c * 256:(c + 1) * 256],
                    KQ[5][:, :256], g1(c), be1(c), OP.mult, OP.add)
            MK("vector", (svec, 1))
            apply_done = svec.n
            W("scalar", svec, apply_done)
            for c in range(NCH):
                OPQ("scalar", "activation", XB[:, c * T + t0:c * T + t0 + 256],
                    STY[:, c * 256:(c + 1) * 256], AF.Identity)
            MK("scalar", (ssca, 1))
            W("gpsimd", svec, apply_done)
            OPQ("gpsimd", "dma_start", out=sbv[:, :, t0:t0 + 256],
                in_=STY.rearrange("p (c t) -> p c t", c=NCH), incs=[(sxs, 16)])
            store_done[tt] = sxs.n
        p3_tns = stns.n

        w1_g = {}
        w2_g = {}
        for s in range(4):
            for d in range(NCH):
                w1_g[(d, s)] = granule(
                    W1a[i][d * 128:(d + 1) * 128, s * 1024:(s + 1) * 1024],
                    AB1[:, d * 4096 + s * 1024:d * 4096 + (s + 1) * 1024], p3_tns)
            for f in range(s * 8, s * 8 + 8):
                w2_g[f] = granule(W2a[i][f * 128:(f + 1) * 128, :],
                                  AB2[:, f * 1024:(f + 1) * 1024], p3_tns,
                                  extra_gps_gate=(first_w2_gate if f == 0 else 0))

        p4st = {"hb_prev": ssca.n, "f1_prev": ssca.n}
        sweep_done = {}
        p4_src = lambda c: STY[:, c * 256:(c + 1) * 256]

        def emit_sweep(tt, st=None):
            t0 = tt * 256
            hbcnt = {}

            def emit_fmm(f):
                W("tensor", ssca, hbcnt[f])
                if f == 0:
                    W("tensor", ssca, p4st["f1_prev"])
                for c in range(NCH):
                    OPQ("tensor", "matmul",
                        PW[c % 4][:, (c // 4) * 256:(c // 4) * 256 + 256],
                        AB2[:, f * 1024 + c * 128:f * 1024 + (c + 1) * 128],
                        HB[:, (f % 2) * 256:(f % 2) * 256 + 256],
                        start=(f == 0 and c < 4), stop=(f == 31 and c >= 4))

            for f in range(32):
                need = max(max(w1_g[(d, f // 8)] for d in range(NCH)), w2_g[f])
                W("tensor", scast, need)
                W("tensor", ssca, hbcnt[f - 1] if f > 0 else p4st["hb_prev"])
                for d in range(NCH):
                    OPQ("tensor", "matmul", pG[:, 0:256],
                        AB1[:, d * 4096 + f * 128:d * 4096 + (f + 1) * 128],
                        XB[:, d * T + t0:d * T + t0 + 256],
                        start=(d == 0), stop=(d == NCH - 1))
                MK("tensor", (stns, 1))
                W("scalar", stns, stns.n)
                hb = HB[:, (f % 2) * 256:(f % 2) * 256 + 256]
                OPQ("scalar", "activation", hb, pG[:, 0:256], AF.Gelu_apprx_tanh,
                    bias=b1(f), incs=[(ssca, 1)])
                hbcnt[f] = ssca.n
                if st is not None:
                    if f == 4:
                        stat_squares(st, 0)
                    elif f == 5:
                        stat_squares(st, 1)
                    elif f == 9:
                        stat_lnexp(st)
                if f >= 1:
                    emit_fmm(f - 1)
                if st is not None:
                    if f == 3:
                        stat_sums(st)
                    elif f == 7:
                        stat_sumsq(st)
                    elif f == 12:
                        stat_rrep(st)
            emit_fmm(31)
            p4st["hb_prev"] = hbcnt[31]
            MK("tensor", (stns, 1))
            sweep_done[tt] = stns.n

        x_ready = {}
        store_done = {}
        adds_end_prev = svec.n
        for tt in range(NT256):
            t0 = tt * 256
            if tt == 0:
                W("gpsimd", svec, svec.n)
                W("gpsimd", ssca, ssca.n)
                W("gpsimd", sxs, sxs.n)
                OPQ("gpsimd", "dma_start", out=XT.rearrange("p (c t) -> p c t", c=NCH),
                    in_=sbv[:, :, 0:256], incs=[(sxl, 16)])
                x_ready[0] = sxl.n
                store_done[-1] = sxs.n
                emit_sweep(0)
            st = {"src": p4_src}
            W("scalar", stns, sweep_done[tt])
            adds_base = svec.n
            W("vector", sxl, x_ready[tt])
            W("vector", sxs, store_done[tt - 1])
            for c in range(NCH):
                kk = c % 2
                W("scalar", svec, (adds_base + c - 1) if c >= 2 else adds_end_prev)
                OPQ("scalar", "activation", KQ[kk][:, :256],
                    PW[c % 4][:, (c // 4) * 256:(c // 4) * 256 + 256],
                    AF.Gelu_apprx_tanh, bias=b2(c), incs=[(ssca, 1)])
                W("vector", ssca, ssca.n)
                OPQ("vector", "tensor_add", STY[:, c * 256:(c + 1) * 256],
                    KQ[kk][:, :256], XT[:, c * 256:(c + 1) * 256], incs=[(svec, 1)])
                OPQ("vector", "tensor_copy", SBQ[:, c * 256:(c + 1) * 256],
                    STY[:, c * 256:(c + 1) * 256])
            MK("vector", (svec, 1))
            adds_end_prev = adds_base + NCH
            st["casts_done"] = svec.n
            p4st["f1_prev"] = ssca.n
            if tt + 1 < NT256:
                W("gpsimd", svec, adds_base + NCH)
                OPQ("gpsimd", "dma_start", out=XT.rearrange("p (c t) -> p c t", c=NCH),
                    in_=sbv[:, :, t0 + 256:t0 + 512], incs=[(sxl, 16)])
                x_ready[tt + 1] = sxl.n
                emit_sweep(tt + 1, st)
            else:
                stat_sums(st)
                stat_sumsq(st)
                stat_rrep(st)
                stat_squares(st, 0)
                stat_squares(st, 1)
                stat_lnexp(st)
            stat_rows(st)
            stat_mrstd(st)
            W("vector", stns, st["rrep_done"])
            for c in range(NCH):
                OPQ("vector", "tensor_tensor", KQ[2][:, :256],
                    STY[:, c * 256:(c + 1) * 256], RREP, OP.mult)
                OPQ("vector", "scalar_tensor_tensor", KQ[3][:, :256], KQ[2][:, :256],
                    0.0, MRREP, OP.add, OP.subtract)
                OPQ("vector", "tensor_scalar", STY[:, c * 256:(c + 1) * 256],
                    KQ[3][:, :256], g2(c), be2(c), OP.mult, OP.add)
            MK("vector", (svec, 1))
            apply_done = svec.n
            W("scalar", svec, apply_done)
            for c in range(NCH):
                OPQ("scalar", "activation", XB[:, c * T + t0:c * T + t0 + 256],
                    STY[:, c * 256:(c + 1) * 256], AF.Identity)
            MK("scalar", (ssca, 1))
            W("gpsimd", svec, apply_done)
            OPQ("gpsimd", "dma_start", out=sav[:, :, t0:t0 + 256],
                in_=STY.rearrange("p (c t) -> p c t", c=NCH), incs=[(sxs, 16)])
            store_done[tt] = sxs.n

    def dump_stream(view):
        for tt in range(NT256):
            t0 = tt * 256
            W("gpsimd", svec, svec.n)
            W("gpsimd", ssca, ssca.n)
            W("gpsimd", sxs, sxs.n)
            OPQ("gpsimd", "dma_start", out=XT.rearrange("p (c t) -> p c t", c=NCH),
                in_=view[:, :, t0:t0 + 256], incs=[(sxl, 16)])
            W("gpsimd", sxl, sxl.n)
            OPQ("gpsimd", "dma_start", out=Yv[:, :, t0:t0 + 256],
                in_=XT.rearrange("p (c t) -> p c t", c=NCH), incs=[(sout, 16)])

    if stop == "S0":
        dump_stream(Xv)
        W("gpsimd", sout, sout.n)
        with es, nc.Block() as block:
            for eng_name in Prog.ENGS:
                getattr(block, eng_name)(
                    lambda e, q=P.q[eng_name], n=eng_name: _runq_outer(e, q, n))
        return nc
    if stop == "LN0":
        for tt in range(NT256):
            t0 = tt * 256
            W("gpsimd", svec, svec.n)
            W("gpsimd", ssca, ssca.n)
            W("gpsimd", sxs, sxs.n)
            OPQ("gpsimd", "dma_start", out=XT.rearrange("p (c t) -> p c t", c=NCH),
                in_=Xv[:, :, t0:t0 + 256], incs=[(sxl, 16)])
            W("vector", sxl, sxl.n)
            W("scalar", sxl, sxl.n)
            ln_stats(EPS_TOP, lambda c: XT[:, c * 256:(c + 1) * 256])
            for c in range(NCH):
                OPQ("vector", "tensor_tensor", KQ[0][:, :256],
                    XT[:, c * 256:(c + 1) * 256], RREP, OP.mult)
                OPQ("vector", "scalar_tensor_tensor", STY[:, c * 256:(c + 1) * 256],
                    KQ[0][:, :256], 0.0, MRREP, OP.add, OP.subtract)
            MK("vector", (svec, 1))
            W("gpsimd", svec, svec.n)
            OPQ("gpsimd", "dma_start", out=Yv[:, :, t0:t0 + 256],
                in_=STY.rearrange("p (c t) -> p c t", c=NCH), incs=[(sout, 16)])
        W("gpsimd", sout, sout.n)
        with es, nc.Block() as block:
            for eng_name in Prog.ENGS:
                getattr(block, eng_name)(
                    lambda e, q=P.q[eng_name], n=eng_name: _runq_outer(e, q, n))
        return nc
    emit_F("F1")
    W("gpsimd", stns, stns.n)
    OPQ("gpsimd", "dma_start", out=ternsp.ap(), in_=AB2[:, AB2_TERN:AB2_TERN + 8192],
        incs=[(ssp, 16)])
    tern_spilled = ssp.n

    n_blocks = {"F1": 0, "F1Y": 0, "QNT": 0, "XH": 0, "XQT": 0, "QCL": 0, "B0": 1, "B1": 2, "B2": 3, "B3": 4, "full": NB}[stop]
    for i in range(n_blocks):
        emit_block(i, first_w2_gate=(tern_spilled if i == 0 else 0))


    if stop == "full":
        W("sync", stns, stns.n)
        W("sync", ssp, tern_spilled)
        OPQ("sync", "dma_start", out=AB2[:, AB2_TERN:AB2_TERN + 8192], in_=ternsp.ap(),
            incs=[(sstp, 16)])
        for d in range(NCH):
            OPQ("sync", "dma_start",
                out=AB2[:, AB2_H + d * 1024:AB2_H + (d + 1) * 1024],
                in_=HAD[d * 128:(d + 1) * 128, :], incs=[(sstp, 16)])
        emit_F("F2")
    elif stop == "F1Y":
        dump_stream(sy1v)
    elif stop in ("QNT", "XH", "XQT", "QCL"):
        pass
    else:
        dump_stream(sav)

    W("gpsimd", sout, sout.n)

    with es, nc.Block() as block:
        for eng_name in Prog.ENGS:
            getattr(block, eng_name)(
                lambda e, q=P.q[eng_name], n=eng_name: _runq_outer(e, q, n))
    return nc


def _hadamard():
    h = np.array([[1.0]], dtype=np.float64)
    while h.shape[0] < D:
        h = np.block([[h, h], [h, -h]])
    return (h / np.sqrt(D)).astype(np.float32)


def _pack_params(inp):
    Pm = np.zeros((NB, 128, PARC), dtype=np.float32)
    cols = lambda v: np.asarray(v, np.float32).reshape(-1, 128).T
    for i in range(NB):
        Pm[i, :, PC_BQ:PC_BQ + 8] = cols(inp["L0_attn_bq"][i])
        Pm[i, :, PC_BK:PC_BK + 8] = cols(inp["L0_attn_bk"][i])
        Pm[i, :, PC_BV:PC_BV + 8] = cols(inp["L0_attn_bv"][i])
        Pm[i, :, PC_BO:PC_BO + 8] = cols(inp["L0_attn_bo"][i])
        Pm[i, :, PC_B1:PC_B1 + 32] = cols(inp["L0_attn_b1"][i])
        Pm[i, :, PC_B2:PC_B2 + 8] = cols(inp["L0_attn_b2"][i])
        Pm[i, :, PC_G1:PC_G1 + 8] = cols(inp["L0_attn_ln1_s"][i])
        Pm[i, :, PC_BE1:PC_BE1 + 8] = cols(inp["L0_attn_ln1_b"][i])
        Pm[i, :, PC_G2:PC_G2 + 8] = cols(inp["L0_attn_ln2_s"][i])
        Pm[i, :, PC_BE2:PC_BE2 + 8] = cols(inp["L0_attn_ln2_b"][i])
    return Pm


def kernel(_stop="full", **inputs):
    inp = {k: np.asarray(v) for k, v in inputs.items()}
    x = np.asarray(inp["L0_attn_x"], np.float32).reshape(16384, D)
    common = dict(
        WQ=np.ascontiguousarray(np.asarray(inp["L0_attn_Wq"], np.float32)),
        WK=np.ascontiguousarray(np.asarray(inp["L0_attn_Wk"], np.float32)),
        WV=np.ascontiguousarray(np.asarray(inp["L0_attn_Wv"], np.float32)),
        WO=np.ascontiguousarray(np.asarray(inp["L0_attn_Wo"], np.float32)),
        W1=np.ascontiguousarray(np.asarray(inp["L0_attn_W1"], np.float32)),
        W2=np.ascontiguousarray(np.asarray(inp["L0_attn_W2"], np.float32)),
        PAR=_pack_params(inp),
        WT=np.ascontiguousarray(np.asarray(inp["L0_ffn_weight"], np.float32)),
        HAD=np.ascontiguousarray(_hadamard().astype(ml_dtypes.bfloat16)),
        IDT=np.eye(128, dtype=ml_dtypes.bfloat16),
        ONE1=np.ones((128, 1), dtype=ml_dtypes.bfloat16),
        ONEF=np.ones((1, 128), dtype=np.float32),
        CONST=np.broadcast_to(np.array(
            [1e-5, 1e-6, -0.5, -1.0, -1e-5, np.log(7.0), -np.log(7.0), 1.0 / 1024,
             0.5, 5e-6, -8.0, 8.5, -5e-6, 0.0, 0.0, 0.0], np.float32),
            (128, 16)).copy(),
    )
    in_maps = []
    for c in range(8):
        m = dict(common)
        m["X"] = np.ascontiguousarray(x[c * T:(c + 1) * T, :].T)
        in_maps.append(m)
    nc = build(stop=_stop)
    res = run_bass_kernel_spmd(nc, in_maps, core_ids=list(range(8)))
    out = np.concatenate([np.asarray(res.results[c]["Y"], np.float32).T
                          for c in range(8)], axis=0)
    return np.ascontiguousarray(out.reshape(2, 8192, D))


# version 10 (final)
# speedup vs baseline: 1.2636x; 1.0242x over previous
import os
import numpy as np
import ml_dtypes
from contextlib import ExitStack
_SKIP_TERN = os.environ.get("K_SKIP_TERN", "0") == "1"
_COARSE = set(os.environ.get("K_COARSE", "").split(","))
_SKIP_TPOSE = os.environ.get("K_SKIP_TPOSE", "0") == "1"

import concourse.bass as bass
import concourse.mybir as mybir
from concourse.bass_utils import run_bass_kernel_spmd

F32 = mybir.dt.float32
BF16 = mybir.dt.bfloat16
AF = mybir.ActivationFunctionType
OP = mybir.AluOpType
AX = mybir.AxisListType.X

D = 1024
T = 2048
NCH = 8
NB = 4
FF = 4096
NT512 = T // 512
NT256 = T // 256
EPS_TOP = 1e-5
EPS_BLK = 1e-6

PC_BQ, PC_BK, PC_BV, PC_BO = 0, 8, 16, 24
PC_B1, PC_B2 = 32, 64
PC_G1, PC_BE1, PC_G2, PC_BE2 = 72, 80, 88, 96
PARC = 104

AB1_WK, AB1_WV, AB1_WQ, AB1_WO = 0, 8192, 16384, 24576
AB2_V = 0
AB2_TERN, AB2_H, AB2_XQ, AB2_LN = 0, 8192, 16384, 18432

CL_KS = 0
CQ_M0, CQ_M1, CQ_MM, CQ_MC, CQ_RC, CQ_R7, CQ_NS = 0, 1, 2, 3, 4, 5, 6
CQ_SO = 24
CL_KV = 0
CQ_HSO = 32

R_A, R_B, R_C, R_D = 0, 256, 512, 768

CC_EPS5, CC_EPS6, CC_NHALF, CC_NONE, CC_NEPS5, CC_LN7, CC_NLN7, CC_R1024 = range(8)
CC_HALF, CC_5E6, CC_N8, CC_85, CC_N5E6 = 8, 9, 10, 11, 12


class Sem:
    def __init__(self, h):
        self.h = h
        self.n = 0


class Prog:
    ENGS = ("sync", "vector", "scalar", "tensor", "gpsimd")

    def __init__(self):
        self.q = {e: [] for e in self.ENGS}

    def op(self, eng, method, *args, incs=(), **kw):
        for s, k in incs:
            s.n += k
        bound = [(s.h, k) for s, k in incs]

        def fn(e, method=method, args=args, kw=kw, bound=bound):
            ins = getattr(e, method)(*args, **kw)
            for h, k in bound:
                ins.then_inc(h, k)
            return ins
        self.q[eng].append(("op", fn))

    def mark(self, eng, *incs):
        for s, k in incs:
            s.n += k
        bound = [(s.h, k) for s, k in incs]
        self.q[eng].append(("mark", bound))

    def wait(self, eng, sem, val):
        if val <= 0:
            return
        self.q[eng].append(("wait", lambda e, h=sem.h, v=val: e.wait_ge(h, v)))

    def dwait(self, eng, sem, ref, key):
        def fn(e, h=sem.h, ref=ref, key=key):
            v = ref[key]
            if v > 0:
                e.wait_ge(h, v)
        self.q[eng].append(("wait", fn))


def _runq_outer(e, q, name):
    last = None
    for idx, ent in enumerate(q):
        kind, payload = ent
        try:
            if kind == "op":
                last = payload(e)
            elif kind == "wait":
                payload(e)
            else:
                assert last is not None, "mark with no preceding op"
                for h, k in payload:
                    last.then_inc(h, k)
        except Exception:
            import sys
            print(f"FAILED {name} entry #{idx} kind={kind}", file=sys.stderr)
            raise


def build(stop="full"):
    nc = bass.Bass("TRN2", target_bir_lowering=False, detect_race_conditions=False)

    X = nc.dram_tensor("X", [D, T], F32, kind="ExternalInput").ap()
    WQa = nc.dram_tensor("WQ", [NB, D, D], F32, kind="ExternalInput").ap()
    WKa = nc.dram_tensor("WK", [NB, D, D], F32, kind="ExternalInput").ap()
    WVa = nc.dram_tensor("WV", [NB, D, D], F32, kind="ExternalInput").ap()
    WOa = nc.dram_tensor("WO", [NB, D, D], F32, kind="ExternalInput").ap()
    W1a = nc.dram_tensor("W1", [NB, D, FF], F32, kind="ExternalInput").ap()
    W2a = nc.dram_tensor("W2", [NB, FF, D], F32, kind="ExternalInput").ap()
    PAR = nc.dram_tensor("PAR", [NB, 128, PARC], F32, kind="ExternalInput").ap()
    WT = nc.dram_tensor("WT", [D, D], F32, kind="ExternalInput").ap()
    HAD = nc.dram_tensor("HAD", [D, D], BF16, kind="ExternalInput").ap()
    IDTd = nc.dram_tensor("IDT", [128, 128], BF16, kind="ExternalInput").ap()
    ONE1d = nc.dram_tensor("ONE1", [128, 1], BF16, kind="ExternalInput").ap()
    ONEFd = nc.dram_tensor("ONEF", [1, 128], F32, kind="ExternalInput").ap()
    CONSTd = nc.dram_tensor("CONST", [128, 16], F32, kind="ExternalInput").ap()
    Y = nc.dram_tensor("Y", [D, T], F32, kind="ExternalOutput").ap()

    sy1 = nc.dram_tensor("sy1", [D, T], F32)
    sa = nc.dram_tensor("sa", [D, T], F32)
    sbm = nc.dram_tensor("sbm", [D, T], F32)
    ccin = nc.dram_tensor("ccin", [128, 16], F32)
    ccout = nc.dram_tensor("ccout", [128, 16], F32)
    ternsp = nc.dram_tensor("ternsp", [128, 8192], BF16)

    Xv = X.rearrange("(c p) t -> p c t", p=128)
    Yv = Y.rearrange("(c p) t -> p c t", p=128)
    sy1v = sy1.ap().rearrange("(c p) t -> p c t", p=128)
    sav = sa.ap().rearrange("(c p) t -> p c t", p=128)
    sbv = sbm.ap().rearrange("(c p) t -> p c t", p=128)

    es = ExitStack()
    sb_ = lambda n, sh, dt=F32: es.enter_context(nc.sbuf_tensor(n, sh, dt))
    ps_ = lambda n, sh, dt=F32: es.enter_context(nc.psum_tensor(n, sh, dt))

    XB = sb_("XB", [128, NCH * T], BF16)
    AB1 = sb_("AB1", [128, 32768], BF16)
    AB2 = sb_("AB2", [128, 32768], BF16)
    WS = sb_("WS", [128, 2048], F32)
    XT = sb_("XT", [128, 2048], F32)
    STY = sb_("STY", [128, 2048], F32)
    SBQ = sb_("SBQ", [128, 2048], BF16)
    KQ = [sb_(f"KQ{i}", [128, 512]) for i in range(6)]
    HB = sb_("HB", [128, 512], BF16)
    COLS = sb_("COLS", [128, 32])
    COLS2 = sb_("COLS2", [128, 40])
    PK = sb_("PK", [128, 16])
    CC = PK
    PARS = sb_("PARS", [128, NB * PARC])
    ROWS = sb_("ROWS", [1, 1024])
    IDT = sb_("IDT_s", [128, 128], BF16)
    ONE1 = sb_("ONE1_s", [128, 1], BF16)
    ONEF = sb_("ONEF_s", [1, 128], F32)
    CONSTC = sb_("CONST_s", [128, 16], F32)

    pA = ps_("pA", [128, 512])
    pB = ps_("pB", [128, 512])
    pC = ps_("pC", [128, 512])
    pD = ps_("pD", [128, 512])
    pEF = ps_("pEF", [128, 1024])
    pE = pEF[:, 0:512]
    pF = pEF[:, 512:1024]
    pG = ps_("pG", [128, 512])
    pT = ps_("pT", [128, 1024], BF16)
    PW = [pA, pB, pC, pD]

    semnames = ("sstp", "sw0", "sw1", "sxl", "sxs", "ssp", "scast", "stns",
                "svec", "ssca", "sgps", "scc", "sout", "swd")
    sems = {k: Sem(es.enter_context(nc.semaphore(k))) for k in semnames}
    (sstp, sw0, sw1, sxl, sxs, ssp, scast, stns, svec, ssca, sgps, scc, sout,
     swd) = (sems[k] for k in semnames)
    SW = [sw0, sw1]

    P = Prog()
    OPQ = P.op
    W = P.wait
    MK = P.mark

    def pcol(blk, base, idx=0):
        c = blk * PARC + base + idx
        return PARS[:, c:c + 1]

    gcount = [0]

    def granule(src_ap, dst_ap, area_gate_tns=0, extra_gps_gate=0):
        g = gcount[0]
        gcount[0] += 1
        slot = WS[:, (g % 2) * 1024:(g % 2) * 1024 + 1024]
        W("sync", scast, g - 1)
        OPQ("sync", "dma_start", out=slot, in_=src_ap, incs=[(SW[g % 2], 16)])
        W("vector", SW[g % 2], SW[g % 2].n)
        if area_gate_tns:
            W("vector", stns, area_gate_tns)
        if extra_gps_gate:
            W("vector", ssp, extra_gps_gate)
        OPQ("vector", "tensor_copy", dst_ap, slot, incs=[(scast, 1)])
        return scast.n

    def granule3(src3, dst3, area_gate_tns=0):
        g = gcount[0]
        gcount[0] += 1
        slot = WS[:, (g % 2) * 1024:(g % 2) * 1024 + 1024]
        slot3 = slot.rearrange("p (d c) -> p d c", d=8)
        W("sync", scast, g - 1)
        OPQ("sync", "dma_start", out=slot3, in_=src3, incs=[(SW[g % 2], 16)])
        W("vector", SW[g % 2], SW[g % 2].n)
        if area_gate_tns:
            W("vector", stns, area_gate_tns)
        OPQ("vector", "tensor_copy", dst3, slot3, incs=[(scast, 1)])
        return scast.n

    for i in range(NB):
        OPQ("sync", "dma_start", out=PARS[:, i * PARC:(i + 1) * PARC], in_=PAR[i],
            incs=[(sstp, 16)])
    OPQ("sync", "dma_start", out=IDT[:, :], in_=IDTd, incs=[(sstp, 16)])
    OPQ("sync", "dma_start", out=ONE1[:, :], in_=ONE1d, incs=[(sstp, 16)])
    OPQ("sync", "dma_start", out=ONEF[:, :], in_=ONEFd, incs=[(sstp, 16)])
    OPQ("sync", "dma_start", out=CONSTC[:, :], in_=CONSTd, incs=[(sstp, 16)])
    setup_dma = sstp.n
    W("vector", sstp, setup_dma)
    W("scalar", sstp, setup_dma)

    for o in range(0 if _SKIP_TERN else NCH):
        g = gcount[0]
        gcount[0] += 1
        slot = WS[:, (g % 2) * 1024:(g % 2) * 1024 + 1024]
        W("sync", scast, g - 1)
        OPQ("sync", "dma_start", out=slot, in_=WT[o * 128:(o + 1) * 128, :],
            incs=[(SW[g % 2], 16)])
        W("vector", SW[g % 2], SW[g % 2].n)
        sraw = COLS[:, CQ_SO + o:CQ_SO + o + 1]
        tcol = COLS[:, 8:9]
        thr_p = COLS2[:, CQ_HSO + o:CQ_HSO + o + 1]
        thr_n = COLS[:, 9:10]
        OPQ("vector", "tensor_reduce", sraw, slot, AX, OP.add,
            apply_absolute_value=True, incs=[(svec, 1)])
        W("scalar", svec, svec.n)
        OPQ("scalar", "activation", tcol, sraw, AF.Relu,
            scale=CONSTC[:, CC_R1024:CC_R1024 + 1], bias=CONSTC[:, CC_NEPS5:CC_NEPS5 + 1])
        OPQ("scalar", "activation", thr_p, tcol, AF.Identity,
            scale=CONSTC[:, CC_HALF:CC_HALF + 1], bias=CONSTC[:, CC_5E6:CC_5E6 + 1])
        OPQ("scalar", "activation", thr_n, tcol, AF.Identity,
            scale=CONSTC[:, CC_NHALF:CC_NHALF + 1], bias=CONSTC[:, CC_N5E6:CC_N5E6 + 1],
            incs=[(ssca, 1)])
        W("vector", ssca, ssca.n)
        for h in range(2):
            sl = slot[:, h * 512:(h + 1) * 512]
            OPQ("vector", "tensor_scalar", KQ[0][:, :], sl, thr_p, None, OP.is_gt)
            OPQ("vector", "tensor_scalar", KQ[1][:, :], sl, thr_n, None, OP.is_lt)
            OPQ("vector", "tensor_tensor",
                AB2[:, AB2_H + o * 1024 + h * 512:AB2_H + o * 1024 + (h + 1) * 512],
                KQ[0][:, :], KQ[1][:, :], OP.subtract)
        MK("vector", (scast, 1))
    W("tensor", scast, scast.n)
    for o in range(0 if (_SKIP_TERN or _SKIP_TPOSE) else NCH):
        for i4 in range(0, NCH, 4):
            W("tensor", svec, svec.n)
            for i in range(i4, i4 + 4):
                OPQ("tensor", "transpose", pT[:, (i - i4) * 128:(i - i4 + 1) * 128],
                    AB2[:, AB2_H + o * 1024 + i * 128:AB2_H + o * 1024 + (i + 1) * 128],
                    IDT[:, :])
            MK("tensor", (stns, 1))
            W("vector", stns, stns.n)
            if os.environ.get("K_SIMPLE_COPY", "0") == "1":
                for i in range(i4, i4 + 4):
                    OPQ("vector", "tensor_copy",
                        AB2[:, AB2_TERN + i * 1024 + o * 128:AB2_TERN + i * 1024 + (o + 1) * 128],
                        pT[:, (i - i4) * 128:(i - i4 + 1) * 128])
                MK("vector", (svec, 1))
            else:
                src = pT[:, 0:512].rearrange("p (j q) -> p j q", j=4)
                dst = AB2[:, AB2_TERN:AB2_TERN + 8192].rearrange(
                    "p (i q) -> p i q", i=NCH)[:, i4:i4 + 4, o * 128:(o + 1) * 128]
                OPQ("vector", "tensor_copy", dst, src, incs=[(svec, 1)])
    tern_ready_vec = svec.n

    W("sync", svec, tern_ready_vec)
    for d in range(NCH):
        OPQ("sync", "dma_start", out=AB2[:, AB2_H + d * 1024:AB2_H + (d + 1) * 1024],
            in_=HAD[d * 128:(d + 1) * 128, :], incs=[(sstp, 16)])
    h_dma = sstp.n
    W("tensor", sstp, h_dma)

    RREP = pEF[:, 0:256]
    MRREP = pEF[:, 512:768]
    PSUM_SUM = pEF[:1, 256:512]
    PSUM_SQ = pEF[:1, 768:1024]

    def ln_stats(eps, src, defer=None):
        eps_col = 0 if eps == EPS_TOP else 1
        for c in range(NCH):
            OPQ("vector", "tensor_copy", SBQ[:, c * 256:(c + 1) * 256], src(c))
        MK("vector", (svec, 1))
        lnsv = svec.n
        if defer is not None:
            defer()
        W("tensor", svec, lnsv)
        W("tensor", ssca, ssca.n)
        for c in range(NCH):
            OPQ("tensor", "matmul", PSUM_SUM[:, :256], ONE1[:, :],
                SBQ[:, c * 256:(c + 1) * 256],
                start=(c == 0), stop=(c == NCH - 1))
        MK("tensor", (stns, 1))
        W("scalar", stns, stns.n)
        for c in range(NCH):
            OPQ("scalar", "activation", SBQ[:, c * 256:(c + 1) * 256], src(c), AF.Square)
        MK("scalar", (ssca, 1))
        W("tensor", ssca, ssca.n)
        for c in range(NCH):
            OPQ("tensor", "matmul", PSUM_SQ[:, :256], ONE1[:, :],
                SBQ[:, c * 256:(c + 1) * 256],
                start=(c == 0), stop=(c == NCH - 1))
        MK("tensor", (stns, 1))
        W("vector", stns, stns.n)
        OPQ("vector", "tensor_copy", ROWS[:, R_A:R_A + 256], PSUM_SUM[:, :256])
        OPQ("vector", "tensor_copy", ROWS[:, R_B:R_B + 256], PSUM_SQ[:, :256])
        OPQ("vector", "tensor_scalar", ROWS[:, R_C:R_C + 256], ROWS[:, R_A:R_A + 256],
            1.0 / D, None, OP.mult)
        OPQ("vector", "scalar_tensor_tensor", ROWS[:, R_A:R_A + 256],
            ROWS[:, R_C:R_C + 256], -1.0, ROWS[:, R_C:R_C + 256], OP.mult, OP.mult)
        OPQ("vector", "scalar_tensor_tensor", ROWS[:, R_D:R_D + 256],
            ROWS[:, R_B:R_B + 256], 1.0 / D, ROWS[:, R_A:R_A + 256], OP.mult, OP.add,
            incs=[(svec, 1)])
        W("scalar", svec, svec.n)
        OPQ("scalar", "activation", ROWS[:, R_B:R_B + 256], ROWS[:, R_D:R_D + 256],
            AF.Ln, bias=CONSTC[:1, eps_col:eps_col + 1])
        OPQ("scalar", "activation", ROWS[:, R_D:R_D + 256], ROWS[:, R_B:R_B + 256],
            AF.Exp, scale=CONSTC[:1, 2:3], incs=[(ssca, 1)])
        W("vector", ssca, ssca.n)
        OPQ("vector", "tensor_tensor", ROWS[:, R_A:R_A + 256], ROWS[:, R_C:R_C + 256],
            ROWS[:, R_D:R_D + 256], OP.mult, incs=[(svec, 1)])
        W("tensor", svec, svec.n)
        OPQ("tensor", "matmul", RREP, ONEF[:, :], ROWS[:1, R_D:R_D + 256],
            start=True, stop=True)
        OPQ("tensor", "matmul", MRREP, ONEF[:, :], ROWS[:1, R_A:R_A + 256],
            start=True, stop=True)
        MK("tensor", (stns, 1))
        W("vector", stns, stns.n)

    def stat_sums(st):
        W("tensor", svec, st["casts_done"])
        for c in range(NCH):
            OPQ("tensor", "matmul", PSUM_SUM[:, :256], ONE1[:, :],
                SBQ[:, c * 256:(c + 1) * 256], start=(c == 0), stop=(c == NCH - 1))
        MK("tensor", (stns, 1))
        st["sum_done"] = stns.n

    def stat_squares(st, half):
        if half == 0:
            P.dwait("scalar", stns, st, "sum_done")
        for c in range(half * 4, half * 4 + 4):
            OPQ("scalar", "activation", SBQ[:, c * 256:(c + 1) * 256], st["src"](c),
                AF.Square)
        if half == 1:
            MK("scalar", (ssca, 1))
            st["sq_done"] = ssca.n

    def stat_sumsq(st):
        P.dwait("tensor", ssca, st, "sq_done")
        for c in range(NCH):
            OPQ("tensor", "matmul", PSUM_SQ[:, :256], ONE1[:, :],
                SBQ[:, c * 256:(c + 1) * 256], start=(c == 0), stop=(c == NCH - 1))
        MK("tensor", (stns, 1))
        st["sumsq_done"] = stns.n

    def stat_lnexp(st):
        P.dwait("scalar", svec, st, "math_done")
        OPQ("scalar", "activation", ROWS[:, R_B:R_B + 256], ROWS[:, R_D:R_D + 256],
            AF.Ln, bias=CONSTC[:1, 1:2])
        OPQ("scalar", "activation", ROWS[:, R_D:R_D + 256], ROWS[:, R_B:R_B + 256],
            AF.Exp, scale=CONSTC[:1, 2:3], incs=[(ssca, 1)])
        st["lnexp_done"] = ssca.n

    def stat_rrep(st):
        P.dwait("tensor", svec, st, "mrstd_done")
        OPQ("tensor", "matmul", RREP, ONEF[:, :], ROWS[:1, R_D:R_D + 256],
            start=True, stop=True)
        OPQ("tensor", "matmul", MRREP, ONEF[:, :], ROWS[:1, R_A:R_A + 256],
            start=True, stop=True)
        MK("tensor", (stns, 1))
        st["rrep_done"] = stns.n

    def stat_rows(st):
        W("vector", stns, st["sumsq_done"])
        OPQ("vector", "tensor_copy", ROWS[:, R_A:R_A + 256], PSUM_SUM[:, :256])
        OPQ("vector", "tensor_copy", ROWS[:, R_B:R_B + 256], PSUM_SQ[:, :256])
        OPQ("vector", "tensor_scalar", ROWS[:, R_C:R_C + 256], ROWS[:, R_A:R_A + 256],
            1.0 / D, None, OP.mult)
        OPQ("vector", "scalar_tensor_tensor", ROWS[:, R_A:R_A + 256],
            ROWS[:, R_C:R_C + 256], -1.0, ROWS[:, R_C:R_C + 256], OP.mult, OP.mult)
        OPQ("vector", "scalar_tensor_tensor", ROWS[:, R_D:R_D + 256],
            ROWS[:, R_B:R_B + 256], 1.0 / D, ROWS[:, R_A:R_A + 256], OP.mult, OP.add,
            incs=[(svec, 1)])
        st["math_done"] = svec.n

    def stat_mrstd(st):
        W("vector", ssca, st["lnexp_done"])
        OPQ("vector", "tensor_tensor", ROWS[:, R_A:R_A + 256], ROWS[:, R_C:R_C + 256],
            ROWS[:, R_D:R_D + 256], OP.mult, incs=[(svec, 1)])
        st["mrstd_done"] = svec.n

    def emit_F(kind):
        W("tensor", ssca, ssca.n)
        W("tensor", svec, svec.n)
        if kind == "F2":
            W("vector", stns, stns.n)
            W("scalar", stns, stns.n)
            W("tensor", sstp, sstp.n)
        for tt in range(NT256):
            t0 = tt * 256
            if kind == "F1":
                W("gpsimd", svec, svec.n)
                W("gpsimd", ssca, ssca.n)
                W("gpsimd", sxs, sxs.n)
                OPQ("gpsimd", "dma_start", out=XT.rearrange("p (c t) -> p c t", c=NCH),
                    in_=Xv[:, :, t0:t0 + 256], incs=[(sxl, 16)])
                xt_ready = sxl.n
                W("vector", sxl, xt_ready)
                W("scalar", sxl, xt_ready)
                base = XT
                out_t = STY
            else:
                W("gpsimd", svec, svec.n)
                W("gpsimd", ssca, ssca.n)
                W("gpsimd", sxs, sxs.n)
                OPQ("gpsimd", "dma_start", out=XT.rearrange("p (c t) -> p c t", c=NCH),
                    in_=sav[:, :, t0:t0 + 256], incs=[(sxl, 16)])
                OPQ("gpsimd", "dma_start", out=STY.rearrange("p (c t) -> p c t", c=NCH),
                    in_=sy1v[:, :, t0:t0 + 256], incs=[(sxl, 16)])
                xt_ready = sxl.n
                W("vector", sxl, xt_ready)
                W("scalar", sxl, xt_ready)
                for c in range(NCH):
                    OPQ("vector", "tensor_add", XT[:, c * 256:(c + 1) * 256],
                        XT[:, c * 256:(c + 1) * 256], STY[:, c * 256:(c + 1) * 256])
                MK("vector", (svec, 1))
                W("scalar", svec, svec.n)
                base = XT
                out_t = STY
            src = lambda c: base[:, c * 256:(c + 1) * 256]
            ln_stats(EPS_TOP, src)
            for c in range(NCH):
                OPQ("vector", "tensor_tensor", KQ[0][:, :256], src(c), RREP,
                    OP.mult)
                OPQ("vector", "scalar_tensor_tensor",
                    AB2[:, AB2_LN + c * 256:AB2_LN + (c + 1) * 256],
                    KQ[0][:, :256], 0.0, MRREP, OP.add, OP.subtract)
            MK("vector", (svec, 1))
            ln_done = svec.n
            W("tensor", svec, ln_done)
            for j in range(2):
                W("tensor", svec, svec.n)
                W("tensor", ssca, ssca.n)
                for half, pX in ((0, pE), (1, pF)):
                    for d in range(NCH):
                        OPQ("tensor", "matmul", pX[:, :],
                            AB2[:, AB2_LN + d * 256 + j * 128:AB2_LN + d * 256 + (j + 1) * 128],
                            AB2[:, AB2_H + d * 1024 + half * 512:AB2_H + d * 1024 + (half + 1) * 512],
                            start=(d == 0), stop=(d == NCH - 1))
                MK("tensor", (stns, 1))
                W("vector", stns, stns.n)
                if stop == "XH":
                    OPQ("vector", "tensor_copy", STY[:, j * 1024:j * 1024 + 512], pE[:, :])
                    OPQ("vector", "tensor_copy", STY[:, j * 1024 + 512:j * 1024 + 1024],
                        pF[:, :])
                    MK("vector", (svec, 1))
                OPQ("vector", "tensor_reduce", COLS[:, CQ_M0:CQ_M0 + 1], pEF[:, :], AX,
                    OP.max, apply_absolute_value=True, incs=[(svec, 1)])
                W("scalar", stns, stns.n)
                W("scalar", svec, svec.n)
                OPQ("scalar", "activation", COLS[:, 1:2], COLS[:, CQ_M0:CQ_M0 + 1],
                    AF.Relu, bias=CONSTC[:, CC_NEPS5:CC_NEPS5 + 1])
                OPQ("scalar", "activation", COLS[:, 2:3], COLS[:, 1:2], AF.Ln,
                    bias=CONSTC[:, CC_EPS5:CC_EPS5 + 1])
                OPQ("scalar", "activation", COLS[:, 3:4], COLS[:, 2:3], AF.Exp,
                    scale=CONSTC[:, CC_NONE:CC_NONE + 1],
                    bias=CONSTC[:, CC_LN7:CC_LN7 + 1])
                OPQ("scalar", "activation", COLS[:, 4:5], COLS[:, 2:3], AF.Exp,
                    bias=CONSTC[:, CC_NLN7:CC_NLN7 + 1])
                OPQ("scalar", "activation", COLS[:, 5:6], COLS[:, 4:5], AF.Identity,
                    scale=CONSTC[:, CC_N8:CC_N8 + 1])
                for half, pX in ((0, pE), (1, pF)):
                    OPQ("scalar", "activation", KQ[half][:, :], pX, AF.Identity,
                        scale=COLS[:, 3:4], bias=CONSTC[:, CC_85:CC_85 + 1])
                MK("scalar", (ssca, 1))
                if stop == "QCL":
                    W("vector", ssca, ssca.n)
                    OPQ("vector", "tensor_copy", STY[:, (tt * 2 + j) * 8:(tt * 2 + j) * 8 + 8],
                        COLS[:, 0:8])
                    MK("vector", (svec, 1))
                W("vector", ssca, ssca.n)
                for half in (0, 1):
                    zin = KQ[half]
                    for cst in (8.0, 4.0, 2.0, 1.0):
                        OPQ("vector", "tensor_scalar", KQ[4 + half][:, :], zin[:, :],
                            cst, None, OP.is_ge)
                        OPQ("vector", "scalar_tensor_tensor", KQ[2 + half][:, :],
                            KQ[4 + half][:, :], -cst, zin[:, :], OP.mult, OP.add)
                        zin = KQ[2 + half]
                    OPQ("vector", "scalar_tensor_tensor", KQ[4 + half][:, :],
                        KQ[half][:, :], 1.0, KQ[2 + half][:, :], OP.mult, OP.subtract)
                MK("vector", (svec, 1))
                W("scalar", svec, svec.n)
                for half in (0, 1):
                    OPQ("scalar", "activation", SBQ[:, half * 512:(half + 1) * 512],
                        KQ[4 + half][:, :], AF.Identity,
                        scale=COLS[:, 4:5], bias=COLS[:, 5:6])
                MK("scalar", (ssca, 1))
                W("vector", ssca, ssca.n)
                if stop == "XQT":
                    OPQ("vector", "tensor_copy", STY[:, j * 1024:(j + 1) * 1024],
                        SBQ[:, 0:1024])
                    MK("vector", (svec, 1))
                W("tensor", svec, svec.n)
                W("tensor", ssca, ssca.n)
                for h in range(NCH):
                    OPQ("tensor", "transpose", pT[:, h * 128:(h + 1) * 128],
                        SBQ[:, h * 128:(h + 1) * 128], IDT[:, :])
                MK("tensor", (stns, 1))
                W("vector", stns, stns.n)
                dst = AB2[:, AB2_XQ:AB2_XQ + 2048].rearrange(
                    "p (h q) -> p h q", h=NCH)[:, :, j * 128:(j + 1) * 128]
                OPQ("vector", "tensor_copy", dst,
                    pT[:, :].rearrange("p (h q) -> p h q", h=NCH), incs=[(svec, 1)])
            xq_ready = svec.n
            if stop in ("XH", "XQT"):
                W("gpsimd", svec, svec.n)
                OPQ("gpsimd", "dma_start", out=Yv[:, :, t0:t0 + 256],
                    in_=STY.rearrange("p (c t) -> p c t", c=NCH), incs=[(sout, 16)])
                continue
            if stop == "QNT":
                for c in range(NCH):
                    OPQ("vector", "tensor_copy", STY[:, c * 256:(c + 1) * 256],
                        AB2[:, AB2_XQ + c * 256:AB2_XQ + (c + 1) * 256])
                MK("vector", (svec, 1))
                W("gpsimd", svec, svec.n)
                OPQ("gpsimd", "dma_start", out=Yv[:, :, t0:t0 + 256],
                    in_=STY.rearrange("p (c t) -> p c t", c=NCH), incs=[(sout, 16)])
                continue
            W("tensor", svec, xq_ready)
            for o in range(NCH):
                pX = pE if (o % 2 == 0) else pF
                if o >= 2:
                    W("tensor", svec, svec.n)
                for h in range(NCH):
                    OPQ("tensor", "matmul", pX[:, :256],
                        AB2[:, AB2_TERN + h * 1024 + o * 128:AB2_TERN + h * 1024 + (o + 1) * 128],
                        AB2[:, AB2_XQ + h * 256:AB2_XQ + (h + 1) * 256],
                        start=(h == 0), stop=(h == NCH - 1))
                MK("tensor", (stns, 1))
                W("vector", stns, stns.n)
                OPQ("vector", "scalar_tensor_tensor", out_t[:, o * 256:(o + 1) * 256],
                    pX[:, :256], COLS2[:, CQ_HSO + o:CQ_HSO + o + 1],
                    base[:, o * 256:(o + 1) * 256], OP.mult, OP.add, incs=[(svec, 1)])
            ytile_done = svec.n
            if kind == "F1":
                W("gpsimd", svec, ytile_done)
                OPQ("gpsimd", "dma_start", out=sy1v[:, :, t0:t0 + 256],
                    in_=STY.rearrange("p (c t) -> p c t", c=NCH), incs=[(sxs, 16)])
                srcb = lambda c: STY[:, c * 256:(c + 1) * 256]
                ln_stats(EPS_TOP, srcb)
                for c in range(NCH):
                    OPQ("vector", "tensor_tensor", KQ[0][:, :256], srcb(c),
                        RREP, OP.mult)
                    OPQ("vector", "scalar_tensor_tensor", XT[:, c * 256:(c + 1) * 256],
                        KQ[0][:, :256], 0.0, MRREP, OP.add, OP.subtract)
                MK("vector", (svec, 1))
                W("scalar", svec, svec.n)
                for c in range(NCH):
                    OPQ("scalar", "activation", XB[:, c * T + t0:c * T + t0 + 256],
                        XT[:, c * 256:(c + 1) * 256], AF.Identity)
                MK("scalar", (ssca, 1))
                W("gpsimd", svec, svec.n)
                W("gpsimd", ssca, ssca.n)
                OPQ("gpsimd", "dma_start", out=sav[:, :, t0:t0 + 256],
                    in_=XT.rearrange("p (c t) -> p c t", c=NCH), incs=[(sxs, 16)])
            else:
                W("gpsimd", svec, ytile_done)
                OPQ("gpsimd", "dma_start", out=Yv[:, :, t0:t0 + 256],
                    in_=STY.rearrange("p (c t) -> p c t", c=NCH), incs=[(sout, 16)])

    def emit_block(i, first_w2_gate=0):
        bq = lambda a: pcol(i, PC_BQ, a)
        bk = lambda a: pcol(i, PC_BK, a)
        bv = lambda a: pcol(i, PC_BV, a)
        bo = lambda c: pcol(i, PC_BO, c)
        b1 = lambda f: pcol(i, PC_B1, f)
        b2 = lambda c: pcol(i, PC_B2, c)
        g1 = lambda c: pcol(i, PC_G1, c)
        be1 = lambda c: pcol(i, PC_BE1, c)
        g2 = lambda c: pcol(i, PC_G2, c)
        be2 = lambda c: pcol(i, PC_BE2, c)

        W("tensor", ssca, ssca.n)
        W("tensor", svec, svec.n)
        entry_tns = stns.n
        qkv_gate = 0 if i == 0 else entry_tns
        src3 = lambda W_, j: W_[i].rearrange("(d p) c -> p d c", p=128)[:, :, j * 128:(j + 1) * 128]
        dst3 = lambda base, j: AB1[:, base:base + 8192].rearrange(
            "p (d c) -> p d c", d=NCH)[:, :, j * 128:(j + 1) * 128]
        wkv_g = []
        for a in range(NCH):
            granule3(src3(WKa, a), dst3(AB1_WK, a), qkv_gate)
            wkv_g.append(granule3(src3(WVa, a), dst3(AB1_WV, a), qkv_gate))
        wq_g = [granule3(src3(WQa, a), dst3(AB1_WQ, a), qkv_gate)
                for a in range(NCH)]
        wo_g = [granule3(src3(WOa, do), dst3(AB1_WO, do), qkv_gate)
                for do in range(NCH)]
        W("gpsimd", stns, entry_tns)
        for f in range(16, 32):
            OPQ("gpsimd", "dma_start", out=AB2[:, f * 1024:(f + 1) * 1024],
                in_=W2a[i][f * 128:(f + 1) * 128, :], incs=[(swd, 16)])
        w2need = {}
        for f in range(16, 32):
            w2need[f] = swd.n

        it = 0
        kvevac = {}
        kbank = [pE, pG]
        vbank = [pF, pA]
        p1_entry = ssca.n
        for a in range(NCH):
            W("tensor", scast, wkv_g[a])
            for t in range(NT512):
                t0 = t * 512
                p = it % 2
                if "p1" in _COARSE:
                    W("tensor", ssca, ssca.n)
                    W("tensor", svec, svec.n)
                W("tensor", ssca, kvevac.get(it - 2, p1_entry))
                for d in range(NCH):
                    OPQ("tensor", "matmul", kbank[p][:, :512],
                        AB1[:, AB1_WK + d * 1024 + a * 128:AB1_WK + d * 1024 + (a + 1) * 128],
                        XB[:, d * T + t0:d * T + t0 + 512],
                        start=(d == 0), stop=(d == NCH - 1))
                MK("tensor", (stns, 1))
                for d in range(NCH):
                    OPQ("tensor", "matmul", vbank[p][:, :512],
                        AB1[:, AB1_WV + d * 1024 + a * 128:AB1_WV + d * 1024 + (a + 1) * 128],
                        XB[:, d * T + t0:d * T + t0 + 512],
                        start=(d == 0), stop=(d == NCH - 1))
                MK("tensor", (stns, 1))
                W("scalar", svec, svec.n)
                W("scalar", stns, stns.n - 1)
                OPQ("scalar", "activation", KQ[p][:, :], kbank[p][:, :512], AF.Exp,
                    bias=bk(a))
                OPQ("scalar", "activation", KQ[2 + p][:, :], kbank[p][:, :512], AF.Relu,
                    bias=bk(a))
                W("scalar", stns, stns.n)
                OPQ("scalar", "activation", KQ[5][:, :], vbank[p][:, :512], AF.Identity,
                    bias=bv(a), incs=[(ssca, 2)])
                kvevac[it] = ssca.n
                W("vector", ssca, ssca.n)
                OPQ("vector", "scalar_tensor_tensor", KQ[4][:, :], KQ[p][:, :],
                    1.0, KQ[2 + p][:, :], OP.min, OP.add,
                    accum_out=COLS[:, CL_KS + a * 4 + t:CL_KS + a * 4 + t + 1])
                OPQ("vector", "scalar_tensor_tensor", KQ[p][:, :], KQ[4][:, :],
                    1.0, KQ[5][:, :], OP.mult, OP.mult,
                    accum_out=COLS2[:, CL_KV + a * 4 + t:CL_KV + a * 4 + t + 1],
                    incs=[(svec, 1)])
                it += 1
        p1_tns = stns.n
        W("scalar", svec, svec.n)
        W("scalar", sgps, sgps.n)
        for a in range(NCH):
            for base, src in ((8 + a, COLS[:, CL_KS + a * 4:]),
                              (a, COLS2[:, CL_KV + a * 4:])):
                OPQ("scalar", "activation", KQ[5][:, 0:1], src[:, 0:1], AF.Identity,
                    bias=src[:, 1:2])
                OPQ("scalar", "activation", KQ[5][:, 1:2], KQ[5][:, 0:1], AF.Identity,
                    bias=src[:, 2:3])
                OPQ("scalar", "activation", PK[:, base:base + 1], KQ[5][:, 1:2],
                    AF.Identity, bias=src[:, 3:4])
        MK("scalar", (ssca, 1))
        pack_done = ssca.n
        W("gpsimd", ssca, pack_done)
        OPQ("gpsimd", "dma_start", out=ccin.ap(), in_=PK[:, :], incs=[(sgps, 16)])
        W("gpsimd", sgps, sgps.n)
        OPQ("gpsimd", "collective_compute", "AllReduce", OP.add,
            replica_groups=[[0, 1, 2, 3], [4, 5, 6, 7]],
            ins=[ccin.ap()], outs=[ccout.ap()], incs=[(scc, 1)])
        W("gpsimd", scc, scc.n)
        OPQ("gpsimd", "dma_start", out=CC[:, :], in_=ccout.ap(), incs=[(sgps, 16)])
        cc_ready = sgps.n

        cc_waited = [False]
        it = 0
        qevac = {}
        qbank = [pE, pF]
        p2_entry = ssca.n
        w1_g = {}
        for a in range(NCH):
            W("tensor", scast, wq_g[a])
            s_, dpair = a // 2, (a % 2) * 2
            for d in (dpair, dpair + 1):
                w1_g[(d, s_)] = granule(
                    W1a[i][d * 128:(d + 1) * 128, s_ * 1024:(s_ + 1) * 1024],
                    AB1[:, d * 4096 + s_ * 1024:d * 4096 + (s_ + 1) * 1024], p1_tns)
            for t in range(NT512):
                t0 = t * 512
                p = it % 2
                pX = qbank[p]
                if "p2" in _COARSE:
                    W("tensor", ssca, ssca.n)
                    W("tensor", svec, svec.n)
                W("tensor", ssca, qevac.get(it - 2, p2_entry))
                for d in range(NCH):
                    OPQ("tensor", "matmul", pX[:, :512],
                        AB1[:, AB1_WQ + d * 1024 + a * 128:AB1_WQ + d * 1024 + (a + 1) * 128],
                        XB[:, d * T + t0:d * T + t0 + 512],
                        start=(d == 0), stop=(d == NCH - 1))
                MK("tensor", (stns, 1))
                W("scalar", svec, svec.n)
                W("scalar", stns, stns.n)
                OPQ("scalar", "activation", KQ[p][:, :], pX[:, :512], AF.Exp, bias=bq(a))
                OPQ("scalar", "activation", KQ[2 + p][:, :], pX[:, :512], AF.Relu,
                    bias=bq(a), incs=[(ssca, 1)])
                qevac[it] = ssca.n
                W("vector", ssca, ssca.n)
                OPQ("vector", "scalar_tensor_tensor", KQ[4][:, :], KQ[p][:, :],
                    1.0, KQ[2 + p][:, :], OP.min, OP.add, incs=[(svec, 1)])
                if not cc_waited[0]:
                    W("vector", sgps, cc_ready)
                    cc_waited[0] = True
                OPQ("vector", "tensor_scalar", KQ[5][:, :], KQ[4][:, :],
                    CC[:, 8 + a:9 + a], 1e-6, OP.mult, OP.add, incs=[(svec, 1)])
                W("scalar", svec, svec.n)
                OPQ("scalar", "activation", KQ[p][:, :], KQ[5][:, :], AF.Ln)
                OPQ("scalar", "activation", KQ[2 + p][:, :], KQ[p][:, :], AF.Exp,
                    scale=CONSTC[:, 3:4], incs=[(ssca, 1)])
                W("vector", ssca, ssca.n)
                OPQ("vector", "tensor_tensor", KQ[5][:, :], KQ[4][:, :],
                    KQ[2 + p][:, :], OP.mult)
                OPQ("vector", "tensor_scalar",
                    AB2[:, AB2_V + a * T + t0:AB2_V + a * T + t0 + 512],
                    KQ[5][:, :], CC[:, a:a + 1], None, OP.mult, incs=[(svec, 1)])
                it += 1
        v_ready = svec.n
        p2_tns = stns.n

        W("tensor", scast, max(wo_g))
        W("tensor", svec, v_ready)
        p3st = {"evac_prev": ssca.n}
        omm_done = {}
        p3_src = lambda c: STY[:, c * 256:(c + 1) * 256]

        def emit_omms(tt, st=None):
            t0 = tt * 256
            bd = p3st.get("bankdone") or [p3st["evac_prev"]] * 4
            for do in range(NCH):
                if do < 4:
                    W("tensor", ssca, bd[do % 4])
                for a in range(NCH):
                    OPQ("tensor", "matmul",
                        PW[do % 4][:, (do // 4) * 256:(do // 4) * 256 + 256],
                        AB1[:, AB1_WO + a * 1024 + do * 128:AB1_WO + a * 1024 + (do + 1) * 128],
                        AB2[:, AB2_V + a * T + t0:AB2_V + a * T + t0 + 256],
                        start=(a == 0), stop=(a == NCH - 1))
                if st is not None:
                    if do == 2:
                        stat_sums(st)
                    elif do == 5:
                        stat_sumsq(st)
                    elif do == 6:
                        stat_rrep(st)
            MK("tensor", (stns, 1))
            omm_done[tt] = stns.n

        x_ready = {}
        store_done = {}
        adds_end_prev = svec.n
        for tt in range(NT256):
            t0 = tt * 256
            if tt == 0:
                W("gpsimd", svec, svec.n)
                W("gpsimd", ssca, ssca.n)
                W("gpsimd", sxs, sxs.n)
                OPQ("gpsimd", "dma_start", out=XT.rearrange("p (c t) -> p c t", c=NCH),
                    in_=sav[:, :, 0:256], incs=[(sxl, 16)])
                x_ready[0] = sxl.n
                store_done[-1] = sxs.n
                emit_omms(0)
            st = {"src": p3_src}
            W("scalar", stns, omm_done[tt])
            adds_base = svec.n
            W("vector", sxl, x_ready[tt])
            W("vector", sxs, store_done[tt - 1])
            bankdone = [0] * 4
            for idx, do in enumerate((0, 4, 1, 5, 2, 6, 3, 7)):
                kk = (idx % 2) * 2
                W("scalar", svec, (adds_base + idx - 1) if idx >= 2 else adds_end_prev)
                OPQ("scalar", "activation", KQ[kk][:, :256],
                    PW[do % 4][:, (do // 4) * 256:(do // 4) * 256 + 256],
                    AF.Gelu_apprx_tanh, bias=bo(do))
                OPQ("scalar", "activation", KQ[kk + 1][:, :256], KQ[kk][:, :256],
                    AF.Gelu_apprx_tanh, incs=[(ssca, 1)])
                if idx % 2 == 1:
                    bankdone[do % 4] = ssca.n
                W("vector", ssca, ssca.n)
                OPQ("vector", "tensor_add", STY[:, do * 256:(do + 1) * 256],
                    KQ[kk + 1][:, :256], XT[:, do * 256:(do + 1) * 256],
                    incs=[(svec, 1)])
                OPQ("vector", "tensor_copy", SBQ[:, do * 256:(do + 1) * 256],
                    STY[:, do * 256:(do + 1) * 256])
            MK("vector", (svec, 1))
            adds_end_prev = adds_base + NCH
            st["casts_done"] = svec.n
            p3st["evac_prev"] = ssca.n
            p3st["bankdone"] = bankdone
            d_, s_ = 4 + tt % 2, tt // 2
            w1_g[(d_, s_)] = granule(
                W1a[i][d_ * 128:(d_ + 1) * 128, s_ * 1024:(s_ + 1) * 1024],
                AB1[:, d_ * 4096 + s_ * 1024:d_ * 4096 + (s_ + 1) * 1024], p2_tns)
            if tt + 1 < NT256:
                W("gpsimd", svec, adds_base + NCH)
                OPQ("gpsimd", "dma_start", out=XT.rearrange("p (c t) -> p c t", c=NCH),
                    in_=sav[:, :, t0 + 256:t0 + 512], incs=[(sxl, 16)])
                x_ready[tt + 1] = sxl.n
                emit_omms(tt + 1, st)
            else:
                stat_sums(st)
                stat_sumsq(st)
                stat_rrep(st)
            stat_squares(st, 0)
            stat_squares(st, 1)
            stat_lnexp(st)
            stat_rows(st)
            stat_mrstd(st)
            W("vector", stns, st["rrep_done"])
            for c in range(NCH):
                OPQ("vector", "tensor_tensor", KQ[4][:, :256],
                    STY[:, c * 256:(c + 1) * 256], RREP, OP.mult)
                OPQ("vector", "scalar_tensor_tensor", KQ[5][:, :256], KQ[4][:, :256],
                    0.0, MRREP, OP.add, OP.subtract)
                OPQ("vector", "tensor_scalar", STY[:, c * 256:(c + 1) * 256],
                    KQ[5][:, :256], g1(c), be1(c), OP.mult, OP.add)
            MK("vector", (svec, 1))
            apply_done = svec.n
            W("scalar", svec, apply_done)
            for c in range(NCH):
                OPQ("scalar", "activation", XB[:, c * T + t0:c * T + t0 + 256],
                    STY[:, c * 256:(c + 1) * 256], AF.Identity)
            MK("scalar", (ssca, 1))
            W("gpsimd", svec, apply_done)
            OPQ("gpsimd", "dma_start", out=sbv[:, :, t0:t0 + 256],
                in_=STY.rearrange("p (c t) -> p c t", c=NCH), incs=[(sxs, 16)])
            store_done[tt] = sxs.n
        p3_tns = stns.n

        for s in range(4):
            for d in (6, 7):
                w1_g[(d, s)] = granule(
                    W1a[i][d * 128:(d + 1) * 128, s * 1024:(s + 1) * 1024],
                    AB1[:, d * 4096 + s * 1024:d * 4096 + (s + 1) * 1024],
                    omm_done[7])
        W("gpsimd", stns, omm_done[3])
        W("gpsimd", swd, w2need[16])
        if first_w2_gate:
            W("gpsimd", ssp, first_w2_gate)
        for f in range(0, 16, 2):
            OPQ("gpsimd", "dma_start", out=AB2[:, f * 1024:(f + 1) * 1024],
                in_=W2a[i][f * 128:(f + 1) * 128, :], incs=[(swd, 16)])
        even_total = swd.n
        W("gpsimd", stns, omm_done[7])
        W("gpsimd", swd, even_total)
        for f in range(1, 16, 2):
            OPQ("gpsimd", "dma_start", out=AB2[:, f * 1024:(f + 1) * 1024],
                in_=W2a[i][f * 128:(f + 1) * 128, :], incs=[(swd, 16)])
        odd_total = swd.n
        for f in range(0, 16):
            w2need[f] = even_total if f % 2 == 0 else odd_total

        p4st = {"hb_prev": ssca.n, "f1_prev": ssca.n}
        sweep_done = {}
        p4_src = lambda c: STY[:, c * 256:(c + 1) * 256]

        def emit_sweep(tt, st=None):
            t0 = tt * 256
            hbcnt = {}

            def emit_fmm(f):
                W("tensor", ssca, hbcnt[f])
                if f == 0:
                    W("tensor", ssca, p4st["f1_prev"])
                for c in range(NCH):
                    OPQ("tensor", "matmul",
                        PW[c % 4][:, (c // 4) * 256:(c // 4) * 256 + 256],
                        AB2[:, f * 1024 + c * 128:f * 1024 + (c + 1) * 128],
                        HB[:, (f % 2) * 256:(f % 2) * 256 + 256],
                        start=(f == 0 and c < 4), stop=(f == 31 and c >= 4))

            for f in range(32):
                W("tensor", scast, max(w1_g[(d, f // 8)] for d in range(NCH)))
                W("tensor", swd, w2need[f])
                W("tensor", ssca, hbcnt[f - 1] if f > 0 else p4st["hb_prev"])
                for d in range(NCH):
                    OPQ("tensor", "matmul", pG[:, 0:256],
                        AB1[:, d * 4096 + f * 128:d * 4096 + (f + 1) * 128],
                        XB[:, d * T + t0:d * T + t0 + 256],
                        start=(d == 0), stop=(d == NCH - 1))
                MK("tensor", (stns, 1))
                W("scalar", stns, stns.n)
                hb = HB[:, (f % 2) * 256:(f % 2) * 256 + 256]
                OPQ("scalar", "activation", hb, pG[:, 0:256], AF.Gelu_apprx_tanh,
                    bias=b1(f), incs=[(ssca, 1)])
                hbcnt[f] = ssca.n
                if st is not None:
                    if f == 4:
                        stat_squares(st, 0)
                    elif f == 5:
                        stat_squares(st, 1)
                    elif f == 9:
                        stat_lnexp(st)
                if f >= 1:
                    emit_fmm(f - 1)
                if st is not None:
                    if f == 3:
                        stat_sums(st)
                    elif f == 7:
                        stat_sumsq(st)
                    elif f == 12:
                        stat_rrep(st)
            emit_fmm(31)
            p4st["hb_prev"] = hbcnt[31]
            MK("tensor", (stns, 1))
            sweep_done[tt] = stns.n

        x_ready = {}
        store_done = {}
        adds_end_prev = svec.n
        for tt in range(NT256):
            t0 = tt * 256
            if tt == 0:
                W("gpsimd", svec, svec.n)
                W("gpsimd", ssca, ssca.n)
                W("gpsimd", sxs, sxs.n)
                OPQ("gpsimd", "dma_start", out=XT.rearrange("p (c t) -> p c t", c=NCH),
                    in_=sbv[:, :, 0:256], incs=[(sxl, 16)])
                x_ready[0] = sxl.n
                store_done[-1] = sxs.n
                emit_sweep(0)
            st = {"src": p4_src}
            W("scalar", stns, sweep_done[tt])
            adds_base = svec.n
            W("vector", sxl, x_ready[tt])
            W("vector", sxs, store_done[tt - 1])
            for c in range(NCH):
                kk = c % 2
                W("scalar", svec, (adds_base + c - 1) if c >= 2 else adds_end_prev)
                OPQ("scalar", "activation", KQ[kk][:, :256],
                    PW[c % 4][:, (c // 4) * 256:(c // 4) * 256 + 256],
                    AF.Gelu_apprx_tanh, bias=b2(c), incs=[(ssca, 1)])
                W("vector", ssca, ssca.n)
                OPQ("vector", "tensor_add", STY[:, c * 256:(c + 1) * 256],
                    KQ[kk][:, :256], XT[:, c * 256:(c + 1) * 256], incs=[(svec, 1)])
                OPQ("vector", "tensor_copy", SBQ[:, c * 256:(c + 1) * 256],
                    STY[:, c * 256:(c + 1) * 256])
            MK("vector", (svec, 1))
            adds_end_prev = adds_base + NCH
            st["casts_done"] = svec.n
            p4st["f1_prev"] = ssca.n
            if tt + 1 < NT256:
                W("gpsimd", svec, adds_base + NCH)
                OPQ("gpsimd", "dma_start", out=XT.rearrange("p (c t) -> p c t", c=NCH),
                    in_=sbv[:, :, t0 + 256:t0 + 512], incs=[(sxl, 16)])
                x_ready[tt + 1] = sxl.n
                emit_sweep(tt + 1, st)
            else:
                stat_sums(st)
                stat_sumsq(st)
                stat_rrep(st)
                stat_squares(st, 0)
                stat_squares(st, 1)
                stat_lnexp(st)
            stat_rows(st)
            stat_mrstd(st)
            W("vector", stns, st["rrep_done"])
            for c in range(NCH):
                OPQ("vector", "tensor_tensor", KQ[2][:, :256],
                    STY[:, c * 256:(c + 1) * 256], RREP, OP.mult)
                OPQ("vector", "scalar_tensor_tensor", KQ[3][:, :256], KQ[2][:, :256],
                    0.0, MRREP, OP.add, OP.subtract)
                OPQ("vector", "tensor_scalar", STY[:, c * 256:(c + 1) * 256],
                    KQ[3][:, :256], g2(c), be2(c), OP.mult, OP.add)
            MK("vector", (svec, 1))
            apply_done = svec.n
            W("scalar", svec, apply_done)
            for c in range(NCH):
                OPQ("scalar", "activation", XB[:, c * T + t0:c * T + t0 + 256],
                    STY[:, c * 256:(c + 1) * 256], AF.Identity)
            MK("scalar", (ssca, 1))
            W("gpsimd", svec, apply_done)
            OPQ("gpsimd", "dma_start", out=sav[:, :, t0:t0 + 256],
                in_=STY.rearrange("p (c t) -> p c t", c=NCH), incs=[(sxs, 16)])
            store_done[tt] = sxs.n

    def dump_stream(view):
        for tt in range(NT256):
            t0 = tt * 256
            W("gpsimd", svec, svec.n)
            W("gpsimd", ssca, ssca.n)
            W("gpsimd", sxs, sxs.n)
            OPQ("gpsimd", "dma_start", out=XT.rearrange("p (c t) -> p c t", c=NCH),
                in_=view[:, :, t0:t0 + 256], incs=[(sxl, 16)])
            W("gpsimd", sxl, sxl.n)
            OPQ("gpsimd", "dma_start", out=Yv[:, :, t0:t0 + 256],
                in_=XT.rearrange("p (c t) -> p c t", c=NCH), incs=[(sout, 16)])

    if stop == "S0":
        dump_stream(Xv)
        W("gpsimd", sout, sout.n)
        with es, nc.Block() as block:
            for eng_name in Prog.ENGS:
                getattr(block, eng_name)(
                    lambda e, q=P.q[eng_name], n=eng_name: _runq_outer(e, q, n))
        return nc
    if stop == "LN0":
        for tt in range(NT256):
            t0 = tt * 256
            W("gpsimd", svec, svec.n)
            W("gpsimd", ssca, ssca.n)
            W("gpsimd", sxs, sxs.n)
            OPQ("gpsimd", "dma_start", out=XT.rearrange("p (c t) -> p c t", c=NCH),
                in_=Xv[:, :, t0:t0 + 256], incs=[(sxl, 16)])
            W("vector", sxl, sxl.n)
            W("scalar", sxl, sxl.n)
            ln_stats(EPS_TOP, lambda c: XT[:, c * 256:(c + 1) * 256])
            for c in range(NCH):
                OPQ("vector", "tensor_tensor", KQ[0][:, :256],
                    XT[:, c * 256:(c + 1) * 256], RREP, OP.mult)
                OPQ("vector", "scalar_tensor_tensor", STY[:, c * 256:(c + 1) * 256],
                    KQ[0][:, :256], 0.0, MRREP, OP.add, OP.subtract)
            MK("vector", (svec, 1))
            W("gpsimd", svec, svec.n)
            OPQ("gpsimd", "dma_start", out=Yv[:, :, t0:t0 + 256],
                in_=STY.rearrange("p (c t) -> p c t", c=NCH), incs=[(sout, 16)])
        W("gpsimd", sout, sout.n)
        with es, nc.Block() as block:
            for eng_name in Prog.ENGS:
                getattr(block, eng_name)(
                    lambda e, q=P.q[eng_name], n=eng_name: _runq_outer(e, q, n))
        return nc
    emit_F("F1")
    W("gpsimd", stns, stns.n)
    OPQ("gpsimd", "dma_start", out=ternsp.ap(), in_=AB2[:, AB2_TERN:AB2_TERN + 8192],
        incs=[(ssp, 16)])
    tern_spilled = ssp.n

    n_blocks = {"F1": 0, "F1Y": 0, "QNT": 0, "XH": 0, "XQT": 0, "QCL": 0, "B0": 1, "B1": 2, "B2": 3, "B3": 4, "full": NB}[stop]
    for i in range(n_blocks):
        emit_block(i, first_w2_gate=(tern_spilled if i == 0 else 0))


    if stop == "full":
        W("sync", stns, stns.n)
        W("sync", ssp, tern_spilled)
        OPQ("sync", "dma_start", out=AB2[:, AB2_TERN:AB2_TERN + 8192], in_=ternsp.ap(),
            incs=[(sstp, 16)])
        for d in range(NCH):
            OPQ("sync", "dma_start",
                out=AB2[:, AB2_H + d * 1024:AB2_H + (d + 1) * 1024],
                in_=HAD[d * 128:(d + 1) * 128, :], incs=[(sstp, 16)])
        emit_F("F2")
    elif stop == "F1Y":
        dump_stream(sy1v)
    elif stop in ("QNT", "XH", "XQT", "QCL"):
        pass
    else:
        dump_stream(sav)

    W("gpsimd", sout, sout.n)

    with es, nc.Block() as block:
        for eng_name in Prog.ENGS:
            getattr(block, eng_name)(
                lambda e, q=P.q[eng_name], n=eng_name: _runq_outer(e, q, n))
    return nc


def _hadamard():
    h = np.array([[1.0]], dtype=np.float64)
    while h.shape[0] < D:
        h = np.block([[h, h], [h, -h]])
    return (h / np.sqrt(D)).astype(np.float32)


def _pack_params(inp):
    Pm = np.zeros((NB, 128, PARC), dtype=np.float32)
    cols = lambda v: np.asarray(v, np.float32).reshape(-1, 128).T
    for i in range(NB):
        Pm[i, :, PC_BQ:PC_BQ + 8] = cols(inp["L0_attn_bq"][i])
        Pm[i, :, PC_BK:PC_BK + 8] = cols(inp["L0_attn_bk"][i])
        Pm[i, :, PC_BV:PC_BV + 8] = cols(inp["L0_attn_bv"][i])
        Pm[i, :, PC_BO:PC_BO + 8] = cols(inp["L0_attn_bo"][i])
        Pm[i, :, PC_B1:PC_B1 + 32] = cols(inp["L0_attn_b1"][i])
        Pm[i, :, PC_B2:PC_B2 + 8] = cols(inp["L0_attn_b2"][i])
        Pm[i, :, PC_G1:PC_G1 + 8] = cols(inp["L0_attn_ln1_s"][i])
        Pm[i, :, PC_BE1:PC_BE1 + 8] = cols(inp["L0_attn_ln1_b"][i])
        Pm[i, :, PC_G2:PC_G2 + 8] = cols(inp["L0_attn_ln2_s"][i])
        Pm[i, :, PC_BE2:PC_BE2 + 8] = cols(inp["L0_attn_ln2_b"][i])
    return Pm


def kernel(_stop="full", **inputs):
    inp = {k: np.asarray(v) for k, v in inputs.items()}
    x = np.asarray(inp["L0_attn_x"], np.float32).reshape(16384, D)
    common = dict(
        WQ=np.ascontiguousarray(np.asarray(inp["L0_attn_Wq"], np.float32)),
        WK=np.ascontiguousarray(np.asarray(inp["L0_attn_Wk"], np.float32)),
        WV=np.ascontiguousarray(np.asarray(inp["L0_attn_Wv"], np.float32)),
        WO=np.ascontiguousarray(np.asarray(inp["L0_attn_Wo"], np.float32)),
        W1=np.ascontiguousarray(np.asarray(inp["L0_attn_W1"], np.float32)),
        W2=np.ascontiguousarray(np.asarray(inp["L0_attn_W2"], np.float32)),
        PAR=_pack_params(inp),
        WT=np.ascontiguousarray(np.asarray(inp["L0_ffn_weight"], np.float32)),
        HAD=np.ascontiguousarray(_hadamard().astype(ml_dtypes.bfloat16)),
        IDT=np.eye(128, dtype=ml_dtypes.bfloat16),
        ONE1=np.ones((128, 1), dtype=ml_dtypes.bfloat16),
        ONEF=np.ones((1, 128), dtype=np.float32),
        CONST=np.broadcast_to(np.array(
            [1e-5, 1e-6, -0.5, -1.0, -1e-5, np.log(7.0), -np.log(7.0), 1.0 / 1024,
             0.5, 5e-6, -8.0, 8.5, -5e-6, 0.0, 0.0, 0.0], np.float32),
            (128, 16)).copy(),
    )
    in_maps = []
    for c in range(8):
        m = dict(common)
        m["X"] = np.ascontiguousarray(x[c * T:(c + 1) * T, :].T)
        in_maps.append(m)
    nc = build(stop=_stop)
    res = run_bass_kernel_spmd(nc, in_maps, core_ids=list(range(8)))
    out = np.concatenate([np.asarray(res.results[c]["Y"], np.float32).T
                          for c in range(8)], axis=0)
    return np.ascontiguousarray(out.reshape(2, 8192, D))
